# version 12
# baseline (speedup 1.0000x reference)
import numpy as np
import concourse.bass as bass
import concourse.mybir as mybir
from concourse.bass_utils import run_bass_kernel_spmd
from concourse.bass_core_rust import IndirectOffsetOnAxis

F32 = mybir.dt.float32
AF = mybir.ActivationFunctionType
ALU = mybir.AluOpType
AX = mybir.AxisListType

T = 2048
D = 1024
NT = 16
ND = 8
NG = 4
EPS = 1e-5
FA = 248
FM = 384
NEG = -1.0e30
C0 = 768
NCH0 = 8 * C0 // 128
TRASH0 = 8 * C0
FMF = ND * T
B2F = 8192
B3F = 4096


def ts(i, n=128):
    return slice(i * n, (i + 1) * n)



class EW:

    def __init__(self, e):
        self._e = e
        self.last = None

    def __getattr__(self, n):
        a = getattr(self._e, n)
        if callable(a):
            def w(*args, **kw):
                r = a(*args, **kw)
                if r is not None and hasattr(r, "then_inc"):
                    self.last = r
                return r
            return w
        return a


class Chan:
    def __init__(self, nc, es, name):
        self.sem = es.enter_context(nc.semaphore(name))
        self.count = 0

    def inc(self, instr, n=1):
        instr.then_inc(self.sem, n)
        self.count += n
        return instr

    def wait(self, eng, upto=None):
        eng.wait_ge(self.sem, self.count if upto is None else upto)

    def inc_eng(self, eng, n=1):
        eng.sem_inc(self.sem, n)
        self.count += n


def build_program(stage=99, debug=False):
    nc = bass.Bass("TRN2", target_bir_lowering=False)

    x_in = nc.dram_tensor("x0", [T, D], F32, kind="ExternalInput").ap()
    wqv = nc.dram_tensor("wqv", [D, 2 * D], F32, kind="ExternalInput").ap()
    w2cat_in = nc.dram_tensor("w2cat", [D, FA], F32, kind="ExternalInput").ap()
    sw_in = nc.dram_tensor("sw_in", [1, 2], F32, kind="ExternalInput").ap()
    wout_in = nc.dram_tensor("wout", [D, D], F32, kind="ExternalInput").ap()
    wg_in = nc.dram_tensor("wg", [D, 8], F32, kind="ExternalInput").ap()
    w1moe = nc.dram_tensor("w1moe", [8, D, 2048], F32, kind="ExternalInput").ap()
    b1moe = nc.dram_tensor("b1moe", [8, 128, 16], F32, kind="ExternalInput").ap()
    w2moe = nc.dram_tensor("w2moe", [8, 2048, D], F32, kind="ExternalInput").ap()
    b2moe = nc.dram_tensor("b2moe", [8, D], F32, kind="ExternalInput").ap()
    gamma_in = nc.dram_tensor("gamma", [1, D], F32, kind="ExternalInput").ap()
    beta_in = nc.dram_tensor("beta", [1, D], F32, kind="ExternalInput").ap()
    wqkT_in = nc.dram_tensor("wqkT", [D, 3 * D], F32, kind="ExternalInput").ap()
    pwT_in = nc.dram_tensor("pwT", [D, 6], F32, kind="ExternalInput").ap()
    ancT_in = nc.dram_tensor("ancT", [6, 6], F32, kind="ExternalInput").ap()
    rtemp_in = nc.dram_tensor("rtemp", [1, 1], F32, kind="ExternalInput").ap()
    w1f = nc.dram_tensor("w1f", [6, D, 512], F32, kind="ExternalInput").ap()
    w2f = nc.dram_tensor("w2f", [6, 512, D], F32, kind="ExternalInput").ap()
    mask4_in = nc.dram_tensor("mask4", [4, 128, 512], F32, kind="ExternalInput").ap()
    ident_in = nc.dram_tensor("ident", [128, 128], F32, kind="ExternalInput").ap()
    iotas_in = nc.dram_tensor("iotas", [128, 16], F32, kind="ExternalInput").ap()
    tri2_in = nc.dram_tensor("tri2", [2, 128, 128], F32, kind="ExternalInput").ap()
    ones_in = nc.dram_tensor("ones_row", [1, 128], F32, kind="ExternalInput").ap()

    y_t = nc.dram_tensor("y", [T, D], F32, kind="ExternalOutput")
    m_scr_t = nc.dram_tensor("m_scr", [4, NT, 128, FM], F32)
    v0_scr_t = nc.dram_tensor("v0_scr", [NT, 128, D], F32)
    atn_scr_t = nc.dram_tensor("atn_scr", [NT, 128, FA], F32)
    qt_scr_t = nc.dram_tensor("qt_scr", [1024, T], F32)
    vl1_scr_t = nc.dram_tensor("vl1_scr", [NT, 128, 16 * 65], F32)
    gwt_scr_t = nc.dram_tensor("gwt_scr", [8, NT, 128], F32)
    xn_scr_t = nc.dram_tensor("xn_scr", [T, D], F32)
    moe_acc_t = nc.dram_tensor("moe_acc", [T, D], F32)
    inv2_t = nc.dram_tensor("inv2_scr", [2, 16384], F32)
    taps = {}
    if debug:
        for nm, sz in [("t_xnT", ND * T), ("t_attn", NT * FA), ("t_x1", NT * D),
                       ("t_gw", 128), ("t_x2", NT * D), ("t_gw6", 96),
                       ("t_kT", ND * T), ("t_rqT", 4096)]:
            taps[nm] = nc.dram_tensor(nm, [128, sz], F32)

    import contextlib
    with contextlib.ExitStack() as es:
        ec = es.enter_context

        x_sb = ec(nc.sbuf_tensor("x_sb", [128, NT, D], F32))
        fmA = ec(nc.sbuf_tensor("fmA", [128, ND, T], F32))
        big2 = ec(nc.sbuf_tensor("big2", [128, B2F], F32))
        big3 = ec(nc.sbuf_tensor("big3", [128, B3F], F32))
        attn_sb = ec(nc.sbuf_tensor("attn_sb", [128, 2048], F32))
        wA = ec(nc.sbuf_tensor("wA", [128, 2048], F32))
        wB = ec(nc.sbuf_tensor("wB", [128, 2048], F32))
        stg = ec(nc.sbuf_tensor("stg", [128, 1, D], F32))
        small = ec(nc.sbuf_tensor("small", [128, 512], F32))
        cols = ec(nc.sbuf_tensor("cols", [128, 64], F32))
        ident = ec(nc.sbuf_tensor("ident_sb", [128, 128], F32))
        iotas = ec(nc.sbuf_tensor("iotas_sb", [128, 16], F32))
        b1sb = ec(nc.sbuf_tensor("b1sb", [128, 16], F32))
        i32g = ec(nc.sbuf_tensor("i32g", [128, 8], mybir.dt.int32))
        i32a = ec(nc.sbuf_tensor("i32a", [128, 48], mybir.dt.int32))
        pb = [ec(nc.psum_tensor(f"pb{i}", [128, 512], F32)) for i in range(8)]
        ones_r = small[0:1, 384:512]

        try:
            print(f"[build] sbuf bytes remaining: {nc.sbuf_bytes_remaining}")
        except Exception:
            pass

        def b2v(off, dims, p0=0, np_=128):
            return bass.AP(big2, p0 * B2F + off, [[B2F, np_]] + dims)

        def b3v(off, dims, p0=0, np_=128):
            return bass.AP(big3, p0 * B3F + off, [[B3F, np_]] + dims)

        def fmv(off, dims, p0=0, np_=128):
            return bass.AP(fmA, p0 * FMF + off, [[FMF, np_]] + dims)

        ld = Chan(nc, es, "ld")
        gdma = Chan(nc, es, "gdma")
        st = Chan(nc, es, "st")
        chW = [Chan(nc, es, f"chW{k}") for k in range(4)]
        st_a = [Chan(nc, es, f"st_a{k}") for k in range(2)]
        st_v = [Chan(nc, es, f"st_v{k}") for k in range(2)]
        st_q = [Chan(nc, es, f"st_q{k}") for k in range(2)]
        st_l = [Chan(nc, es, f"st_l{k}") for k in range(2)]
        vr = [Chan(nc, es, f"vr{k}") for k in range(4)]
        mr = [Chan(nc, es, f"mr{k}") for k in range(2)]
        sc = [Chan(nc, es, f"sc{k}") for k in range(2)]
        hq = Chan(nc, es, "hq")
        chb1 = Chan(nc, es, "chb1")
        pe_d = Chan(nc, es, "pe_d")
        pe_a = Chan(nc, es, "pe_a")
        pe_g = Chan(nc, es, "pe_g")
        d_pe = Chan(nc, es, "d_pe")
        a_pe = Chan(nc, es, "a_pe")
        d_a = Chan(nc, es, "d_a")
        a_d = Chan(nc, es, "a_d")
        d_sp = Chan(nc, es, "d_sp")
        block = ec(nc.Block())

        @block.sync
        def _(sync):
            ld.inc(sync.dma_start(out=x_sb[:, :, :],
                                  in_=x_in.rearrange("(a p) d -> p a d", p=128)), 16)
            ld.inc(sync.dma_start(out=ident[:, :], in_=ident_in), 16)
            ld.inc(sync.dma_start(out=iotas[:, :], in_=iotas_in), 16)
            ld.inc(sync.dma_start(out=small[0:1, 0:2], in_=sw_in), 16)
            ld.inc(sync.dma_start(out=small[0:1, 2:3], in_=rtemp_in), 16)
            ld.inc(sync.dma_start(out=b2v(5632, [[248, 8], [1, 248]]),
                                  in_=w2cat_in.rearrange("(c p) f -> p c f", p=128)), 16)
        LD0 = ld.count

        @block.vector
        def _(vector):
            vector = EW(vector)
            vector.wait_ge(ld.sem, LD0)
            vector.reduce_max(small[0:1, 4:5], small[0:1, 0:2], AX.X)
            vector.drain()
            vector.tensor_scalar(small[0:1, 6:8], small[0:1, 0:2], small[0:1, 4:5],
                                 None, ALU.subtract)
            d_a.inc(vector.drain())

        @block.scalar
        def _(scalar):
            scalar = EW(scalar)
            d_a.wait(scalar)
            scalar.activation(small[0:1, 8:10], small[0:1, 6:8], AF.Exp,
                              accum_out=small[0:1, 10:11])
            a_d.inc(scalar.drain())

        @block.vector
        def _(vector):
            vector = EW(vector)
            a_d.wait(vector)
            vector.reciprocal(small[0:1, 11:12], small[0:1, 10:11])
            vector.drain()
            vector.tensor_scalar(small[0:1, 12:14], small[0:1, 8:10],
                                 small[0:1, 11:12], None, ALU.mult)
            vector.tensor_scalar(small[0:1, 14:15], small[0:1, 2:3], 0.1,
                                 None, ALU.max)
            vector.drain()
            vector.reciprocal(small[0:1, 15:16], small[0:1, 14:15])
            vector.drain()
            vector.tensor_scalar(small[0:1, 16:17], small[0:1, 15:16], 0.5,
                                 None, ALU.mult)
            vector.memset(ones_r, 1.0)
            vector.memset(cols[:, 62:63], EPS)
            vector.memset(cols[:, 63:64], float(1.0 / 0.3))
            d_pe.inc(vector.drain())

        @block.tensor
        def _(tensor):
            tensor = EW(tensor)
            d_pe.wait(tensor)
            tensor.matmul(pb[0][:, 0:5], ones_r, small[0:1, 12:17],
                          start=True, stop=True)
            pe_d.inc(tensor.last)

        @block.vector
        def _(vector):
            vector = EW(vector)
            pe_d.wait(vector)
            vector.tensor_copy(cols[:, 56:61], pb[0][:, 0:5])
            d_pe.inc(vector.drain())

        @block.vector
        def _(vector):
            vector = EW(vector)
            vector.memset(attn_sb[:, :], 0.0)
            d_sp.inc(vector.drain())

        @block.gpsimd
        def _(gpsimd):
            gpsimd = EW(gpsimd)
            d_sp.wait(gpsimd)
            zsrc = bass.AP(attn_sb, 0, [[2048, 128], [1, 2048]])
            for k in range(12):
                gdma.inc(gpsimd.dma_start(
                    out=bass.AP(m_scr_t, k * 128 * 2048, [[2048, 128], [1, 2048]]),
                    in_=zsrc), 16)
            for k in range(8):
                gdma.inc(gpsimd.dma_start(
                    out=bass.AP(moe_acc_t, k * 128 * 2048, [[2048, 128], [1, 2048]]),
                    in_=zsrc), 16)
            gdma.inc(gpsimd.dma_start(
                out=bass.AP(inv2_t, 0,
                                [[16384, 2], [2048, 4], [1, 2048]]),
                in_=bass.AP(attn_sb, 0, [[2048, 8], [1, 2048]])), 16)
        MZERO = gdma.count

        def emit_ln_stats(idxs, src_fn, junk_ap, first_wait=None,
                          keep_s1=False):
            n = len(idxs)

            @block.scalar
            def _(scalar):
                scalar = EW(scalar)
                if first_wait is not None:
                    scalar.wait_ge(*first_wait)
                pe_a.wait(scalar)
                for k, i in enumerate(idxs):
                    if k:
                        scalar.drain()
                    scalar.activation(junk_ap, src_fn(i), AF.Square,
                                      accum_out=cols[:, k:k + 1])
                a_d.inc(scalar.drain())

            @block.vector
            def _(vector):
                vector = EW(vector)
                a_d.wait(vector)
                for k, i in enumerate(idxs):
                    vector.reduce_sum(cols[:, 16 + k:17 + k], src_fn(i), AX.X)
                vector.drain()
                vector.tensor_scalar(cols[:, 16:16 + n], cols[:, 16:16 + n],
                                     1.0 / D, None, ALU.mult)
                vector.drain()
                vector.tensor_tensor(small[:, 240:240 + n], cols[:, 16:16 + n],
                                     cols[:, 16:16 + n], ALU.mult)
                vector.drain()
                vector.tensor_scalar(cols[:, 0:n], cols[:, 0:n], 1.0 / D,
                                     None, ALU.mult)
                vector.drain()
                vector.tensor_tensor(cols[:, 0:n], cols[:, 0:n],
                                     small[:, 240:240 + n], ALU.subtract)
                if keep_s1:
                    vector.drain()
                    vector.tensor_copy(cols[:, 48:48 + n], cols[:, 0:n])
                d_a.inc(vector.drain())

            @block.scalar
            def _(scalar):
                scalar = EW(scalar)
                d_a.wait(scalar)
                if keep_s1:
                    scalar.activation(cols[:, 32:32 + n], cols[:, 0:n],
                                      AF.Sqrt, bias=cols[:, 62:63])
                else:
                    scalar.activation(cols[:, 0:n], cols[:, 0:n],
                                      AF.Sqrt, bias=cols[:, 62:63])
                a_d.inc(scalar.drain())

            @block.vector
            def _(vector):
                vector = EW(vector)
                a_d.wait(vector)
                if keep_s1:
                    vector.reciprocal(cols[:, 0:n], cols[:, 32:32 + n])
                else:
                    vector.reciprocal(cols[:, 0:n], cols[:, 0:n])
                d_a.inc(vector.drain())

        def emit_transpose_to(src_slot_ap8, dst_base_off, dst_tensor, dst_F, i):
            @block.tensor
            def _(tensor):
                tensor = EW(tensor)
                d_pe.wait(tensor)
                for half in range(2):
                    for c in range(4):
                        tensor.transpose(pb[6 + half][:, ts(c)],
                                         src_slot_ap8[half * 4 + c],
                                         ident[:, :])
                    pe_a.inc(tensor.last)

            @block.vector
            def _(vector):
                vector = EW(vector)
                pe_a.wait(vector)
                for half in range(2):
                    vector.tensor_copy(
                        bass.AP(dst_tensor, dst_base_off + (4 * half) * T + i * 128,
                                [[dst_F, 128], [T, 4], [1, 128]]),
                        pb[6 + half][:, 0:512].rearrange("p (a d) -> p a d", d=128))
                d_pe.inc(vector.drain())

        if stage >= 1:
            emit_ln_stats(list(range(NT)), lambda i: x_sb[:, i, :],
                          stg[:, 0, :], first_wait=(ld.sem, LD0))
            for i in range(NT):
                @block.vector
                def _(vector, i=i):
                    vector = EW(vector)
                    pe_a.wait(vector)
                    st.wait(vector)
                    vector.tensor_scalar(stg[:, 0, :], x_sb[:, i, :],
                                         cols[:, 16 + i:17 + i],
                                         cols[:, i:i + 1],
                                         ALU.subtract, ALU.mult)
                    d_pe.inc(vector.drain())
                    d_sp.inc_eng(vector)

                @block.sync
                def _(sync, i=i):
                    d_sp.wait(sync)
                    st.inc(sync.dma_start(
                        out=bass.AP(xn_scr_t, i * 128 * D, [[D, 128], [1, D]]),
                        in_=stg[:, 0, :]), 16)

                emit_transpose_to([stg[:, 0, ts(c)] for c in range(ND)],
                                  0, fmA, FMF, i)

            if debug:
                @block.sync
                def _(sync):
                    d_sp.wait(sync)
                    d_pe.wait(sync)
                    st.inc(sync.dma_start(out=taps["t_xnT"].ap(),
                                          in_=bass.AP(fmA, 0, [[FMF, 128], [1, FMF]])), 16)

        buf_free = [None, None]

        def _ap_half(a, h):
            dims = [list(d) for d in a.ap]
            s0, n0 = dims[0]
            return bass.AP(a.tensor, a.offset + h * (n0 // 2) * s0,
                           [[s0, n0 // 2]] + dims[1:])

        def load_weight(sync, bi, dst_ap, src_ap):
            tok = buf_free[bi]
            if tok is not None:
                sync.wait_ge(tok[0].sem, tok[1])
            for h in range(2):
                chW[2 * bi + h].inc(
                    sync.dma_start(out=_ap_half(dst_ap, h),
                                   in_=_ap_half(src_ap, h)), 16)
            return ((chW[2 * bi].sem, chW[2 * bi].count),
                    (chW[2 * bi + 1].sem, chW[2 * bi + 1].count))

        if stage >= 2:
            rq_tok = [None]
            atn_tok = {}
            vsl_tok = {}
            for g in range(NG):
                gs = ts(g, 512)
                for qb in range(4):
                    bi = qb % 2
                    buf = [wA, wB][bi]

                    @block.sync
                    def _(sync, qb=qb, bi=bi, buf=buf):
                        load_weight(sync, bi,
                                    bass.AP(buf, 0, [[2048, 128], [256, 8], [1, 256]]),
                                    wqv[:, qb * 256:(qb + 1) * 256]
                                    .rearrange("(c p) f -> p c f", p=128))
                    lq = ((chW[2 * bi].sem, chW[2 * bi].count),
                          (chW[2 * bi + 1].sem, chW[2 * bi + 1].count))

                    @block.tensor
                    def _(tensor, qb=qb, bi=bi, buf=buf, gs=gs, lq=lq, g=g):
                        tensor = EW(tensor)
                        [tensor.wait_ge(s_, c_) for s_, c_ in lq]
                        a_pe.wait(tensor)
                        for fc in range(2):
                            for dc in range(ND):
                                tensor.matmul(pb[fc][:, 0:512],
                                              bass.AP(buf, dc * 256 + fc * 128,
                                                      [[2048, 128], [1, 128]]),
                                              fmA[:, dc, gs],
                                              start=(dc == 0), stop=(dc == ND - 1))
                            pe_a.inc(tensor.last)
                        buf_free[bi] = (pe_a, pe_a.count)

                    @block.scalar
                    def _(scalar, qb=qb, g=g):
                        scalar = EW(scalar)
                        if qb == 0 and rq_tok[0] is not None:
                            scalar.wait_ge(rq_tok[0][0].sem, rq_tok[0][1])
                        if qb == 0 and g > 0:
                            st_v[0].wait(scalar)
                            st_v[1].wait(scalar)
                        for fc in range(2):
                            pe_a.wait(scalar)
                            scalar.activation(
                                bass.AP(big3, (qb * 2 + fc) * 512,
                                        [[B3F, 128], [1, 512]]),
                                pb[fc][:, 0:512], AF.Relu)
                            a_pe.inc(scalar.drain())

                for tt in range(4):
                    i = g * 4 + tt

                    @block.tensor
                    def _(tensor, tt=tt):
                        tensor = EW(tensor)
                        a_pe.wait(tensor)
                        d_pe.wait(tensor)
                        for qc in range(ND):
                            tensor.matmul(pb[4][:, 0:248],
                                          bass.AP(big3, qc * 512 + tt * 128,
                                                  [[B3F, 128], [1, 128]]),
                                          b2v(5632, [[248, 8], [1, 248]])[:, qc, :],
                                          start=(qc == 0), stop=(qc == ND - 1))
                        pe_d.inc(tensor.last)

                    @block.vector
                    def _(vector, i=i):
                        vector = EW(vector)
                        pe_d.wait(vector)
                        if i == 0:
                            vector.wait_ge(gdma.sem, MZERO)
                        vector.reduce_max(
                            cols[:, 28:32].rearrange("p (a b) -> p a b", b=1),
                            pb[4][:, 0:84].rearrange("p (a b) -> p a b", b=21),
                            AX.X, negate=True)
                        vector.reduce_max(
                            cols[:, 32:36].rearrange("p (a b) -> p a b", b=1),
                            pb[4][:, 84:248].rearrange("p (a b) -> p a b", b=41),
                            AX.X, negate=True)
                        vector.drain()
                        vector.tensor_tensor(
                            small[:, 256:340].rearrange("p (a b) -> p a b", b=21),
                            pb[4][:, 0:84].rearrange("p (a b) -> p a b", b=21),
                            bass.AP(cols, 28, [[64, 128], [1, 4], [0, 21]]),
                            ALU.add)
                        vector.tensor_tensor(
                            small[:, 340:504].rearrange("p (a b) -> p a b", b=41),
                            pb[4][:, 84:248].rearrange("p (a b) -> p a b", b=41),
                            bass.AP(cols, 32, [[64, 128], [1, 4], [0, 41]]),
                            ALU.add)
                        d_a.inc(vector.drain())
                        d_pe.inc_eng(vector)

                    @block.scalar
                    def _(scalar):
                        scalar = EW(scalar)
                        d_a.wait(scalar)
                        scalar.activation(small[:, 256:504], small[:, 256:504],
                                          AF.Exp)
                        a_d.inc(scalar.drain())

                    @block.vector
                    def _(vector, i=i):
                        vector = EW(vector)
                        a_d.wait(vector)
                        vector.reduce_sum(
                            cols[:, 36:40].rearrange("p (a b) -> p a b", b=1),
                            small[:, 256:340].rearrange("p (a b) -> p a b", b=21),
                            AX.X)
                        vector.reduce_sum(
                            cols[:, 40:44].rearrange("p (a b) -> p a b", b=1),
                            small[:, 340:504].rearrange("p (a b) -> p a b", b=41),
                            AX.X)
                        vector.drain()
                        vector.reciprocal(cols[:, 44:52], cols[:, 36:44])
                        vector.drain()
                        islot = i % 2
                        tok = atn_tok.get(islot)
                        if tok is not None:
                            vector.wait_ge(tok[0].sem, tok[1])
                        vector.scalar_tensor_tensor(
                            attn_sb[:, islot * 248:islot * 248 + 84]
                            .rearrange("p (a b) -> p a b", b=21),
                            small[:, 256:340].rearrange("p (a b) -> p a b", b=21),
                            cols[:, 56:57],
                            bass.AP(cols, 44, [[64, 128], [1, 4], [0, 21]]),
                            ALU.mult, ALU.mult)
                        vector.scalar_tensor_tensor(
                            attn_sb[:, islot * 248 + 84:islot * 248 + 248]
                            .rearrange("p (a b) -> p a b", b=41),
                            small[:, 340:504].rearrange("p (a b) -> p a b", b=41),
                            cols[:, 57:58],
                            bass.AP(cols, 48, [[64, 128], [1, 4], [0, 41]]),
                            ALU.mult, ALU.mult)
                        d_sp.inc(vector.drain())

                    @block.sync
                    def _(sync, i=i):
                        islot = i % 2
                        d_sp.wait(sync)
                        st_a[islot].inc(sync.dma_start(
                            out=bass.AP(atn_scr_t, i * 128 * FA,
                                        [[FA, 128], [1, FA]]),
                            in_=attn_sb[:, islot * 248:(islot + 1) * 248]), 16)
                        atn_tok[islot] = (st_a[islot], st_a[islot].count)

                rq_tok[0] = (pe_d, pe_d.count)

                for vb in range(4):
                    bi = vb % 2
                    buf = [wA, wB][bi]

                    @block.sync
                    def _(sync, vb=vb, bi=bi, buf=buf):
                        load_weight(sync, bi,
                                    bass.AP(buf, 0, [[2048, 128], [256, 8], [1, 256]]),
                                    wqv[:, 1024 + vb * 256:1024 + (vb + 1) * 256]
                                    .rearrange("(c p) f -> p c f", p=128))
                    lv = ((chW[2 * bi].sem, chW[2 * bi].count),
                          (chW[2 * bi + 1].sem, chW[2 * bi + 1].count))

                    @block.tensor
                    def _(tensor, vb=vb, bi=bi, buf=buf, g=g, lv=lv):
                        tensor = EW(tensor)
                        [tensor.wait_ge(s_, c_) for s_, c_ in lv]
                        if vb == 0:
                            d_a.wait(tensor)
                            a_d.wait(tensor)
                        for tsub in range(4):
                            bank = 2 + tsub
                            d_pe.wait(tensor)
                            for dc in range(ND):
                                tensor.matmul(pb[bank][:, 0:256],
                                              fmA[:, dc, ts(g * 4 + tsub)],
                                              bass.AP(buf, dc * 256,
                                                      [[2048, 128], [1, 256]]),
                                              start=(dc == 0), stop=(dc == ND - 1))
                            pe_d.inc(tensor.last)
                        buf_free[bi] = (pe_d, pe_d.count)

                    for tsub in range(4):
                        vslot = (vb * 4 + tsub) % 2

                        @block.vector
                        def _(vector, vb=vb, tsub=tsub, vslot=vslot):
                            vector = EW(vector)
                            bank = 2 + tsub
                            pe_d.wait(vector, pe_d.count - 3 + tsub)
                            tok = vsl_tok.get(vslot)
                            if tok is not None:
                                vector.wait_ge(tok[0].sem, tok[1])
                            vector.tensor_copy(
                                bass.AP(big3, vslot * 256, [[B3F, 128], [1, 256]]),
                                pb[bank][:, 0:256])
                            d_pe.inc(vector.drain())
                            d_sp.inc_eng(vector)

                        @block.sync
                        def _(sync, vb=vb, tsub=tsub, vslot=vslot, g=g):
                            d_sp.wait(sync)
                            st_v[vslot].inc(sync.dma_start(
                                out=bass.AP(v0_scr_t,
                                            (g * 4 + tsub) * 128 * D + vb * 256,
                                            [[D, 128], [1, 256]]),
                                in_=bass.AP(big3, vslot * 256,
                                            [[B3F, 128], [1, 256]])), 16)
                            vsl_tok[vslot] = (st_v[vslot], st_v[vslot].count)

            if debug:
                @block.sync
                def _(sync):
                    st_a[0].wait(sync)
                    st_a[1].wait(sync)
                    st.inc(sync.dma_start(
                        out=bass.AP(taps["t_attn"], 0,
                                    [[NT * FA, 128], [FA, NT], [1, FA]]),
                        in_=bass.AP(atn_scr_t, 0,
                                    [[FA, 128], [128 * FA, NT], [1, FA]])), 16)

        if stage >= 3:
            @block.gpsimd
            def _(gpsimd):
                gpsimd = EW(gpsimd)
                st_a[0].wait(gpsimd)
                st_a[1].wait(gpsimd)
                gpsimd.wait_ge(gdma.sem, MZERO)
                for ci, (c, half, coff) in enumerate([(21, 10, 0), (41, 20, 84)]):
                    if ci == 1:
                        gpsimd.wait_ge(gdma.sem, gdma.count)
                    for h in range(4):
                        dst = bass.AP(m_scr_t, h * NT * 128 * FM + (128 - half),
                                      [[FM + 1, 128], [128 * FM, NT], [1, c]])
                        src = bass.AP(atn_scr_t, coff + h * c,
                                      [[FA, 128], [128 * FA, NT], [1, c]])
                        gdma.inc(gpsimd.dma_start(
                            out=dst, in_=src,
                            accum_op=(ALU.add if ci == 1 else ALU.bypass)), 16)
            SKEW_DONE = gdma.count

            vring_tok = {}
            vr_seen = {}

            def load_vring(gpsimd, tile):
                slot = tile % 4
                tok = vring_tok.get(slot)
                if tok is not None:
                    gpsimd.wait_ge(tok[0].sem, tok[1])
                vr[slot].inc(gpsimd.dma_start(
                    out=bass.AP(big2, slot * 1024, [[B2F, 128], [1, 1024]]),
                    in_=bass.AP(v0_scr_t, tile * 128 * D, [[D, 128], [1, D]])), 16)
                vr_seen[tile] = (vr[slot], vr[slot].count)

            @block.gpsimd
            def _(gpsimd):
                gpsimd = EW(gpsimd)
                st_v[0].wait(gpsimd)
                st_v[1].wait(gpsimd)
                load_vring(gpsimd, 0)
                load_vring(gpsimd, 1)
            mrd_tok = {}
            mtc_state = {}

            for g in range(NG):
                for tt in range(4):
                    tile = g * 4 + tt
                    @block.gpsimd
                    def _(gpsimd, tile=tile):
                        gpsimd = EW(gpsimd)
                        if tile + 2 < NT:
                            load_vring(gpsimd, tile + 2)

                    for h in range(4):
                        u = (tile * 4 + h)
                        mslot = u % 2

                        @block.gpsimd
                        def _(gpsimd, tile=tile, h=h, mslot=mslot):
                            gpsimd = EW(gpsimd)
                            if tile == 0 and h == 0:
                                gpsimd.wait_ge(gdma.sem, SKEW_DONE)
                            tok = mrd_tok.get(mslot)
                            if tok is not None:
                                gpsimd.wait_ge(tok[0].sem, tok[1])
                            mr[mslot].inc(gpsimd.dma_start(
                                out=bass.AP(big2, 4096 + mslot * 384,
                                            [[B2F, 128], [1, FM]]),
                                in_=bass.AP(m_scr_t,
                                            (h * NT + tile) * 128 * FM,
                                            [[FM, 128], [1, FM]])), 16)
                        RB = (mr[mslot].sem, mr[mslot].count)

                        @block.tensor
                        def _(tensor, tile=tile, h=h, mslot=mslot, RB=RB):
                            tensor = EW(tensor)
                            tensor.wait_ge(*RB)
                            d_pe.wait(tensor)
                            for k in range(3):
                                tensor.transpose(pb[6][:, ts(k)],
                                                 bass.AP(big2,
                                                         4096 + mslot * 384 + k * 128,
                                                         [[B2F, 128], [1, 128]]),
                                                 ident[:, :])
                            pe_d.inc(tensor.last)
                        mrd_tok[mslot] = (pe_d, pe_d.count)

                        @block.vector
                        def _(vector, mslot=mslot):
                            vector = EW(vector)
                            pe_d.wait(vector)
                            vector.tensor_copy(
                                bass.AP(big2, 4864 + mslot * 384,
                                        [[B2F, 128], [1, FM]]),
                                pb[6][:, 0:FM])
                            d_pe.inc(vector.drain())
                        MTC = d_pe.count

                        @block.tensor
                        def _(tensor, tile=tile, h=h, mslot=mslot, MTC=MTC):
                            tensor = EW(tensor)
                            tensor.wait_ge(d_pe.sem, MTC)
                            if h == 0:
                                for dt_ in (tile - 1, tile, tile + 1):
                                    if 0 <= dt_ < NT and dt_ in vr_seen:
                                        tensor.wait_ge(vr_seen[dt_][0].sem,
                                                       vr_seen[dt_][1])
                            ks = [k for k in range(3)
                                  if 0 <= tile - 1 + k < NT]
                            for d2 in range(2):
                                first = True
                                for k in ks:
                                    vt = (tile - 1 + k) % 4
                                    tensor.matmul(
                                        pb[7][:, d2 * 128:(d2 + 1) * 128],
                                        bass.AP(big2,
                                                vt * 1024 + h * 256 + d2 * 128,
                                                [[B2F, 128], [1, 128]]),
                                        bass.AP(big2, 4864 + mslot * 384 + k * 128,
                                                [[B2F, 128], [1, 128]]),
                                        start=first, stop=(k == ks[-1]))
                                    first = False
                            pe_d.inc(tensor.last)

                        @block.vector
                        def _(vector, tt=tt, h=h):
                            vector = EW(vector)
                            pe_d.wait(vector)
                            vector.tensor_copy(
                                bass.AP(big3, (2 * h) * 512 + tt * 128,
                                        [[B3F, 128], [512, 2], [1, 128]]),
                                pb[7][:, 0:256].rearrange("p (a d) -> p a d", d=128))
                            d_pe.inc(vector.drain())
                    vring_tok[(tile - 1) % 4] = (pe_d, pe_d.count)

                MIX = d_pe.count

                for ob in range(4):
                    bi = ob % 2
                    buf = [wA, wB][bi]

                    @block.sync
                    def _(sync, ob=ob, bi=bi, buf=buf):
                        load_weight(sync, bi,
                                    bass.AP(buf, 0, [[2048, 128], [256, 8], [1, 256]]),
                                    wout_in[:, ob * 256:(ob + 1) * 256]
                                    .rearrange("(c p) f -> p c f", p=128))
                    lo = ((chW[2 * bi].sem, chW[2 * bi].count),
                          (chW[2 * bi + 1].sem, chW[2 * bi + 1].count))

                    @block.tensor
                    def _(tensor, ob=ob, bi=bi, buf=buf, g=g, lo=lo, MIX=MIX):
                        tensor = EW(tensor)
                        [tensor.wait_ge(s_, c_) for s_, c_ in lo]
                        tensor.wait_ge(d_pe.sem, MIX)
                        for tsub in range(4):
                            bank = tsub
                            d_pe.wait(tensor)
                            for dc in range(ND):
                                tensor.matmul(pb[bank][:, 0:256],
                                              bass.AP(big3, dc * 512 + tsub * 128,
                                                      [[B3F, 128], [1, 128]]),
                                              bass.AP(buf, dc * 256,
                                                      [[2048, 128], [1, 256]]),
                                              start=(dc == 0), stop=(dc == ND - 1))
                            pe_d.inc(tensor.last)
                        buf_free[bi] = (pe_d, pe_d.count)

                    @block.vector
                    def _(vector, ob=ob, g=g):
                        vector = EW(vector)
                        base = pe_d.count - 4
                        for tsub in range(4):
                            bank = tsub
                            pe_d.wait(vector, base + tsub + 1)
                            i = g * 4 + tsub
                            vector.tensor_add(x_sb[:, i, ob * 256:(ob + 1) * 256],
                                              x_sb[:, i, ob * 256:(ob + 1) * 256],
                                              pb[bank][:, 0:256])
                            d_pe.inc(vector.drain())

            if debug:
                @block.sync
                def _(sync):
                    d_pe.wait(sync)
                    st.inc(sync.dma_start(out=taps["t_x1"].ap(),
                                          in_=bass.AP(x_sb, 0,
                                                      [[NT * D, 128], [1, NT * D]])), 16)

        if stage >= 4:
            @block.sync
            def _(sync):
                pe_d.wait(sync)
                d_pe.wait(sync)
                st.wait(sync)
                ld.inc(sync.dma_start(
                    out=bass.AP(big2, 0, [[B2F, 128], [8, 8], [1, 8]]),
                    in_=wg_in.rearrange("(c p) e -> p c e", p=128)), 16)
                ld.inc(sync.dma_start(
                    out=bass.AP(big2, 512, [[B2F, 128], [128, 2], [1, 128]]),
                    in_=tri2_in.rearrange("a p d -> p a d")), 16)
            LWSM = ld.count

            for i in range(NT):
                @block.tensor
                def _(tensor, i=i):
                    tensor = EW(tensor)
                    d_pe.wait(tensor)
                    if i == 0:
                        tensor.wait_ge(ld.sem, LWSM)
                    for dc in range(ND):
                        tensor.matmul(pb[5][:, 0:8], fmA[:, dc, ts(i)],
                                      bass.AP(big2, dc * 8, [[B2F, 128], [1, 8]]),
                                      start=(dc == 0), stop=(dc == ND - 1))
                    pe_d.inc(tensor.last)

                @block.vector
                def _(vector):
                    vector = EW(vector)
                    pe_d.wait(vector)
                    st_v[0].wait(vector)
                    st_v[1].wait(vector)
                    vector.reduce_max(cols[:, 28:29], pb[5][:, 0:8], AX.X,
                                      negate=True)
                    vector.drain()
                    vector.tensor_scalar(small[:, 256:264], pb[5][:, 0:8],
                                         cols[:, 28:29], None, ALU.add)
                    d_a.inc(vector.drain())
                    d_pe.inc_eng(vector)

                @block.scalar
                def _(scalar):
                    scalar = EW(scalar)
                    d_a.wait(scalar)
                    scalar.activation(small[:, 264:272], small[:, 256:264],
                                      AF.Exp, accum_out=cols[:, 29:30])
                    a_d.inc(scalar.drain())

                @block.vector
                def _(vector, i=i):
                    vector = EW(vector)
                    a_d.wait(vector)
                    vector.reciprocal(cols[:, 30:31], cols[:, 29:30])
                    vector.drain()
                    vector.tensor_scalar(small[:, 272:280], small[:, 264:272],
                                         cols[:, 30:31], None, ALU.mult)
                    vector.drain()
                    vector.reduce_max(cols[:, 31:32], small[:, 272:280], AX.X)
                    vector.drain()
                    vector.tensor_scalar(small[:, 280:288], small[:, 272:280],
                                         cols[:, 31:32], None, ALU.is_ge)
                    vector.drain()
                    vector.scalar_tensor_tensor(small[:, 288:296], small[:, 280:288],
                                                -1000.0, iotas[:, 0:8],
                                                ALU.mult, ALU.add)
                    vector.drain()
                    vector.tensor_reduce(cols[:, 32:33], small[:, 288:296], AX.X,
                                         ALU.min)
                    vector.drain()
                    vector.tensor_scalar(cols[:, 33:34], cols[:, 32:33], 1000.0,
                                         None, ALU.add)
                    vector.drain()
                    vector.tensor_scalar(small[:, 280:288], iotas[:, 0:8],
                                         cols[:, 33:34], None, ALU.is_equal)
                    vector.drain()
                    vector.scalar_tensor_tensor(small[:, 296:304], small[:, 280:288],
                                                -2.0, small[:, 272:280],
                                                ALU.mult, ALU.add)
                    vector.drain()
                    vector.reduce_max(cols[:, 34:35], small[:, 296:304], AX.X)
                    vector.drain()
                    vector.tensor_scalar(small[:, 304:312], small[:, 296:304],
                                         cols[:, 34:35], None, ALU.is_ge)
                    vector.drain()
                    vector.scalar_tensor_tensor(small[:, 312:320], small[:, 304:312],
                                                -1000.0, iotas[:, 0:8],
                                                ALU.mult, ALU.add)
                    vector.drain()
                    vector.tensor_reduce(cols[:, 35:36], small[:, 312:320], AX.X,
                                         ALU.min)
                    vector.drain()
                    vector.tensor_scalar(cols[:, 36:37], cols[:, 35:36], 1000.0,
                                         None, ALU.add)
                    vector.drain()
                    vector.tensor_scalar(small[:, 304:312], iotas[:, 0:8],
                                         cols[:, 36:37], None, ALU.is_equal)
                    vector.drain()
                    vector.tensor_add(
                        bass.AP(big2, 1024 + i * 8, [[B2F, 128], [1, 8]]),
                        small[:, 280:288], small[:, 304:312])
                    vector.tensor_copy(small[:, 320 + i * 4:321 + i * 4],
                                       cols[:, 32:33])
                    vector.tensor_copy(small[:, 321 + i * 4:322 + i * 4],
                                       cols[:, 35:36])
                    vector.tensor_copy(small[:, 322 + i * 4:323 + i * 4],
                                       cols[:, 31:32])
                    vector.tensor_copy(small[:, 323 + i * 4:324 + i * 4],
                                       cols[:, 34:35])
                    vector.drain()
                    vector.tensor_scalar(small[:, 288:296], small[:, 280:288],
                                         cols[:, 31:32], None, ALU.mult)
                    vector.drain()
                    vector.tensor_scalar(small[:, 304:312], small[:, 304:312],
                                         cols[:, 34:35], None, ALU.mult)
                    vector.drain()
                    vector.tensor_add(small[:, i * 8:(i + 1) * 8],
                                      small[:, 288:296], small[:, 304:312])
                    d_pe.inc(vector.drain())

                @block.tensor
                def _(tensor, i=i):
                    tensor = EW(tensor)
                    d_pe.wait(tensor)
                    tensor.transpose(pb[5][0:8, 0:128], small[:, i * 8:(i + 1) * 8],
                                     ident[:, :])
                    pe_d.inc(tensor.last)

                @block.vector
                def _(vector):
                    vector = EW(vector)
                    pe_d.wait(vector)
                    st.wait(vector)
                    vector.tensor_copy(small[0:8, 384:512], pb[5][0:8, 0:128])
                    d_sp.inc(vector.drain())
                    d_pe.inc_eng(vector)

                @block.sync
                def _(sync, i=i):
                    d_sp.wait(sync)
                    st.inc(sync.dma_start(
                        out=bass.AP(gwt_scr_t, i * 128, [[NT * 128, 8], [1, 128]]),
                        in_=small[0:8, 384:512]), 16)

            if debug:
                @block.sync
                def _(sync):
                    d_pe.wait(sync)
                    st.inc(sync.dma_start(out=taps["t_gw"].ap(),
                                          in_=small[:, 0:128]), 16)

        if stage >= 5:
            bank_tok = {b: (d_pe, 0) for b in range(8)}
            h1t_tok = (pe_d, pe_d.count)
            w2slot_tok = {}
            b1_tok = [None]

            GTOK = {0: None, 1: None}
            for i in range(NT):
                par = i % 2

                @block.tensor
                def _(tensor, i=i):
                    tensor = EW(tensor)
                    d_pe.wait(tensor)
                    for j in range(i):
                        tensor.matmul(pb[4][:, 0:8],
                                      bass.AP(big2, 512, [[B2F, 128], [1, 128]]),
                                      bass.AP(big2, 1024 + j * 8,
                                              [[B2F, 128], [1, 8]]),
                                      start=(j == 0), stop=False)
                    tensor.matmul(pb[4][:, 0:8],
                                  bass.AP(big2, 640, [[B2F, 128], [1, 128]]),
                                  bass.AP(big2, 1024 + i * 8,
                                          [[B2F, 128], [1, 8]]),
                                  start=(i == 0), stop=True)
                    pe_d.inc(tensor.last)

                @block.vector
                def _(vector, i=i, par=par):
                    vector = EW(vector)
                    pe_d.wait(vector)
                    tok = GTOK[par]
                    if tok is not None:
                        vector.wait_ge(tok[0].sem, tok[1])
                    vector.tensor_copy(bass.AP(big2, 1152 + i * 8,
                                               [[B2F, 128], [1, 8]]),
                                       pb[4][:, 0:8])
                    vector.drain()
                    d_pe.inc_eng(vector)
                    vector.tensor_scalar(cols[:, 28 + par:29 + par],
                                         iotas[:, 14:15],
                                         float(i * 128), None, ALU.add)
                    vector.drain()
                    for k in range(2):
                        idxc = small[:, 320 + i * 4 + k:321 + i * 4 + k]
                        vector.tensor_scalar(cols[:, 40:41], idxc, 1000.0,
                                             None, ALU.add)
                        vector.drain()
                        vector.tensor_scalar(small[:, 288:296], iotas[:, 0:8],
                                             cols[:, 40:41], None, ALU.is_equal)
                        vector.drain()
                        vector.tensor_tensor(small[:, 296:304],
                                             small[:, 288:296],
                                             bass.AP(big2, 1152 + i * 8,
                                                     [[B2F, 128], [1, 8]]),
                                             ALU.mult)
                        vector.drain()
                        vector.reduce_sum(cols[:, 41:42], small[:, 296:304],
                                          AX.X)
                        vector.drain()
                        vector.tensor_scalar(cols[:, 42:43], cols[:, 41:42],
                                             float(C0), None, ALU.is_ge)
                        vector.tensor_scalar(cols[:, 43:44], idxc, float(C0),
                                             cols[:, 41:42], ALU.mult, ALU.add)
                        vector.drain()
                        vector.tensor_scalar(cols[:, 47:48],
                                             cols[:, 28 + par:29 + par],
                                             2.0, float(TRASH0 + k),
                                             ALU.mult, ALU.add)
                        vector.drain()
                        vector.tensor_scalar(cols[:, 44:45], cols[:, 43:44],
                                             -1.0, cols[:, 47:48], ALU.mult,
                                             ALU.add)
                        vector.drain()
                        vector.scalar_tensor_tensor(cols[:, 45:46],
                                                    cols[:, 44:45],
                                                    cols[:, 42:43],
                                                    cols[:, 43:44],
                                                    ALU.mult, ALU.add)
                        vector.drain()
                        vector.tensor_copy(i32g[:, 4 * par + k:4 * par + k + 1],
                                           cols[:, 45:46])
                        vector.drain()
                    d_sp.inc_eng(vector)
                DSPR = d_sp.count

                @block.gpsimd
                def _(gpsimd, i=i, par=par, DSPR=DSPR):
                    gpsimd = EW(gpsimd)
                    gpsimd.wait_ge(d_sp.sem, DSPR)
                    if i == 0:
                        gpsimd.wait_ge(gdma.sem, MZERO)
                    for k in range(2):
                        sc[par].inc(gpsimd.indirect_dma_start(
                            out=bass.AP(inv2_t, 0, [[1, 32768], [1, 1]]),
                            out_offset=IndirectOffsetOnAxis(
                                i32g[:, 4 * par + k:4 * par + k + 1], 0),
                            in_=cols[:, 28 + par:29 + par], in_offset=None), 16)
                        sc[par].inc(gpsimd.indirect_dma_start(
                            out=bass.AP(inv2_t, 0, [[1, 32768], [1, 1]]),
                            out_offset=IndirectOffsetOnAxis(
                                i32g[:, 4 * par + k:4 * par + k + 1], 0),
                            in_=small[:, 322 + i * 4 + k:323 + i * 4 + k],
                            in_offset=None, element_offset=16384), 16)
                GTOK[par] = (sc[par], sc[par].count)
            SCAT_DONE = (sc[0].count, sc[1].count)
            h1t_tok = (d_sp, d_sp.count)

            @block.gpsimd
            def _(gpsimd):
                gpsimd = EW(gpsimd)
                gpsimd.wait_ge(sc[0].sem, SCAT_DONE[0])
                gpsimd.wait_ge(sc[1].sem, SCAT_DONE[1])
                mr[0].inc(gpsimd.dma_start(
                    out=bass.AP(attn_sb, 1024, [[2048, 48], [1, 128]]),
                    in_=bass.AP(inv2_t, 0, [[128, 48], [1, 128]])), 16)
                mr[0].inc(gpsimd.dma_start(
                    out=bass.AP(attn_sb, 48 * 2048 + 1024, [[2048, 48], [1, 128]]),
                    in_=bass.AP(inv2_t, 16384, [[128, 48], [1, 128]])), 16)
            MRP = mr[0].count

            @block.tensor
            def _(tensor, MRP=MRP):
                tensor = EW(tensor)
                tensor.wait_ge(mr[0].sem, MRP)
                d_pe.wait(tensor)
                tensor.transpose(pb[7][:, 0:96],
                                 bass.AP(attn_sb, 1024, [[2048, 96], [1, 128]]),
                                 ident[0:96, 0:96])
                pe_d.inc(tensor.last)
            PEP = pe_d.count

            @block.vector
            def _(vector, PEP=PEP):
                vector = EW(vector)
                vector.wait_ge(pe_d.sem, PEP)
                vector.tensor_copy(
                    bass.AP(attn_sb, 1152, [[2048, 128], [1, 96]]),
                    pb[7][:, 0:96])
                vector.drain()
                vector.tensor_copy(
                    i32a[:, 0:48],
                    bass.AP(attn_sb, 1152, [[2048, 128], [1, 48]]))
                d_pe.inc(vector.drain())
                d_sp.inc_eng(vector)
            IDP_RDY = d_sp.count

            ring_slots = [(stg, 0, 1024), (attn_sb, 0, 2048),
                          (fmA, 0, FMF), (fmA, 1024, FMF)]
            ring_tok = {0: None, 1: None, 2: None, 3: None}
            SPAN = C0 // 2
            NCSP = SPAN // 128
            pend_esc = [None]

            def emit_esc0(args):
                if args is None:
                    return
                cc0, W2DONE = args
                for cts in range(NCSP):
                    @block.vector
                    def _(vector, cts=cts, cc0=cc0, W2DONE=W2DONE):
                        vector = EW(vector)
                        vector.wait_ge(pe_g.sem, W2DONE)
                        vector.wait_ge(mr[1].sem, mr[1].count)
                        for db in range(2):
                            bank = cts * 2 + db
                            vector.tensor_scalar(
                                bass.AP(big3, 3072 + db * 512,
                                        [[B3F, 128], [1, 512]]),
                                pb[bank][:, 0:512],
                                bass.AP(attn_sb, 1200 + cc0 + cts,
                                        [[2048, 128], [1, 1]]),
                                None, ALU.mult)
                            vector.drain()
                            d_pe.inc_eng(vector)
                            bank_tok[bank] = (d_pe, d_pe.count)
                        d_sp.inc_eng(vector)
                    DSE = d_sp.count

                    @block.gpsimd
                    def _(gpsimd, cts=cts, cc0=cc0, DSE=DSE):
                        gpsimd = EW(gpsimd)
                        gpsimd.wait_ge(d_sp.sem, DSE)
                        gpsimd.wait_ge(mr[1].sem, mr[1].count)
                        mr[1].inc(gpsimd.indirect_dma_start(
                            out=bass.AP(moe_acc_t, 0, [[D, T], [1, D]]),
                            out_offset=IndirectOffsetOnAxis(
                                i32a[:, cc0 + cts:cc0 + cts + 1], 0),
                            in_=bass.AP(big3, 3072, [[B3F, 128], [1, 1024]]),
                            in_offset=None,
                            compute_op=ALU.add), 16)

            def emit_gather0(cc):
                @block.gpsimd
                def _(gpsimd, cc=cc):
                    gpsimd = EW(gpsimd)
                    if cc == 0:
                        gpsimd.wait_ge(d_sp.sem, IDP_RDY)
                    tok = ring_tok[cc % 4]
                    if tok is not None:
                        gpsimd.wait_ge(tok[0].sem, tok[1])
                    rt, roff, F = ring_slots[cc % 4]
                    vr[cc % 4].inc(gpsimd.indirect_dma_start(
                        out=bass.AP(rt, roff, [[F, 128], [1, 1024]]),
                        out_offset=None,
                        in_=bass.AP(xn_scr_t, 0, [[D, T], [1, D]]),
                        in_offset=IndirectOffsetOnAxis(
                            i32a[:, cc:cc + 1], 0)), 16)
                return vr[cc % 4].count
            gather_vrc = {c: emit_gather0(c) for c in range(NCSP)}

            for e in range(8):
                @block.sync
                def _(sync, e=e):
                    sync = EW(sync)
                    if b1_tok[0] is not None:
                        sync.wait_ge(b1_tok[0][0].sem, b1_tok[0][1])
                    chb1.inc(sync.dma_start(out=b1sb[:, :],
                                            in_=b1moe[e]), 16)
                LB1 = chb1.count

                for half in range(2):
                    for c in range(NCSP):
                        cc = e * 2 * NCSP + half * NCSP + c
                        VRC = gather_vrc.pop(cc)

                        @block.tensor
                        def _(tensor, c=c, cc=cc, half=half, VRC=VRC):
                            tensor = EW(tensor)
                            tensor.wait_ge(vr[cc % 4].sem, VRC)
                            d_pe.wait(tensor)
                            rt, roff, F = ring_slots[cc % 4]
                            for half2 in range(2):
                                for cch in range(4):
                                    tensor.transpose(
                                        pb[6 + half2][:, ts(cch)],
                                        bass.AP(rt, roff + (half2 * 4 + cch) * 128,
                                                [[F, 128], [1, 128]]),
                                        ident[:, :])
                                pe_a.inc(tensor.last)
                        ring_tok[cc % 4] = (pe_a, pe_a.count)

                        @block.vector
                        def _(vector, c=c):
                            vector = EW(vector)
                            pe_a.wait(vector)
                            for half2 in range(2):
                                vector.tensor_copy(
                                    bass.AP(big3, (4 * half2) * SPAN + c * 128,
                                            [[B3F, 128], [SPAN, 4], [1, 128]]),
                                    pb[6 + half2][:, 0:512].rearrange(
                                        "p (a d) -> p a d", d=128))
                            d_pe.inc(vector.drain())
                    XST_DONE = d_pe.count
                    nh0 = (e * 2 + half + 1) * NCSP
                    for cc2 in range(nh0, min(nh0 + NCSP, 16 * NCSP)):
                        gather_vrc[cc2] = emit_gather0(cc2)
                    emit_esc0(pend_esc[0])
                    pend_esc[0] = None

                    for blk in range(8):
                        bi = blk % 2
                        buf = [wA, wB][bi]

                        @block.sync
                        def _(sync, e=e, blk=blk, bi=bi, buf=buf):
                            load_weight(sync, bi,
                                        bass.AP(buf, 0,
                                                [[2048, 128], [256, 8], [1, 256]]),
                                        w1moe[e][:, blk * 256:(blk + 1) * 256]
                                        .rearrange("(c p) f -> p c f", p=128))
                        lw = ((chW[2 * bi].sem, chW[2 * bi].count),
                              (chW[2 * bi + 1].sem, chW[2 * bi + 1].count))

                        @block.tensor
                        def _(tensor, blk=blk, bi=bi, buf=buf, lw=lw,
                              XST_DONE=XST_DONE):
                            tensor = EW(tensor)
                            [tensor.wait_ge(s_, c_) for s_, c_ in lw]
                            tensor.wait_ge(d_pe.sem, XST_DONE)
                            for hh in range(2):
                                hc = blk * 2 + hh
                                bank = hc % 2
                                tok = bank_tok[bank]
                                tensor.wait_ge(tok[0].sem, tok[1])
                                for dc in range(ND):
                                    tensor.matmul(pb[bank][:, 0:SPAN],
                                                  bass.AP(buf, dc * 256 + hh * 128,
                                                          [[2048, 128], [1, 128]]),
                                                  bass.AP(big3, dc * SPAN,
                                                          [[B3F, 128], [1, SPAN]]),
                                                  start=(dc == 0),
                                                  stop=(dc == ND - 1))
                                pe_a.inc(tensor.last)
                            buf_free[bi] = (pe_a, pe_a.count)

                        @block.scalar
                        def _(scalar, blk=blk, e=e, LB1=LB1):
                            scalar = EW(scalar)
                            if blk == 0:
                                scalar.wait_ge(chb1.sem, LB1)
                                scalar.wait_ge(h1t_tok[0].sem, h1t_tok[1])
                            for hh in range(2):
                                hc = blk * 2 + hh
                                bank = hc % 2
                                pe_a.wait(scalar, pe_a.count - 1 + hh)
                                scalar.activation(
                                    b2v(hc * SPAN, [[1, SPAN]]),
                                    pb[bank][:, 0:SPAN], AF.Relu,
                                    bias=b1sb[:, hc:hc + 1])
                                a_pe.inc(scalar.drain())
                                bank_tok[bank] = (a_pe, a_pe.count)

                    A_PE_H1 = a_pe.count
                    for hc in range(16):
                        slot = hc % 4
                        sbuf_t = [wA, wA, wB, wB][slot]
                        soff = [0, 1024, 0, 1024][slot]

                        @block.sync
                        def _(sync, e=e, hc=hc, slot=slot, sbuf_t=sbuf_t,
                              soff=soff):
                            sync = EW(sync)
                            tok = w2slot_tok.get(slot)
                            if tok is not None:
                                sync.wait_ge(tok[0].sem, tok[1])
                            tok2 = buf_free[0 if sbuf_t is wA else 1]
                            if tok2 is not None:
                                sync.wait_ge(tok2[0].sem, tok2[1])
                            chW[slot].inc(sync.dma_start(
                                out=bass.AP(sbuf_t, soff, [[2048, 128], [1, 1024]]),
                                in_=w2moe[e][hc * 128:(hc + 1) * 128, :]), 16)
                        lw2 = ((chW[slot].sem, chW[slot].count),)

                        @block.tensor
                        def _(tensor, hc=hc, slot=slot, sbuf_t=sbuf_t, soff=soff,
                              lw2=lw2, A_PE_H1=A_PE_H1, e=e):
                            tensor = EW(tensor)
                            [tensor.wait_ge(s_, c_) for s_, c_ in lw2]
                            if hc == 0:
                                tensor.wait_ge(a_pe.sem, A_PE_H1)
                                for b in range(6):
                                    tok = bank_tok[b]
                                    tensor.wait_ge(tok[0].sem, tok[1])
                            for cts in range(NCSP):
                                for db in range(2):
                                    bank = cts * 2 + db
                                    tensor.matmul(
                                        pb[bank][:, 0:512],
                                        b2v(hc * SPAN + cts * 128, [[1, 128]]),
                                        bass.AP(sbuf_t, soff + db * 512,
                                                [[2048, 128], [1, 512]]),
                                        start=(hc == 0), stop=(hc == 15))
                            pe_g.inc(tensor.last)
                        w2slot_tok[slot] = (pe_g, pe_g.count)
                    h1t_tok = (pe_g, pe_g.count)
                    buf_free[0] = (pe_g, pe_g.count)
                    buf_free[1] = (pe_g, pe_g.count)
                    b1_tok[0] = (a_pe, a_pe.count)
                    W2DONE = pe_g.count

                    pend_esc[0] = (e * 2 * NCSP + half * NCSP, W2DONE)

            emit_esc0(pend_esc[0])

            MOE_SC = mr[1].count
            add_tok = {0: None, 1: None, 2: None, 3: None}
            for i in range(NT):
                slot = i % 4

                @block.gpsimd
                def _(gpsimd, i=i, slot=slot):
                    gpsimd = EW(gpsimd)
                    if i == 0:
                        gpsimd.wait_ge(mr[1].sem, MOE_SC)
                    tok = add_tok[slot]
                    if tok is not None:
                        gpsimd.wait_ge(tok[0].sem, tok[1])
                    vr[slot].inc(gpsimd.dma_start(
                        out=bass.AP(big3, slot * 1024, [[B3F, 128], [1, 1024]]),
                        in_=bass.AP(moe_acc_t, i * 128 * D, [[D, 128], [1, D]])),
                        16)
                VRA = vr[slot].count

                @block.vector
                def _(vector, i=i, slot=slot, VRA=VRA):
                    vector = EW(vector)
                    vector.wait_ge(vr[slot].sem, VRA)
                    vector.tensor_add(x_sb[:, i, :], x_sb[:, i, :],
                                      bass.AP(big3, slot * 1024,
                                              [[B3F, 128], [1, 1024]]))
                    d_pe.inc(vector.drain())
                add_tok[slot] = (d_pe, d_pe.count)

            @block.sync
            def _(sync):
                tok = buf_free[0]
                if tok is not None:
                    sync.wait_ge(tok[0].sem, tok[1])
                chW[0].inc(sync.dma_start(
                    out=bass.AP(wA, 0, [[2048, 8], [1, 1024]]),
                    in_=b2moe), 16)
                tok = buf_free[1]
                if tok is not None:
                    sync.wait_ge(tok[0].sem, tok[1])
                st.wait(sync)
                chW[1].inc(sync.dma_start(
                    out=bass.AP(wB, 0, [[2048, 8], [1, NT * 128]]),
                    in_=bass.AP(gwt_scr_t, 0, [[NT * 128, 8], [1, NT * 128]])),
                    16)
            LB2 = (chW[0].sem, chW[0].count)
            LGW = (chW[1].sem, chW[1].count)
            buf_free[0] = None
            buf_free[1] = None

            for i in range(NT):
                @block.tensor
                def _(tensor, i=i):
                    tensor = EW(tensor)
                    if i == 0:
                        tensor.wait_ge(*LB2)
                        tensor.wait_ge(*LGW)
                    for db in range(2):
                        bank = db
                        tok = bank_tok[bank]
                        tensor.wait_ge(tok[0].sem, tok[1])
                        tensor.matmul(pb[bank][:, 0:512],
                                      bass.AP(wB, i * 128, [[2048, 8], [1, 128]]),
                                      bass.AP(wA, db * 512, [[2048, 8], [1, 512]]),
                                      start=True, stop=True)
                        pe_d.inc(tensor.last)

                @block.vector
                def _(vector, i=i):
                    vector = EW(vector)
                    base = pe_d.count - 2
                    for db in range(2):
                        bank = db
                        pe_d.wait(vector, base + db + 1)
                        vector.tensor_add(x_sb[:, i, db * 512:(db + 1) * 512],
                                          x_sb[:, i, db * 512:(db + 1) * 512],
                                          pb[bank][:, 0:512])
                        d_pe.inc(vector.drain())
                        bank_tok[bank] = (d_pe, d_pe.count)
                    d_sp.inc_eng(vector)

            buf_free[0] = (pe_d, pe_d.count)
            buf_free[1] = (pe_d, pe_d.count)

            @block.vector
            def _(vector):
                vector = EW(vector)
                vector.wait_ge(pe_g.sem, pe_g.count)
                vector.memset(bass.AP(big2, 0, [[B2F, 128], [1, 2048]]), 0.0)
                d_sp.inc(vector.drain())
            DZ6 = d_sp.count

            @block.gpsimd
            def _(gpsimd, DZ6=DZ6):
                gpsimd = EW(gpsimd)
                gpsimd.wait_ge(d_sp.sem, DZ6)
                gpsimd.wait_ge(mr[1].sem, mr[1].count)
                for ch_ in vr:
                    ch_.wait(gpsimd)
                zs = bass.AP(big2, 0, [[B2F, 128], [1, 2048]])
                for k in range(8):
                    gdma.inc(gpsimd.dma_start(
                        out=bass.AP(moe_acc_t, k * 128 * 2048,
                                    [[2048, 128], [1, 2048]]), in_=zs), 16)
                gdma.inc(gpsimd.dma_start(
                    out=bass.AP(inv2_t, 0,
                                [[16384, 2], [2048, 4], [1, 2048]]),
                    in_=bass.AP(big2, 0, [[B2F, 8], [1, 2048]])), 16)
            MZERO6 = gdma.count

            if debug:
                @block.sync
                def _(sync):
                    d_pe.wait(sync)
                    st.inc(sync.dma_start(out=taps["t_x2"].ap(),
                                          in_=bass.AP(x_sb, 0,
                                                      [[NT * D, 128], [1, NT * D]])), 16)

        if stage >= 6:
            P5A, P5D, P5S = pe_a.count, d_pe.count, d_sp.count
            @block.sync
            def _(sync):
                d_pe.wait(sync)
                pe_g.wait(sync)
                ld.inc(sync.dma_start(out=bass.AP(big3, 0, [[B3F, 1], [1, D]]),
                                      in_=gamma_in), 16)
                ld.inc(sync.dma_start(out=bass.AP(big3, D, [[B3F, 1], [1, D]]),
                                      in_=beta_in), 16)
                ld.inc(sync.dma_start(
                    out=bass.AP(big2, 7688, [[B2F, 128], [6, 8], [1, 6]]),
                    in_=pwT_in.rearrange("(c p) e -> p c e", p=128)), 16)
                ld.inc(sync.dma_start(
                    out=bass.AP(big2, 7736, [[B2F, 6], [1, 6]]),
                    in_=ancT_in), 16)
            LGB = ld.count

            @block.vector
            def _(vector):
                vector = EW(vector)
                vector.wait_ge(pe_d.sem, pe_d.count)
                st.wait(vector)
                vector.memset(ones_r, 1.0)
                d_pe.inc(vector.drain())

            @block.tensor
            def _(tensor, LGB=LGB):
                tensor = EW(tensor)
                tensor.wait_ge(ld.sem, LGB)
                d_pe.wait(tensor)
                for k in range(4):
                    tensor.matmul(pb[4 + k][:, 0:512],
                                  ones_r,
                                  bass.AP(big3, (k // 2) * D + (k % 2) * 512,
                                          [[B3F, 1], [1, 512]]),
                                  start=True, stop=True)
                    pe_d.inc(tensor.last)

            @block.vector
            def _(vector):
                vector = EW(vector)
                base = pe_d.count - 4
                for k in range(4):
                    pe_d.wait(vector, base + k + 1)
                    vector.tensor_copy(
                        attn_sb[:, (k // 2) * 1024 + (k % 2) * 512:
                                (k // 2) * 1024 + (k % 2) * 512 + 512],
                        pb[4 + k][:, 0:512])
                d_pe.inc(vector.drain())
                d_a.inc_eng(vector)

            gbG = attn_sb[:, 0:D]
            gbB = attn_sb[:, D:2 * D]

            qkv_bank_tok = {b: (d_pe, 0) for b in range(8)}
            vst_tok = {}
            qst_tok = {}
            for g in range(NG):
                gs = ts(g, 512)
                emit_ln_stats([g * 4 + tt for tt in range(4)],
                              lambda i: x_sb[:, i, :], stg[:, 0, :],
                              first_wait=((d_pe.sem, d_pe.count)
                                          if g == 0 else None),
                              keep_s1=True)

                @block.vector
                def _(vector):
                    vector = EW(vector)
                    vector.tensor_tensor(cols[:, 36:40], cols[:, 0:4],
                                         cols[:, 0:4], ALU.mult)
                    vector.drain()
                    vector.tensor_tensor(cols[:, 40:44], cols[:, 48:52],
                                         cols[:, 36:40], ALU.mult)
                    d_a.inc(vector.drain())

                @block.scalar
                def _(scalar):
                    scalar = EW(scalar)
                    d_a.wait(scalar)
                    scalar.activation(cols[:, 44:48], cols[:, 40:44],
                                      AF.Sqrt, bias=cols[:, 62:63])
                    a_d.inc(scalar.drain())

                @block.vector
                def _(vector):
                    vector = EW(vector)
                    a_d.wait(vector)
                    vector.reciprocal(cols[:, 44:48], cols[:, 44:48])
                    d_a.inc(vector.drain())

                for tt in range(4):
                    i = g * 4 + tt

                    @block.vector
                    def _(vector, i=i, tt=tt):
                        vector = EW(vector)
                        pe_a.wait(vector)
                        vector.tensor_scalar(stg[:, 0, :], x_sb[:, i, :],
                                             cols[:, 16 + tt:17 + tt],
                                             cols[:, tt:tt + 1],
                                             ALU.subtract, ALU.mult)
                        d_pe.inc(vector.drain())
                        d_sp.inc_eng(vector)

                    @block.sync
                    def _(sync, i=i):
                        d_sp.wait(sync)
                        st.inc(sync.dma_start(
                            out=bass.AP(xn_scr_t, i * 128 * D,
                                        [[D, 128], [1, D]]),
                            in_=stg[:, 0, :]), 16)

                    @block.tensor
                    def _(tensor, tt=tt):
                        tensor = EW(tensor)
                        d_pe.wait(tensor)
                        for half in range(2):
                            for c in range(4):
                                tensor.transpose(pb[6 + half][:, ts(c)],
                                                 stg[:, 0, ts(half * 4 + c)],
                                                 ident[:, :])
                            pe_a.inc(tensor.last)

                    @block.vector
                    def _(vector, g=g, tt=tt):
                        vector = EW(vector)
                        pe_a.wait(vector)
                        if g == 0 and tt == 0:
                            vector.wait_ge(gdma.sem, MZERO6)
                        for half in range(2):
                            vector.tensor_copy(
                                bass.AP(big2, (4 * half) * 512 + tt * 128,
                                        [[B2F, 128], [512, 4], [1, 128]]),
                                pb[6 + half][:, 0:512].rearrange(
                                    "p (a d) -> p a d", d=128))
                        vector.drain()
                        st.wait(vector)
                        vector.scalar_tensor_tensor(stg[:, 0, :], stg[:, 0, :],
                                                    cols[:, 44 + tt:45 + tt],
                                                    gbG, ALU.mult, ALU.mult)
                        vector.drain()
                        vector.tensor_tensor(stg[:, 0, :], stg[:, 0, :], gbB,
                                             ALU.add)
                        d_pe.inc(vector.drain())

                    @block.tensor
                    def _(tensor, tt=tt):
                        tensor = EW(tensor)
                        d_pe.wait(tensor)
                        for half in range(2):
                            for c in range(4):
                                tensor.transpose(pb[6 + half][:, ts(c)],
                                                 stg[:, 0, ts(half * 4 + c)],
                                                 ident[:, :])
                            pe_a.inc(tensor.last)

                    @block.vector
                    def _(vector, tt=tt):
                        vector = EW(vector)
                        pe_a.wait(vector)
                        for half in range(2):
                            vector.tensor_copy(
                                bass.AP(big3, (4 * half) * 512 + tt * 128,
                                        [[B3F, 128], [512, 4], [1, 128]]),
                                pb[6 + half][:, 0:512].rearrange(
                                    "p (a d) -> p a d", d=128))
                        d_pe.inc(vector.drain())

                XN_DONE = d_pe.count

                for fb in range(12):
                    bi = fb % 2
                    buf = [wA, wB][bi]

                    @block.sync
                    def _(sync, fb=fb, bi=bi, buf=buf):
                        load_weight(sync, bi,
                                    bass.AP(buf, 0, [[2048, 128], [256, 8], [1, 256]]),
                                    wqkT_in[:, fb * 256:(fb + 1) * 256]
                                    .rearrange("(c p) f -> p c f", p=128))
                    lf = ((chW[2 * bi].sem, chW[2 * bi].count),
                          (chW[2 * bi + 1].sem, chW[2 * bi + 1].count))

                    if fb < 8:
                        @block.tensor
                        def _(tensor, fb=fb, bi=bi, buf=buf, lf=lf,
                              XN_DONE=XN_DONE):
                            tensor = EW(tensor)
                            [tensor.wait_ge(s_, c_) for s_, c_ in lf]
                            tensor.wait_ge(d_pe.sem, XN_DONE)
                            for fc in range(2):
                                bank = fc
                                tok = qkv_bank_tok[bank]
                                tensor.wait_ge(tok[0].sem, tok[1])
                                for ec in range(ND):
                                    tensor.matmul(pb[bank][:, 0:512],
                                                  bass.AP(buf, ec * 256 + fc * 128,
                                                          [[2048, 128], [1, 128]]),
                                                  bass.AP(big3, ec * 512,
                                                          [[B3F, 128], [1, 512]]),
                                                  start=(ec == 0),
                                                  stop=(ec == ND - 1))
                                pe_d.inc(tensor.last)
                            buf_free[bi] = (pe_d, pe_d.count)

                        @block.vector
                        def _(vector, fb=fb, g=g):
                            vector = EW(vector)
                            base = pe_d.count - 2
                            for fc in range(2):
                                bank = fc
                                pe_d.wait(vector, base + fc + 1)
                                fq = fb * 2 + fc
                                if fb < 4:
                                    tok = qst_tok.get(fc)
                                    if tok is not None:
                                        vector.wait_ge(tok[0].sem, tok[1])
                                    vector.tensor_scalar(
                                        bass.AP(big2, 6664 + fc * 512,
                                                [[B2F, 128], [1, 512]]),
                                        pb[bank][:, 0:512], 0.125, None, ALU.mult)
                                    d_sp.inc(vector.drain())
                                    d_pe.inc_eng(vector)
                                else:
                                    vector.tensor_copy(fmA[:, fq - 8, ts(g, 512)],
                                                       pb[bank][:, 0:512])
                                    d_pe.inc(vector.drain())
                                qkv_bank_tok[bank] = (d_pe, d_pe.count)

                        if fb < 4:
                            for fc in range(2):
                                @block.sync
                                def _(sync, fb=fb, fc=fc, g=g):
                                    d_sp.wait(sync, d_sp.count - 1 + fc)
                                    fq = fb * 2 + fc
                                    st_q[fc].inc(sync.dma_start(
                                        out=bass.AP(qt_scr_t, fq * 128 * T + g * 512,
                                                    [[T, 128], [1, 512]]),
                                        in_=bass.AP(big2, 6664 + fc * 512,
                                                    [[B2F, 128], [1, 512]])), 16)
                                qst_tok[fc] = (st_q[fc], st_q[fc].count)
                    else:
                        vb = fb - 8

                        @block.tensor
                        def _(tensor, vb=vb, bi=bi, buf=buf, g=g, lf=lf,
                              XN_DONE=XN_DONE):
                            tensor = EW(tensor)
                            [tensor.wait_ge(s_, c_) for s_, c_ in lf]
                            tensor.wait_ge(d_pe.sem, XN_DONE)
                            for tsub in range(4):
                                bank = 2 + tsub
                                tok = qkv_bank_tok[bank]
                                tensor.wait_ge(tok[0].sem, tok[1])
                                for ec in range(ND):
                                    tensor.matmul(pb[bank][:, 0:256],
                                                  bass.AP(big3, ec * 512 + tsub * 128,
                                                          [[B3F, 128], [1, 128]]),
                                                  bass.AP(buf, ec * 256,
                                                          [[2048, 128], [1, 256]]),
                                                  start=(ec == 0),
                                                  stop=(ec == ND - 1))
                                pe_d.inc(tensor.last)
                            buf_free[bi] = (pe_d, pe_d.count)

                        PDB = pe_d.count - 4
                        for tsub in range(4):
                            @block.vector
                            def _(vector, vb=vb, g=g, tsub=tsub, PDB=PDB):
                                vector = EW(vector)
                                bank = 2 + tsub
                                slot = tsub % 2
                                pe_d.wait(vector, PDB + tsub + 1)
                                tok = vst_tok.get(slot)
                                if tok is not None:
                                    vector.wait_ge(tok[0].sem, tok[1])
                                vector.tensor_copy(
                                    bass.AP(big2, 6144 + slot * 260,
                                            [[B2F, 128], [65, 4], [1, 64]]),
                                    pb[bank][:, 0:256].rearrange(
                                        "p (a d) -> p a d", d=64))
                                vector.drain()
                                vector.memset(
                                    bass.AP(big2, 6144 + slot * 260 + 64,
                                            [[B2F, 128], [65, 4], [1, 1]]), 1.0)
                                d_sp.inc(vector.drain())
                                d_pe.inc_eng(vector)
                                qkv_bank_tok[bank] = (d_pe, d_pe.count)

                            @block.sync
                            def _(sync, vb=vb, g=g, tsub=tsub):
                                slot = tsub % 2
                                sync.wait_ge(d_sp.sem, d_sp.count)
                                tile = g * 4 + tsub
                                st_l[slot].inc(sync.dma_start(
                                    out=bass.AP(vl1_scr_t,
                                                tile * 128 * 1040 + vb * 260,
                                                [[1040, 128], [1, 260]]),
                                    in_=bass.AP(big2, 6144 + slot * 260,
                                                [[B2F, 128], [1, 260]])), 16)
                                vst_tok[slot] = (st_l[slot], st_l[slot].count)

                DP0, DA0, APE0 = d_pe.count, d_a.count, pe_a.count
                TS6 = [128, 256, 384, 0]
                stage6 = [bass.AP(big2, 7936, [[B2F, 6], [1, 128]]),
                          bass.AP(big2, 8064, [[B2F, 6], [1, 128]]),
                          small[0:6, 384:512],
                          small[0:6, 96:224]]
                for tt in range(4):
                    i = g * 4 + tt
                    BB = 7744 + tt * 48

                    def bb(off, n=6, BB=BB):
                        return bass.AP(big2, BB + off, [[B2F, 128], [1, n]])

                    @block.tensor
                    def _(tensor, tt=tt, DP0=DP0, APE0=APE0):
                        tensor = EW(tensor)
                        tensor.wait_ge(d_pe.sem, DP0)
                        tensor.wait_ge(pe_a.sem, APE0)
                        for dc in range(ND):
                            tensor.matmul(pb[5][:, tt * 6:(tt + 1) * 6],
                                          bass.AP(big2, dc * 512 + tt * 128,
                                                  [[B2F, 128], [1, 128]]),
                                          bass.AP(big2, 7688 + dc * 6,
                                                  [[B2F, 128], [1, 6]]),
                                          start=(dc == 0), stop=(dc == ND - 1))
                        pe_a.inc(tensor.last)

                    @block.scalar
                    def _(scalar, tt=tt, bb=bb, DP0=DP0, DA0=DA0):
                        scalar = EW(scalar)
                        pe_a.wait(scalar)
                        scalar.wait_ge(d_pe.sem, DP0)
                        scalar.wait_ge(d_a.sem, DA0)
                        scalar.activation(bb(0), pb[5][:, tt * 6:(tt + 1) * 6],
                                          AF.Tanh, scale=cols[:, 63:64])
                        a_pe.inc(scalar.drain())
                        a_d.inc_eng(scalar)

                    @block.tensor
                    def _(tensor, tt=tt, bb=bb):
                        tensor = EW(tensor)
                        a_pe.wait(tensor)
                        tensor.transpose(pb[5][0:6, TS6[tt]:TS6[tt] + 128],
                                         bb(0), ident[:, :])
                        pe_d.inc(tensor.last)

                    @block.vector
                    def _(vector, tt=tt):
                        vector = EW(vector)
                        pe_d.wait(vector)
                        vector.tensor_copy(stage6[tt],
                                           pb[5][0:6, TS6[tt]:TS6[tt] + 128])
                        d_pe.inc(vector.drain())
                    DT6 = d_pe.count

                    @block.tensor
                    def _(tensor, tt=tt, DT6=DT6, DA0=DA0):
                        tensor = EW(tensor)
                        tensor.wait_ge(d_pe.sem, DT6)
                        tensor.wait_ge(d_a.sem, DA0)
                        tensor.matmul(pb[4][:, 256 + tt * 6:256 + (tt + 1) * 6],
                                      stage6[tt],
                                      bass.AP(big2, 7736, [[B2F, 6], [1, 6]]),
                                      start=True, stop=True)
                        pe_d.inc(tensor.last)

                    @block.vector
                    def _(vector, i=i, tt=tt, bb=bb):
                        vector = EW(vector)
                        pe_d.wait(vector)
                        vector.tensor_scalar(bb(6),
                                             pb[4][:, 256 + tt * 6:
                                                   256 + (tt + 1) * 6],
                                             6.0, cols[:, 60:61],
                                             ALU.subtract, ALU.mult)
                        vector.drain()
                        d_pe.inc(vector.last)
                        vector.reduce_max(cols[:, 28:29], bb(6), AX.X)
                        vector.drain()
                        vector.tensor_scalar(bb(12), bb(6),
                                             cols[:, 28:29], None, ALU.is_ge)
                        vector.drain()
                        vector.scalar_tensor_tensor(bb(18),
                                                    bb(12), -1000.0,
                                                    iotas[:, 8:14],
                                                    ALU.mult, ALU.add)
                        vector.drain()
                        vector.tensor_reduce(cols[:, 29:30], bb(18),
                                             AX.X, ALU.min)
                        vector.drain()
                        vector.tensor_scalar(cols[:, 30:31], cols[:, 29:30],
                                             1000.0, None, ALU.add)
                        vector.drain()
                        vector.tensor_scalar(bb(12), iotas[:, 8:14],
                                             cols[:, 30:31], None, ALU.is_equal)
                        vector.drain()
                        vector.scalar_tensor_tensor(bb(24),
                                                    bb(12), -1.0e5,
                                                    bb(6),
                                                    ALU.mult, ALU.add)
                        vector.drain()
                        vector.reduce_max(cols[:, 31:32], bb(24), AX.X)
                        vector.drain()
                        vector.tensor_scalar(bb(30), bb(24),
                                             cols[:, 31:32], None, ALU.is_ge)
                        vector.drain()
                        vector.scalar_tensor_tensor(bb(36),
                                                    bb(30), -1000.0,
                                                    iotas[:, 8:14],
                                                    ALU.mult, ALU.add)
                        vector.drain()
                        vector.tensor_reduce(cols[:, 32:33], bb(36),
                                             AX.X, ALU.min)
                        vector.drain()
                        vector.tensor_scalar(cols[:, 33:34], cols[:, 32:33],
                                             1000.0, None, ALU.add)
                        vector.drain()
                        vector.tensor_scalar(bb(30), iotas[:, 8:14],
                                             cols[:, 33:34], None, ALU.is_equal)
                        vector.drain()
                        vector.tensor_scalar(cols[:, 34:35], cols[:, 31:32],
                                             cols[:, 28:29], None, ALU.subtract)
                        d_a.inc(vector.drain())

                    @block.scalar
                    def _(scalar):
                        scalar = EW(scalar)
                        d_a.wait(scalar)
                        scalar.activation(cols[:, 35:36], cols[:, 34:35], AF.Exp)
                        a_d.inc(scalar.drain())

                    @block.vector
                    def _(vector, i=i, tt=tt, bb=bb):
                        vector = EW(vector)
                        a_d.wait(vector)
                        vector.tensor_scalar(cols[:, 36:37], cols[:, 35:36], 1.0,
                                             None, ALU.add)
                        vector.drain()
                        vector.reciprocal(cols[:, 37:38], cols[:, 36:37])
                        vector.drain()
                        vector.tensor_tensor(cols[:, 38:39], cols[:, 35:36],
                                             cols[:, 37:38], ALU.mult)
                        vector.drain()
                        vector.tensor_add(small[:, i * 6:(i + 1) * 6],
                                          bb(12), bb(30))
                        vector.tensor_copy(small[:, 320 + i * 4:321 + i * 4],
                                           cols[:, 29:30])
                        vector.tensor_copy(small[:, 321 + i * 4:322 + i * 4],
                                           cols[:, 32:33])
                        vector.tensor_copy(small[:, 322 + i * 4:323 + i * 4],
                                           cols[:, 37:38])
                        vector.tensor_copy(small[:, 323 + i * 4:324 + i * 4],
                                           cols[:, 38:39])
                        d_pe.inc(vector.drain())
                qkv_bank_tok[4] = (d_a, d_a.count)
                qkv_bank_tok[5] = (d_pe, d_pe.count)

            C1 = 1024
            TRASH1 = 6 * C1
            XN1ST = st.count
            @block.gpsimd
            def _(gpsimd):
                gpsimd = EW(gpsimd)
                pe_a.wait(gpsimd)
                gpsimd.wait_ge(pe_d.sem, pe_d.count)
                gpsimd.wait_ge(pe_g.sem, pe_g.count)
                gpsimd.wait_ge(gdma.sem, MZERO6)
                mr[0].inc(gpsimd.dma_start(
                    out=bass.AP(big2, 512, [[B2F, 128], [128, 2], [1, 128]]),
                    in_=tri2_in.rearrange("a p d -> p a d")), 16)
            LTRI6 = mr[0].count

            GTOK6 = {0: None, 1: None}
            for i in range(NT):
                par = i % 2

                @block.tensor
                def _(tensor, i=i, LTRI6=LTRI6):
                    tensor = EW(tensor)
                    if i == 0:
                        tensor.wait_ge(mr[0].sem, LTRI6)
                    d_pe.wait(tensor)
                    for j in range(i):
                        tensor.matmul(pb[4][:, 0:6],
                                      bass.AP(big2, 512, [[B2F, 128], [1, 128]]),
                                      small[:, j * 6:(j + 1) * 6],
                                      start=(j == 0), stop=False)
                    tensor.matmul(pb[4][:, 0:6],
                                  bass.AP(big2, 640, [[B2F, 128], [1, 128]]),
                                  small[:, i * 6:(i + 1) * 6],
                                  start=(i == 0), stop=True)
                    pe_d.inc(tensor.last)

                @block.vector
                def _(vector, i=i, par=par):
                    vector = EW(vector)
                    pe_d.wait(vector)
                    tok = GTOK6[par]
                    if tok is not None:
                        vector.wait_ge(tok[0].sem, tok[1])
                    vector.tensor_copy(bass.AP(big2, 1152 + i * 8,
                                               [[B2F, 128], [1, 6]]),
                                       pb[4][:, 0:6])
                    vector.drain()
                    d_pe.inc_eng(vector)
                    vector.tensor_scalar(cols[:, 28 + par:29 + par],
                                         iotas[:, 14:15],
                                         float(i * 128), None, ALU.add)
                    vector.drain()
                    for k in range(2):
                        idxc = small[:, 320 + i * 4 + k:321 + i * 4 + k]
                        vector.tensor_scalar(cols[:, 40:41], idxc, 1000.0,
                                             None, ALU.add)
                        vector.drain()
                        vector.tensor_scalar(small[:, 288:294], iotas[:, 8:14],
                                             cols[:, 40:41], None, ALU.is_equal)
                        vector.drain()
                        vector.tensor_tensor(small[:, 296:302],
                                             small[:, 288:294],
                                             bass.AP(big2, 1152 + i * 8,
                                                     [[B2F, 128], [1, 6]]),
                                             ALU.mult)
                        vector.drain()
                        vector.reduce_sum(cols[:, 41:42], small[:, 296:302],
                                          AX.X)
                        vector.drain()
                        vector.tensor_scalar(cols[:, 42:43], cols[:, 41:42],
                                             float(C1), None, ALU.is_ge)
                        vector.tensor_scalar(cols[:, 43:44], idxc, float(C1),
                                             cols[:, 41:42], ALU.mult, ALU.add)
                        vector.drain()
                        vector.tensor_scalar(cols[:, 47:48],
                                             cols[:, 28 + par:29 + par],
                                             2.0, float(TRASH1 + k),
                                             ALU.mult, ALU.add)
                        vector.drain()
                        vector.tensor_scalar(cols[:, 44:45], cols[:, 43:44],
                                             -1.0, cols[:, 47:48], ALU.mult,
                                             ALU.add)
                        vector.drain()
                        vector.scalar_tensor_tensor(cols[:, 45:46],
                                                    cols[:, 44:45],
                                                    cols[:, 42:43],
                                                    cols[:, 43:44],
                                                    ALU.mult, ALU.add)
                        vector.drain()
                        vector.tensor_copy(i32g[:, 4 * par + k:4 * par + k + 1],
                                           cols[:, 45:46])
                        vector.drain()
                    d_sp.inc_eng(vector)
                DSPR6 = d_sp.count

                @block.gpsimd
                def _(gpsimd, i=i, par=par, DSPR6=DSPR6):
                    gpsimd = EW(gpsimd)
                    gpsimd.wait_ge(d_sp.sem, DSPR6)
                    if i == 0:
                        gpsimd.wait_ge(gdma.sem, MZERO6)
                    for k in range(2):
                        sc[par].inc(gpsimd.indirect_dma_start(
                            out=bass.AP(inv2_t, 0, [[1, 32768], [1, 1]]),
                            out_offset=IndirectOffsetOnAxis(
                                i32g[:, 4 * par + k:4 * par + k + 1], 0),
                            in_=cols[:, 28 + par:29 + par], in_offset=None), 16)
                        sc[par].inc(gpsimd.indirect_dma_start(
                            out=bass.AP(inv2_t, 0, [[1, 32768], [1, 1]]),
                            out_offset=IndirectOffsetOnAxis(
                                i32g[:, 4 * par + k:4 * par + k + 1], 0),
                            in_=small[:, 322 + i * 4 + k:323 + i * 4 + k],
                            in_offset=None, element_offset=16384), 16)
                GTOK6[par] = (sc[par], sc[par].count)
            SCAT6 = (sc[0].count, sc[1].count)

            @block.gpsimd
            def _(gpsimd):
                gpsimd = EW(gpsimd)
                gpsimd.wait_ge(sc[0].sem, SCAT6[0])
                gpsimd.wait_ge(sc[1].sem, SCAT6[1])
                for ch_ in (st_l[0], st_l[1], st_q[0], st_q[1]):
                    ch_.wait(gpsimd)
                mr[0].inc(gpsimd.dma_start(
                    out=bass.AP(big2, 6144, [[B2F, 48], [1, 128]]),
                    in_=bass.AP(inv2_t, 0, [[128, 48], [1, 128]])), 16)
                mr[0].inc(gpsimd.dma_start(
                    out=bass.AP(big2, 48 * B2F + 6144, [[B2F, 48], [1, 128]]),
                    in_=bass.AP(inv2_t, 16384, [[128, 48], [1, 128]])), 16)
            MRP6 = mr[0].count

            @block.tensor
            def _(tensor, MRP6=MRP6):
                tensor = EW(tensor)
                tensor.wait_ge(mr[0].sem, MRP6)
                d_pe.wait(tensor)
                tensor.transpose(pb[7][:, 0:96],
                                 bass.AP(big2, 6144, [[B2F, 96], [1, 128]]),
                                 ident[0:96, 0:96])
                pe_d.inc(tensor.last)
            PEP6 = pe_d.count

            @block.vector
            def _(vector, PEP6=PEP6):
                vector = EW(vector)
                vector.wait_ge(pe_d.sem, PEP6)
                vector.tensor_copy(
                    bass.AP(big2, 6272, [[B2F, 128], [1, 96]]),
                    pb[7][:, 0:96])
                vector.drain()
                vector.tensor_copy(
                    i32a[:, 0:48],
                    bass.AP(big2, 6272, [[B2F, 128], [1, 48]]))
                d_pe.inc(vector.drain())
                d_sp.inc_eng(vector)
            IDP6_RDY = d_sp.count

            ring6 = [(big2, 0, B2F), (big2, 1024, B2F),
                     (big2, 2048, B2F), (big2, 3072, B2F)]
            ring6_tok = {0: None, 1: None, 2: None, 3: None}
            ost6_slots = [(stg, 0, 1024), (attn_sb, 0, 2048),
                          (attn_sb, 1024, 2048), (big2, 6368, B2F)]
            pend_sc6 = []

            def emit_sc6(cts, cc0, DSE6):
                ot, ooff, OF = ost6_slots[cts]

                @block.gpsimd
                def _(gpsimd, cts=cts, cc0=cc0, DSE6=DSE6,
                      ot=ot, ooff=ooff, OF=OF):
                    gpsimd = EW(gpsimd)
                    gpsimd.wait_ge(d_sp.sem, DSE6)
                    gpsimd.wait_ge(mr[1].sem, mr[1].count)
                    mr[1].inc(gpsimd.indirect_dma_start(
                        out=bass.AP(moe_acc_t, 0, [[D, T], [1, D]]),
                        out_offset=IndirectOffsetOnAxis(
                            i32a[:, cc0 + cts:cc0 + cts + 1], 0),
                        in_=bass.AP(ot, ooff, [[OF, 128], [1, 1024]]),
                        in_offset=None,
                        compute_op=ALU.add), 16)
            h1t6_tok = (d_sp, d_sp.count)
            SPAN1 = 512
            NCSP1 = 4
            ffn_bank_tok = {b: (d_pe, d_pe.count) for b in range(8)}

            def emit_gather6(cc):
                @block.gpsimd
                def _(gpsimd, cc=cc):
                    gpsimd = EW(gpsimd)
                    if cc == 0:
                        gpsimd.wait_ge(d_sp.sem, IDP6_RDY)
                        gpsimd.wait_ge(st.sem, XN1ST)
                    tok = ring6_tok[cc % 4]
                    if tok is not None:
                        gpsimd.wait_ge(tok[0].sem, tok[1])
                    rt, roff, F = ring6[cc % 4]
                    vr[cc % 4].inc(gpsimd.indirect_dma_start(
                        out=bass.AP(rt, roff, [[F, 128], [1, 1024]]),
                        out_offset=None,
                        in_=bass.AP(xn_scr_t, 0, [[D, T], [1, D]]),
                        in_offset=IndirectOffsetOnAxis(
                            i32a[:, cc:cc + 1], 0)), 16)
                return vr[cc % 4].count
            gather_vrc6 = {c: emit_gather6(c) for c in range(NCSP1)}

            for e in range(6):
                for half in range(2):
                    for c in range(NCSP1):
                        cc = e * 2 * NCSP1 + half * NCSP1 + c
                        VRC = gather_vrc6.pop(cc)

                        @block.tensor
                        def _(tensor, c=c, cc=cc, VRC=VRC):
                            tensor = EW(tensor)
                            tensor.wait_ge(vr[cc % 4].sem, VRC)
                            d_pe.wait(tensor)
                            rt, roff, F = ring6[cc % 4]
                            for half2 in range(2):
                                for cch in range(4):
                                    tensor.transpose(
                                        pb[6 + half2][:, ts(cch)],
                                        bass.AP(rt, roff + (half2 * 4 + cch) * 128,
                                                [[F, 128], [1, 128]]),
                                        ident[:, :])
                                pe_a.inc(tensor.last)
                        ring6_tok[cc % 4] = (pe_a, pe_a.count)

                        @block.vector
                        def _(vector, c=c):
                            vector = EW(vector)
                            pe_a.wait(vector)
                            for half2 in range(2):
                                vector.tensor_copy(
                                    bass.AP(big3, (4 * half2) * SPAN1 + c * 128,
                                            [[B3F, 128], [SPAN1, 4], [1, 128]]),
                                    pb[6 + half2][:, 0:512].rearrange(
                                        "p (a d) -> p a d", d=128))
                            d_pe.inc(vector.drain())
                    XST6 = d_pe.count
                    nh0 = (e * 2 + half + 1) * NCSP1
                    for cc2 in range(nh0, min(nh0 + NCSP1, 12 * NCSP1)):
                        gather_vrc6[cc2] = emit_gather6(cc2)
                    for args in pend_sc6:
                        emit_sc6(*args)
                    pend_sc6 = []

                    for blk in range(2):
                        bi = blk % 2
                        buf = [wA, wB][bi]

                        @block.sync
                        def _(sync, e=e, blk=blk, bi=bi, buf=buf):
                            load_weight(sync, bi,
                                        bass.AP(buf, 0,
                                                [[2048, 128], [256, 8], [1, 256]]),
                                        w1f[e][:, blk * 256:(blk + 1) * 256]
                                        .rearrange("(c p) f -> p c f", p=128))
                        lw = ((chW[2 * bi].sem, chW[2 * bi].count),
                              (chW[2 * bi + 1].sem, chW[2 * bi + 1].count))

                        @block.tensor
                        def _(tensor, blk=blk, bi=bi, buf=buf, lw=lw,
                              XST6=XST6):
                            tensor = EW(tensor)
                            [tensor.wait_ge(s_, c_) for s_, c_ in lw]
                            tensor.wait_ge(d_pe.sem, XST6)
                            for hh in range(2):
                                hc = blk * 2 + hh
                                bank = hc % 2
                                tok = ffn_bank_tok[bank]
                                tensor.wait_ge(tok[0].sem, tok[1])
                                for dc in range(ND):
                                    tensor.matmul(pb[bank][:, 0:512],
                                                  bass.AP(buf, dc * 256 + hh * 128,
                                                          [[2048, 128], [1, 128]]),
                                                  bass.AP(big3, dc * SPAN1,
                                                          [[B3F, 128], [1, SPAN1]]),
                                                  start=(dc == 0),
                                                  stop=(dc == ND - 1))
                                pe_a.inc(tensor.last)
                            buf_free[bi] = (pe_a, pe_a.count)

                        @block.scalar
                        def _(scalar, blk=blk):
                            scalar = EW(scalar)
                            if blk == 0:
                                scalar.wait_ge(h1t6_tok[0].sem, h1t6_tok[1])
                            for hh in range(2):
                                hc = blk * 2 + hh
                                bank = hc % 2
                                pe_a.wait(scalar, pe_a.count - 1 + hh)
                                scalar.activation(
                                    bass.AP(big2, 4096 + hc * 512,
                                            [[B2F, 128], [1, 512]]),
                                    pb[bank][:, 0:512], AF.Gelu_apprx_tanh)
                                a_pe.inc(scalar.drain())
                                ffn_bank_tok[bank] = (a_pe, a_pe.count)

                    A_PE_H6 = a_pe.count
                    for hc in range(4):
                        slot = hc % 4
                        sbuf_t = [wA, wA, wB, wB][slot]
                        soff = [0, 1024, 0, 1024][slot]

                        @block.sync
                        def _(sync, e=e, hc=hc, slot=slot, sbuf_t=sbuf_t,
                              soff=soff):
                            sync = EW(sync)
                            tok = w2slot_tok.get(slot)
                            if tok is not None:
                                sync.wait_ge(tok[0].sem, tok[1])
                            tok2 = buf_free[0 if sbuf_t is wA else 1]
                            if tok2 is not None:
                                sync.wait_ge(tok2[0].sem, tok2[1])
                            chW[slot].inc(sync.dma_start(
                                out=bass.AP(sbuf_t, soff, [[2048, 128], [1, 1024]]),
                                in_=w2f[e][hc * 128:(hc + 1) * 128, :]), 16)
                        lw2 = ((chW[slot].sem, chW[slot].count),)

                        @block.tensor
                        def _(tensor, hc=hc, slot=slot, sbuf_t=sbuf_t,
                              soff=soff, lw2=lw2, A_PE_H6=A_PE_H6):
                            tensor = EW(tensor)
                            [tensor.wait_ge(s_, c_) for s_, c_ in lw2]
                            if hc == 0:
                                tensor.wait_ge(a_pe.sem, A_PE_H6)
                                for b in range(8):
                                    tok = ffn_bank_tok[b]
                                    tensor.wait_ge(tok[0].sem, tok[1])
                            for cts in range(NCSP1):
                                for db in range(2):
                                    bank = cts * 2 + db
                                    tensor.matmul(
                                        pb[bank][:, 0:512],
                                        bass.AP(big2, 4096 + hc * 512 + cts * 128,
                                                [[B2F, 128], [1, 128]]),
                                        bass.AP(sbuf_t, soff + db * 512,
                                                [[2048, 128], [1, 512]]),
                                        start=(hc == 0), stop=(hc == 3))
                            pe_g.inc(tensor.last)
                        w2slot_tok[slot] = (pe_g, pe_g.count)
                    h1t6_tok = (pe_g, pe_g.count)
                    buf_free[0] = (pe_g, pe_g.count)
                    buf_free[1] = (pe_g, pe_g.count)
                    W2D6 = pe_g.count

                    cc0 = e * 2 * NCSP1 + half * NCSP1
                    for cts in range(NCSP1):
                        ot, ooff, OF = ost6_slots[cts]

                        @block.vector
                        def _(vector, cts=cts, cc0=cc0, W2D6=W2D6,
                              ot=ot, ooff=ooff, OF=OF):
                            vector = EW(vector)
                            vector.wait_ge(pe_g.sem, W2D6)
                            vector.wait_ge(mr[1].sem, mr[1].count)
                            for db in range(2):
                                bank = cts * 2 + db
                                vector.tensor_scalar(
                                    bass.AP(ot, ooff + db * 512,
                                            [[OF, 128], [1, 512]]),
                                    pb[bank][:, 0:512],
                                    bass.AP(big2, 6320 + cc0 + cts,
                                            [[B2F, 128], [1, 1]]),
                                    None, ALU.mult)
                                vector.drain()
                                d_pe.inc_eng(vector)
                                ffn_bank_tok[bank] = (d_pe, d_pe.count)
                            d_sp.inc_eng(vector)
                        pend_sc6.append((cts, cc0, d_sp.count))

            for args in pend_sc6:
                emit_sc6(*args)
            pend_sc6 = []

            MOE6_SC = mr[1].count
            add6_tok = {0: None, 1: None, 2: None, 3: None}
            for i in range(NT):
                slot = i % 4

                @block.gpsimd
                def _(gpsimd, i=i, slot=slot):
                    gpsimd = EW(gpsimd)
                    if i == 0:
                        gpsimd.wait_ge(mr[1].sem, MOE6_SC)
                    tok = add6_tok[slot]
                    if tok is not None:
                        gpsimd.wait_ge(tok[0].sem, tok[1])
                    vr[slot].inc(gpsimd.dma_start(
                        out=bass.AP(big3, slot * 1024, [[B3F, 128], [1, 1024]]),
                        in_=bass.AP(moe_acc_t, i * 128 * D, [[D, 128], [1, D]])),
                        16)
                VRA = vr[slot].count

                @block.vector
                def _(vector, i=i, slot=slot, VRA=VRA):
                    vector = EW(vector)
                    vector.wait_ge(vr[slot].sem, VRA)
                    vector.tensor_add(x_sb[:, i, :], x_sb[:, i, :],
                                      bass.AP(big3, slot * 1024,
                                              [[B3F, 128], [1, 1024]]))
                    d_pe.inc(vector.drain())
                    d_sp.inc_eng(vector)
                add6_tok[slot] = (d_pe, d_pe.count)
                DSY = d_sp.count

                @block.sync
                def _(sync, i=i, DSY=DSY):
                    sync = EW(sync)
                    sync.wait_ge(d_sp.sem, DSY)
                    st.inc(sync.dma_start(
                        out=bass.AP(y_t, (i * 128) * D, [[D, 128], [1, D]]),
                        in_=x_sb[:, i, :]), 16)
            Y_BASE = st.count

            if debug:
                @block.sync
                def _(sync):
                    d_pe.wait(sync)
                    st.inc(sync.dma_start(out=taps["t_kT"].ap(),
                                          in_=bass.AP(fmA, 0, [[FMF, 128], [1, FMF]])), 16)

        if stage >= 7:
            @block.sync
            def _(sync):
                sync.wait_ge(d_pe.sem, P5D)
                sync.wait_ge(pe_a.sem, P5A)
                sync.wait_ge(d_sp.sem, P5S)
                sync.wait_ge(mr[1].sem, mr[1].count)
                ld.inc(sync.dma_start(
                    out=bass.AP(attn_sb, 0, [[2048, 128], [512, 4], [1, 512]]),
                    in_=mask4_in.rearrange("a p d -> p a d")), 16)
            LMASK = ld.count
            PEA7, PEG7 = pe_a.count, pe_g.count
            MR7 = mr[1].count

            exp_tok = {0: (pe_g, 0), 1: (pe_g, 0), 2: (pe_g, 0), 3: (pe_g, 0)}
            ut_tok = (d_pe, 0)
            ost_tok = (gdma, 0)
            head_tok = {0: None, 1: None}
            SHIFT = 3
            for h in range(16):
                p0 = (h % 2) * 64
                vhoff = 4096 + (h % 2) * 1040

                @block.gpsimd
                def _(gpsimd, h=h, p0=p0, vhoff=vhoff):
                    gpsimd = EW(gpsimd)
                    if h < 2:
                        gpsimd.wait_ge(pe_a.sem, PEA7)
                        gpsimd.wait_ge(pe_g.sem, PEG7)
                        gpsimd.wait_ge(mr[1].sem, MR7)
                        for ch_ in (st_q[0], st_q[1], st_l[0], st_l[1]):
                            ch_.wait(gpsimd)
                    else:
                        tok = head_tok[h % 2]
                        gpsimd.wait_ge(tok[0].sem, tok[1])
                    hq.inc(gpsimd.dma_start(
                        out=bass.AP(big2, p0 * B2F + 2048, [[B2F, 64], [1, T]]),
                        in_=bass.AP(qt_scr_t, (h * 64) * T, [[T, 64], [1, T]])), 16)
                    hq.inc(gpsimd.dma_start(
                        out=bass.AP(big2, vhoff, [[B2F, 128], [65, 16], [1, 65]]),
                        in_=bass.AP(vl1_scr_t, h * 65,
                                    [[1040, 128], [128 * 1040, 16], [1, 65]])), 16)
                HL = hq.count

                for g in range(NG):
                    nch = 4 * g + 4

                    def emit_pv(c, sslot, EXPC, h=h, g=g, nch=nch,
                                vhoff=vhoff):
                        @block.tensor
                        def _(tensor, c=c, sslot=sslot, EXPC=EXPC):
                            tensor = EW(tensor)
                            tensor.wait_ge(a_pe.sem, EXPC)
                            if c == 0:
                                tensor.wait_ge(ut_tok[0].sem, ut_tok[1])
                            tensor.matmul(
                                pb[4][0:65, 0:512],
                                bass.AP(big2, vhoff + c * 65,
                                        [[B2F, 128], [1, 65]]),
                                bass.AP(big3, sslot * 512,
                                        [[B3F, 128], [1, 512]]),
                                start=(c == 0), stop=(c == nch - 1))
                            pe_g.inc(tensor.last)
                            exp_tok[sslot] = (pe_g, pe_g.count)

                    pend = []
                    for c in range(nch):
                        sslot = c % 4

                        @block.tensor
                        def _(tensor, h=h, g=g, c=c, sslot=sslot, p0=p0, HL=HL):
                            tensor = EW(tensor)
                            if c == 0 and g == 0:
                                tensor.wait_ge(hq.sem, HL)
                            tok = exp_tok[sslot]
                            tensor.wait_ge(tok[0].sem, tok[1])
                            tensor.matmul(
                                pb[sslot][:, 0:512],
                                bass.AP(fmA, p0 * FMF + (h // 2) * T + c * 128,
                                        [[FMF, 64], [1, 128]]),
                                bass.AP(big2, p0 * B2F + 2048 + g * 512,
                                        [[B2F, 64], [1, 512]]),
                                start=True, stop=True)
                            pe_d.inc(tensor.last)
                        SMM = pe_d.count

                        dd = c - 4 * g

                        if dd >= 0:
                            @block.vector
                            def _(vector, h=h, c=c, g=g, sslot=sslot, dd=dd,
                                  SMM=SMM):
                                vector = EW(vector)
                                if h == 0 and g == 0 and c == 0:
                                    vector.wait_ge(ld.sem, LMASK)
                                vector.wait_ge(pe_d.sem, SMM)
                                vector.tensor_add(
                                    pb[sslot][:, 0:512], pb[sslot][:, 0:512],
                                    bass.AP(attn_sb, dd * 512,
                                            [[2048, 128], [1, 512]]))
                                d_a.inc(vector.drain())
                        DAC = d_a.count

                        @block.scalar
                        def _(scalar, sslot=sslot, dd=dd, SMM=SMM, DAC=DAC):
                            scalar = EW(scalar)
                            if dd >= 0:
                                scalar.wait_ge(d_a.sem, DAC)
                            else:
                                scalar.wait_ge(pe_d.sem, SMM)
                            scalar.activation(
                                bass.AP(big3, sslot * 512,
                                        [[B3F, 128], [1, 512]]),
                                pb[sslot][:, 0:512], AF.Exp)
                            a_pe.inc(scalar.drain())
                        EXPC = a_pe.count

                        pend.append((c, sslot, EXPC))
                        if len(pend) > SHIFT:
                            emit_pv(*pend.pop(0))
                    while pend:
                        emit_pv(*pend.pop(0))

                    @block.vector
                    def _(vector):
                        vector = EW(vector)
                        pe_g.wait(vector)
                        vector.tensor_copy(
                            bass.AP(big2, 6416, [[B2F, 65], [1, 512]], ),
                            pb[4][0:65, 0:512])
                        d_pe.inc(vector.drain())
                    UT = d_pe.count
                    ut_tok = (d_pe, UT)

                    @block.tensor
                    def _(tensor, UT=UT):
                        tensor = EW(tensor)
                        tensor.wait_ge(d_pe.sem, UT)
                        d_pe.wait(tensor)
                        for tq in range(4):
                            tensor.transpose(
                                pb[5][:, tq * 65:(tq + 1) * 65],
                                bass.AP(big2, 6416 + tq * 128,
                                        [[B2F, 65], [1, 128]]),
                                ident[0:65, 0:65])
                        pe_d.inc(tensor.last)
                    TRP = pe_d.count

                    @block.vector
                    def _(vector, TRP=TRP):
                        vector = EW(vector)
                        vector.wait_ge(pe_d.sem, TRP)
                        vector.wait_ge(ost_tok[0].sem, ost_tok[1])
                        vector.reciprocal(cols[:, 52:56],
                                          bass.AP(pb[5], 64, [[512, 128], [65, 4], [1, 1]]))
                        vector.drain()
                        vector.tensor_tensor(
                            bass.AP(big2, 6928, [[B2F, 128], [64, 4], [1, 64]]),
                            bass.AP(pb[5], 0, [[512, 128], [65, 4], [1, 64]]),
                            bass.AP(cols, 52, [[64, 128], [1, 4], [0, 64]]),
                            ALU.mult)
                        d_pe.inc(vector.drain())
                        d_sp.inc_eng(vector)
                    OSG = d_sp.count

                    @block.gpsimd
                    def _(gpsimd, h=h, g=g, OSG=OSG):
                        gpsimd = EW(gpsimd)
                        if h == 0 and g == 0:
                            st.wait(gpsimd)
                        gpsimd.wait_ge(d_sp.sem, OSG)
                        gdma.inc(gpsimd.dma_start(
                            out=bass.AP(y_t, (g * 512) * D + h * 64,
                                        [[D, 128], [128 * D, 4], [1, 64]]),
                            in_=bass.AP(big2, 6928, [[B2F, 128], [64, 4], [1, 64]]),
                            accum_op=ALU.add), 16)
                    ost_tok = (gdma, gdma.count)
                head_tok[h % 2] = (pe_g, pe_g.count)

        @block.sync
        def _(sync):
            for ch_ in ([gdma, st, ld, hq, chb1] + chW + st_a + st_v +
                        st_q + st_l + vr + mr + sc):
                sync.wait_ge(ch_.sem, ch_.count)

    return nc


def make_consts():
    ident = np.eye(128, dtype=np.float32)
    iotas = np.zeros((128, 16), dtype=np.float32)
    iotas[:, 0:8] = np.arange(1000, 1008, dtype=np.float32)[None, :]
    iotas[:, 8:14] = np.arange(1000, 1006, dtype=np.float32)[None, :]
    iotas[:, 14] = np.arange(128, dtype=np.float32)
    ones_row = np.ones((1, 128), dtype=np.float32)
    mask4 = np.zeros((4, 128, 512), dtype=np.float32)
    for d in range(4):
        kp = np.arange(128)[:, None]
        qf = np.arange(512)[None, :]
        mask4[d] = np.where(qf >= 128 * d + kp, 0.0, NEG).astype(np.float32)
    tri2 = np.zeros((2, 128, 128), dtype=np.float32)
    tri2[0] = 1.0
    tri2[1] = np.triu(np.ones((128, 128), np.float32), 1)
    return ident, iotas, ones_row, mask4, tri2


def make_in_map(b, L0_attn_query, L0_attn_W_qv, L0_attn_W2_0, L0_attn_W2_1,
                L0_attn_scale_weights, L0_attn_W_out, L0_ffn_Wg, L0_ffn_W1,
                L0_ffn_b1, L0_ffn_W2, L0_ffn_b2, L1_attn_gamma, L1_attn_beta,
                L1_attn_w_qkv, L1_ffn_proj_w, L1_ffn_anchors, L1_ffn_routing_temp,
                L1_ffn_w1, L1_ffn_w2, shared):
    ident, iotas, ones_row, mask4, tri2 = shared["consts"]
    f32 = np.float32
    m = {
        "x0": np.ascontiguousarray(L0_attn_query[b], dtype=f32),
        "wqv": shared["wqv"],
        "w2cat": shared["w2cat"],
        "sw_in": shared["sw"],
        "wout": shared["wout"],
        "wg": shared["wg"],
        "w1moe": shared["w1moe"],
        "b1moe": shared["b1moe"],
        "w2moe": shared["w2moe"],
        "b2moe": shared["b2moe"],
        "gamma": shared["gamma"],
        "beta": shared["beta"],
        "wqkT": shared["wqkT"],
        "pwT": shared["pwT"],
        "ancT": shared["ancT"],
        "rtemp": shared["rtemp"],
        "w1f": shared["w1f"],
        "w2f": shared["w2f"],
        "mask4": mask4,
        "ident": ident,
        "iotas": iotas,
        "ones_row": ones_row,
        "tri2": tri2,
    }
    return m


def prep_shared(L0_attn_W_qv, L0_attn_W2_0, L0_attn_W2_1, L0_attn_scale_weights,
                L0_attn_W_out, L0_ffn_Wg, L0_ffn_W1, L0_ffn_b1, L0_ffn_W2,
                L0_ffn_b2, L1_attn_gamma, L1_attn_beta, L1_attn_w_qkv,
                L1_ffn_proj_w, L1_ffn_anchors, L1_ffn_routing_temp, L1_ffn_w1,
                L1_ffn_w2):
    f32 = np.float32
    c = np.ascontiguousarray
    shared = {
        "consts": make_consts(),
        "wqv": c(L0_attn_W_qv, dtype=f32),
        "w2cat": c(np.concatenate([np.asarray(L0_attn_W2_0),
                                   np.asarray(L0_attn_W2_1)], axis=1), dtype=f32),
        "sw": c(np.asarray(L0_attn_scale_weights, dtype=f32).reshape(1, 2)),
        "wout": c(L0_attn_W_out, dtype=f32),
        "wg": c(L0_ffn_Wg, dtype=f32),
        "w1moe": c(L0_ffn_W1, dtype=f32),
        "b1moe": c(np.asarray(L0_ffn_b1, dtype=f32).reshape(8, 16, 128).transpose(0, 2, 1)),
        "w2moe": c(L0_ffn_W2, dtype=f32),
        "b2moe": c(L0_ffn_b2, dtype=f32),
        "gamma": c(np.asarray(L1_attn_gamma, dtype=f32).reshape(1, D)),
        "beta": c(np.asarray(L1_attn_beta, dtype=f32).reshape(1, D)),
        "wqkT": c(np.asarray(L1_attn_w_qkv, dtype=f32).T),
        "pwT": c(np.asarray(L1_ffn_proj_w, dtype=f32).T),
        "ancT": c(np.asarray(L1_ffn_anchors, dtype=f32).T),
        "rtemp": c(np.asarray(L1_ffn_routing_temp, dtype=f32).reshape(1, 1)),
        "w1f": c(L1_ffn_w1, dtype=f32),
        "w2f": c(L1_ffn_w2, dtype=f32),
    }
    return shared


def kernel(**inputs):
    shared = prep_shared(**{k: v for k, v in inputs.items()
                            if k not in ("L0_attn_query", "L0_attn_key",
                                         "L0_attn_value")})
    nc = build_program()
    in_maps = [make_in_map(b, shared=shared,
                           **{k: v for k, v in inputs.items()
                              if k not in ("L0_attn_key", "L0_attn_value")})
               for b in range(8)]
    res = run_bass_kernel_spmd(nc, in_maps, core_ids=list(range(8)))
    return np.stack([r["y"] for r in res.results], axis=0)


# version 13
# speedup vs baseline: 1.0328x; 1.0328x over previous
import numpy as np
import concourse.bass as bass
import concourse.mybir as mybir
from concourse.bass_utils import run_bass_kernel_spmd
from concourse.bass_core_rust import IndirectOffsetOnAxis

F32 = mybir.dt.float32
AF = mybir.ActivationFunctionType
ALU = mybir.AluOpType
AX = mybir.AxisListType

T = 2048
D = 1024
NT = 16
ND = 8
NG = 4
EPS = 1e-5
FA = 248
FM = 384
NEG = -1.0e30
C0 = 768
NCH0 = 8 * C0 // 128
TRASH0 = 8 * C0
FMF = ND * T
B2F = 8192
B3F = 4096


def ts(i, n=128):
    return slice(i * n, (i + 1) * n)



class EW:

    def __init__(self, e):
        self._e = e
        self.last = None

    def __getattr__(self, n):
        a = getattr(self._e, n)
        if callable(a):
            def w(*args, **kw):
                r = a(*args, **kw)
                if r is not None and hasattr(r, "then_inc"):
                    self.last = r
                return r
            return w
        return a


class Chan:
    def __init__(self, nc, es, name):
        self.sem = es.enter_context(nc.semaphore(name))
        self.count = 0

    def inc(self, instr, n=1):
        instr.then_inc(self.sem, n)
        self.count += n
        return instr

    def wait(self, eng, upto=None):
        eng.wait_ge(self.sem, self.count if upto is None else upto)

    def inc_eng(self, eng, n=1):
        eng.sem_inc(self.sem, n)
        self.count += n


def build_program(stage=99, debug=False):
    nc = bass.Bass("TRN2", target_bir_lowering=False)

    x_in = nc.dram_tensor("x0", [T, D], F32, kind="ExternalInput").ap()
    wqv = nc.dram_tensor("wqv", [D, 2 * D], F32, kind="ExternalInput").ap()
    w2cat_in = nc.dram_tensor("w2cat", [D, FA], F32, kind="ExternalInput").ap()
    sw_in = nc.dram_tensor("sw_in", [1, 2], F32, kind="ExternalInput").ap()
    wout_in = nc.dram_tensor("wout", [D, D], F32, kind="ExternalInput").ap()
    wg_in = nc.dram_tensor("wg", [D, 8], F32, kind="ExternalInput").ap()
    w1moe = nc.dram_tensor("w1moe", [8, D, 2048], F32, kind="ExternalInput").ap()
    b1moe = nc.dram_tensor("b1moe", [8, 128, 16], F32, kind="ExternalInput").ap()
    w2moe = nc.dram_tensor("w2moe", [8, 2048, D], F32, kind="ExternalInput").ap()
    b2moe = nc.dram_tensor("b2moe", [8, D], F32, kind="ExternalInput").ap()
    gamma_in = nc.dram_tensor("gamma", [1, D], F32, kind="ExternalInput").ap()
    beta_in = nc.dram_tensor("beta", [1, D], F32, kind="ExternalInput").ap()
    wqkT_in = nc.dram_tensor("wqkT", [D, 3 * D], F32, kind="ExternalInput").ap()
    pwT_in = nc.dram_tensor("pwT", [D, 6], F32, kind="ExternalInput").ap()
    ancT_in = nc.dram_tensor("ancT", [6, 6], F32, kind="ExternalInput").ap()
    rtemp_in = nc.dram_tensor("rtemp", [1, 1], F32, kind="ExternalInput").ap()
    w1f = nc.dram_tensor("w1f", [6, D, 512], F32, kind="ExternalInput").ap()
    w2f = nc.dram_tensor("w2f", [6, 512, D], F32, kind="ExternalInput").ap()
    mask4_in = nc.dram_tensor("mask4", [4, 128, 512], F32, kind="ExternalInput").ap()
    ident_in = nc.dram_tensor("ident", [128, 128], F32, kind="ExternalInput").ap()
    iotas_in = nc.dram_tensor("iotas", [128, 16], F32, kind="ExternalInput").ap()
    tri2_in = nc.dram_tensor("tri2", [2, 128, 128], F32, kind="ExternalInput").ap()
    ones_in = nc.dram_tensor("ones_row", [1, 128], F32, kind="ExternalInput").ap()

    y_t = nc.dram_tensor("y", [T, D], F32, kind="ExternalOutput")
    m_scr_t = nc.dram_tensor("m_scr", [4, NT, 128, FM], F32)
    v0_scr_t = nc.dram_tensor("v0_scr", [NT, 128, D], F32)
    atn_scr_t = nc.dram_tensor("atn_scr", [NT, 128, FA], F32)
    qt_scr_t = nc.dram_tensor("qt_scr", [1024, T], F32)
    vl1_scr_t = nc.dram_tensor("vl1_scr", [NT, 128, 16 * 65], F32)
    gwt_scr_t = nc.dram_tensor("gwt_scr", [8, NT, 128], F32)
    xn_scr_t = nc.dram_tensor("xn_scr", [T, D], F32)
    moe_acc_t = nc.dram_tensor("moe_acc", [T, D], F32)
    inv2_t = nc.dram_tensor("inv2_scr", [2, 16384], F32)
    taps = {}
    if debug:
        for nm, sz in [("t_xnT", ND * T), ("t_attn", NT * FA), ("t_x1", NT * D),
                       ("t_gw", 128), ("t_x2", NT * D), ("t_gw6", 96),
                       ("t_kT", ND * T), ("t_rqT", 4096)]:
            taps[nm] = nc.dram_tensor(nm, [128, sz], F32)

    import contextlib
    with contextlib.ExitStack() as es:
        ec = es.enter_context

        x_sb = ec(nc.sbuf_tensor("x_sb", [128, NT, D], F32))
        fmA = ec(nc.sbuf_tensor("fmA", [128, ND, T], F32))
        big2 = ec(nc.sbuf_tensor("big2", [128, B2F], F32))
        big3 = ec(nc.sbuf_tensor("big3", [128, B3F], F32))
        attn_sb = ec(nc.sbuf_tensor("attn_sb", [128, 2048], F32))
        wA = ec(nc.sbuf_tensor("wA", [128, 2048], F32))
        wB = ec(nc.sbuf_tensor("wB", [128, 2048], F32))
        stg = ec(nc.sbuf_tensor("stg", [128, 1, D], F32))
        small = ec(nc.sbuf_tensor("small", [128, 512], F32))
        cols = ec(nc.sbuf_tensor("cols", [128, 64], F32))
        ident = ec(nc.sbuf_tensor("ident_sb", [128, 128], F32))
        iotas = ec(nc.sbuf_tensor("iotas_sb", [128, 16], F32))
        b1sb = ec(nc.sbuf_tensor("b1sb", [128, 16], F32))
        i32g = ec(nc.sbuf_tensor("i32g", [128, 8], mybir.dt.int32))
        i32a = ec(nc.sbuf_tensor("i32a", [128, 48], mybir.dt.int32))
        pb = [ec(nc.psum_tensor(f"pb{i}", [128, 512], F32)) for i in range(8)]
        ones_r = small[0:1, 384:512]

        try:
            print(f"[build] sbuf bytes remaining: {nc.sbuf_bytes_remaining}")
        except Exception:
            pass

        def b2v(off, dims, p0=0, np_=128):
            return bass.AP(big2, p0 * B2F + off, [[B2F, np_]] + dims)

        def b3v(off, dims, p0=0, np_=128):
            return bass.AP(big3, p0 * B3F + off, [[B3F, np_]] + dims)

        def fmv(off, dims, p0=0, np_=128):
            return bass.AP(fmA, p0 * FMF + off, [[FMF, np_]] + dims)

        ld = Chan(nc, es, "ld")
        gdma = Chan(nc, es, "gdma")
        st = Chan(nc, es, "st")
        chW = [Chan(nc, es, f"chW{k}") for k in range(4)]
        st_a = [Chan(nc, es, f"st_a{k}") for k in range(2)]
        st_v = [Chan(nc, es, f"st_v{k}") for k in range(2)]
        st_q = [Chan(nc, es, f"st_q{k}") for k in range(2)]
        st_l = [Chan(nc, es, f"st_l{k}") for k in range(2)]
        vr = [Chan(nc, es, f"vr{k}") for k in range(4)]
        mr = [Chan(nc, es, f"mr{k}") for k in range(2)]
        sc = [Chan(nc, es, f"sc{k}") for k in range(2)]
        hq = Chan(nc, es, "hq")
        chb1 = Chan(nc, es, "chb1")
        pe_d = Chan(nc, es, "pe_d")
        pe_a = Chan(nc, es, "pe_a")
        pe_g = Chan(nc, es, "pe_g")
        d_pe = Chan(nc, es, "d_pe")
        a_pe = Chan(nc, es, "a_pe")
        d_a = Chan(nc, es, "d_a")
        a_d = Chan(nc, es, "a_d")
        d_sp = Chan(nc, es, "d_sp")
        block = ec(nc.Block())

        @block.sync
        def _(sync):
            ld.inc(sync.dma_start(out=x_sb[:, :, :],
                                  in_=x_in.rearrange("(a p) d -> p a d", p=128)), 16)
            ld.inc(sync.dma_start(out=ident[:, :], in_=ident_in), 16)
            ld.inc(sync.dma_start(out=iotas[:, :], in_=iotas_in), 16)
            ld.inc(sync.dma_start(out=small[0:1, 0:2], in_=sw_in), 16)
            ld.inc(sync.dma_start(out=small[0:1, 2:3], in_=rtemp_in), 16)
            ld.inc(sync.dma_start(out=b2v(5632, [[248, 8], [1, 248]]),
                                  in_=w2cat_in.rearrange("(c p) f -> p c f", p=128)), 16)
        LD0 = ld.count

        @block.vector
        def _(vector):
            vector = EW(vector)
            vector.wait_ge(ld.sem, LD0)
            vector.reduce_max(small[0:1, 4:5], small[0:1, 0:2], AX.X)
            vector.drain()
            vector.tensor_scalar(small[0:1, 6:8], small[0:1, 0:2], small[0:1, 4:5],
                                 None, ALU.subtract)
            d_a.inc(vector.drain())

        @block.scalar
        def _(scalar):
            scalar = EW(scalar)
            d_a.wait(scalar)
            scalar.activation(small[0:1, 8:10], small[0:1, 6:8], AF.Exp,
                              accum_out=small[0:1, 10:11])
            a_d.inc(scalar.drain())

        @block.vector
        def _(vector):
            vector = EW(vector)
            a_d.wait(vector)
            vector.reciprocal(small[0:1, 11:12], small[0:1, 10:11])
            vector.drain()
            vector.tensor_scalar(small[0:1, 12:14], small[0:1, 8:10],
                                 small[0:1, 11:12], None, ALU.mult)
            vector.tensor_scalar(small[0:1, 14:15], small[0:1, 2:3], 0.1,
                                 None, ALU.max)
            vector.drain()
            vector.reciprocal(small[0:1, 15:16], small[0:1, 14:15])
            vector.drain()
            vector.tensor_scalar(small[0:1, 16:17], small[0:1, 15:16], 0.5,
                                 None, ALU.mult)
            vector.memset(ones_r, 1.0)
            vector.memset(cols[:, 62:63], EPS)
            vector.memset(cols[:, 63:64], float(1.0 / 0.3))
            d_pe.inc(vector.drain())

        @block.tensor
        def _(tensor):
            tensor = EW(tensor)
            d_pe.wait(tensor)
            tensor.matmul(pb[0][:, 0:5], ones_r, small[0:1, 12:17],
                          start=True, stop=True)
            pe_d.inc(tensor.last)

        @block.vector
        def _(vector):
            vector = EW(vector)
            pe_d.wait(vector)
            vector.tensor_copy(cols[:, 56:61], pb[0][:, 0:5])
            d_pe.inc(vector.drain())

        @block.vector
        def _(vector):
            vector = EW(vector)
            vector.memset(attn_sb[:, :], 0.0)
            d_sp.inc(vector.drain())

        @block.gpsimd
        def _(gpsimd):
            gpsimd = EW(gpsimd)
            d_sp.wait(gpsimd)
            zsrc = bass.AP(attn_sb, 0, [[2048, 128], [1, 2048]])
            for k in range(12):
                gdma.inc(gpsimd.dma_start(
                    out=bass.AP(m_scr_t, k * 128 * 2048, [[2048, 128], [1, 2048]]),
                    in_=zsrc), 16)
            for k in range(8):
                gdma.inc(gpsimd.dma_start(
                    out=bass.AP(moe_acc_t, k * 128 * 2048, [[2048, 128], [1, 2048]]),
                    in_=zsrc), 16)
            gdma.inc(gpsimd.dma_start(
                out=bass.AP(inv2_t, 0,
                                [[16384, 2], [2048, 4], [1, 2048]]),
                in_=bass.AP(attn_sb, 0, [[2048, 8], [1, 2048]])), 16)
        MZERO = gdma.count

        def emit_ln_stats(idxs, src_fn, junk_ap, first_wait=None,
                          keep_s1=False):
            n = len(idxs)

            @block.scalar
            def _(scalar):
                scalar = EW(scalar)
                if first_wait is not None:
                    scalar.wait_ge(*first_wait)
                pe_a.wait(scalar)
                for k, i in enumerate(idxs):
                    if k:
                        scalar.drain()
                    scalar.activation(junk_ap, src_fn(i), AF.Square,
                                      accum_out=cols[:, k:k + 1])
                a_d.inc(scalar.drain())

            @block.vector
            def _(vector):
                vector = EW(vector)
                a_d.wait(vector)
                for k, i in enumerate(idxs):
                    vector.reduce_sum(cols[:, 16 + k:17 + k], src_fn(i), AX.X)
                vector.drain()
                vector.tensor_scalar(cols[:, 16:16 + n], cols[:, 16:16 + n],
                                     1.0 / D, None, ALU.mult)
                vector.drain()
                vector.tensor_tensor(small[:, 240:240 + n], cols[:, 16:16 + n],
                                     cols[:, 16:16 + n], ALU.mult)
                vector.drain()
                vector.tensor_scalar(cols[:, 0:n], cols[:, 0:n], 1.0 / D,
                                     None, ALU.mult)
                vector.drain()
                vector.tensor_tensor(cols[:, 0:n], cols[:, 0:n],
                                     small[:, 240:240 + n], ALU.subtract)
                if keep_s1:
                    vector.drain()
                    vector.tensor_copy(cols[:, 48:48 + n], cols[:, 0:n])
                d_a.inc(vector.drain())

            @block.scalar
            def _(scalar):
                scalar = EW(scalar)
                d_a.wait(scalar)
                if keep_s1:
                    scalar.activation(cols[:, 32:32 + n], cols[:, 0:n],
                                      AF.Sqrt, bias=cols[:, 62:63])
                else:
                    scalar.activation(cols[:, 0:n], cols[:, 0:n],
                                      AF.Sqrt, bias=cols[:, 62:63])
                a_d.inc(scalar.drain())

            @block.vector
            def _(vector):
                vector = EW(vector)
                a_d.wait(vector)
                if keep_s1:
                    vector.reciprocal(cols[:, 0:n], cols[:, 32:32 + n])
                else:
                    vector.reciprocal(cols[:, 0:n], cols[:, 0:n])
                d_a.inc(vector.drain())

        def emit_transpose_to(src_slot_ap8, dst_base_off, dst_tensor, dst_F, i):
            @block.tensor
            def _(tensor):
                tensor = EW(tensor)
                d_pe.wait(tensor)
                for half in range(2):
                    for c in range(4):
                        tensor.transpose(pb[6 + half][:, ts(c)],
                                         src_slot_ap8[half * 4 + c],
                                         ident[:, :])
                    pe_a.inc(tensor.last)

            @block.vector
            def _(vector):
                vector = EW(vector)
                pe_a.wait(vector)
                for half in range(2):
                    vector.tensor_copy(
                        bass.AP(dst_tensor, dst_base_off + (4 * half) * T + i * 128,
                                [[dst_F, 128], [T, 4], [1, 128]]),
                        pb[6 + half][:, 0:512].rearrange("p (a d) -> p a d", d=128))
                d_pe.inc(vector.drain())

        if stage >= 1:
            emit_ln_stats(list(range(NT)), lambda i: x_sb[:, i, :],
                          stg[:, 0, :], first_wait=(ld.sem, LD0))
            for i in range(NT):
                @block.vector
                def _(vector, i=i):
                    vector = EW(vector)
                    pe_a.wait(vector)
                    st.wait(vector)
                    vector.tensor_scalar(stg[:, 0, :], x_sb[:, i, :],
                                         cols[:, 16 + i:17 + i],
                                         cols[:, i:i + 1],
                                         ALU.subtract, ALU.mult)
                    d_pe.inc(vector.drain())
                    d_sp.inc_eng(vector)

                @block.sync
                def _(sync, i=i):
                    d_sp.wait(sync)
                    st.inc(sync.dma_start(
                        out=bass.AP(xn_scr_t, i * 128 * D, [[D, 128], [1, D]]),
                        in_=stg[:, 0, :]), 16)

                emit_transpose_to([stg[:, 0, ts(c)] for c in range(ND)],
                                  0, fmA, FMF, i)

            if debug:
                @block.sync
                def _(sync):
                    d_sp.wait(sync)
                    d_pe.wait(sync)
                    st.inc(sync.dma_start(out=taps["t_xnT"].ap(),
                                          in_=bass.AP(fmA, 0, [[FMF, 128], [1, FMF]])), 16)

        buf_free = [None, None]

        def _ap_half(a, h):
            dims = [list(d) for d in a.ap]
            s0, n0 = dims[0]
            return bass.AP(a.tensor, a.offset + h * (n0 // 2) * s0,
                           [[s0, n0 // 2]] + dims[1:])

        def load_weight(sync, bi, dst_ap, src_ap):
            tok = buf_free[bi]
            if tok is not None:
                sync.wait_ge(tok[0].sem, tok[1])
            for h in range(2):
                chW[2 * bi + h].inc(
                    sync.dma_start(out=_ap_half(dst_ap, h),
                                   in_=_ap_half(src_ap, h)), 16)
            return ((chW[2 * bi].sem, chW[2 * bi].count),
                    (chW[2 * bi + 1].sem, chW[2 * bi + 1].count))

        if stage >= 2:
            rq_tok = [None]
            atn_tok = {}
            vsl_tok = {}
            for g in range(NG):
                gs = ts(g, 512)
                for qb in range(4):
                    bi = qb % 2
                    buf = [wA, wB][bi]

                    @block.sync
                    def _(sync, qb=qb, bi=bi, buf=buf):
                        load_weight(sync, bi,
                                    bass.AP(buf, 0, [[2048, 128], [256, 8], [1, 256]]),
                                    wqv[:, qb * 256:(qb + 1) * 256]
                                    .rearrange("(c p) f -> p c f", p=128))
                    lq = ((chW[2 * bi].sem, chW[2 * bi].count),
                          (chW[2 * bi + 1].sem, chW[2 * bi + 1].count))

                    @block.tensor
                    def _(tensor, qb=qb, bi=bi, buf=buf, gs=gs, lq=lq, g=g):
                        tensor = EW(tensor)
                        [tensor.wait_ge(s_, c_) for s_, c_ in lq]
                        a_pe.wait(tensor)
                        for fc in range(2):
                            for dc in range(ND):
                                tensor.matmul(pb[fc][:, 0:512],
                                              bass.AP(buf, dc * 256 + fc * 128,
                                                      [[2048, 128], [1, 128]]),
                                              fmA[:, dc, gs],
                                              start=(dc == 0), stop=(dc == ND - 1))
                            pe_a.inc(tensor.last)
                        buf_free[bi] = (pe_a, pe_a.count)

                    @block.scalar
                    def _(scalar, qb=qb, g=g):
                        scalar = EW(scalar)
                        if qb == 0 and rq_tok[0] is not None:
                            scalar.wait_ge(rq_tok[0][0].sem, rq_tok[0][1])
                        if qb == 0 and g > 0:
                            st_v[0].wait(scalar)
                            st_v[1].wait(scalar)
                        for fc in range(2):
                            pe_a.wait(scalar)
                            scalar.activation(
                                bass.AP(big3, (qb * 2 + fc) * 512,
                                        [[B3F, 128], [1, 512]]),
                                pb[fc][:, 0:512], AF.Relu)
                            a_pe.inc(scalar.drain())

                for tt in range(4):
                    i = g * 4 + tt

                    @block.tensor
                    def _(tensor, tt=tt):
                        tensor = EW(tensor)
                        a_pe.wait(tensor)
                        d_pe.wait(tensor)
                        for qc in range(ND):
                            tensor.matmul(pb[4][:, 0:248],
                                          bass.AP(big3, qc * 512 + tt * 128,
                                                  [[B3F, 128], [1, 128]]),
                                          b2v(5632, [[248, 8], [1, 248]])[:, qc, :],
                                          start=(qc == 0), stop=(qc == ND - 1))
                        pe_d.inc(tensor.last)

                    @block.vector
                    def _(vector, i=i):
                        vector = EW(vector)
                        pe_d.wait(vector)
                        if i == 0:
                            vector.wait_ge(gdma.sem, MZERO)
                        vector.reduce_max(
                            cols[:, 28:32].rearrange("p (a b) -> p a b", b=1),
                            pb[4][:, 0:84].rearrange("p (a b) -> p a b", b=21),
                            AX.X, negate=True)
                        vector.reduce_max(
                            cols[:, 32:36].rearrange("p (a b) -> p a b", b=1),
                            pb[4][:, 84:248].rearrange("p (a b) -> p a b", b=41),
                            AX.X, negate=True)
                        vector.drain()
                        vector.tensor_tensor(
                            small[:, 256:340].rearrange("p (a b) -> p a b", b=21),
                            pb[4][:, 0:84].rearrange("p (a b) -> p a b", b=21),
                            bass.AP(cols, 28, [[64, 128], [1, 4], [0, 21]]),
                            ALU.add)
                        vector.tensor_tensor(
                            small[:, 340:504].rearrange("p (a b) -> p a b", b=41),
                            pb[4][:, 84:248].rearrange("p (a b) -> p a b", b=41),
                            bass.AP(cols, 32, [[64, 128], [1, 4], [0, 41]]),
                            ALU.add)
                        d_a.inc(vector.drain())
                        d_pe.inc_eng(vector)

                    @block.scalar
                    def _(scalar):
                        scalar = EW(scalar)
                        d_a.wait(scalar)
                        scalar.activation(small[:, 256:504], small[:, 256:504],
                                          AF.Exp)
                        a_d.inc(scalar.drain())

                    @block.vector
                    def _(vector, i=i):
                        vector = EW(vector)
                        a_d.wait(vector)
                        vector.reduce_sum(
                            cols[:, 36:40].rearrange("p (a b) -> p a b", b=1),
                            small[:, 256:340].rearrange("p (a b) -> p a b", b=21),
                            AX.X)
                        vector.reduce_sum(
                            cols[:, 40:44].rearrange("p (a b) -> p a b", b=1),
                            small[:, 340:504].rearrange("p (a b) -> p a b", b=41),
                            AX.X)
                        vector.drain()
                        vector.reciprocal(cols[:, 44:52], cols[:, 36:44])
                        vector.drain()
                        islot = i % 2
                        tok = atn_tok.get(islot)
                        if tok is not None:
                            vector.wait_ge(tok[0].sem, tok[1])
                        vector.scalar_tensor_tensor(
                            attn_sb[:, islot * 248:islot * 248 + 84]
                            .rearrange("p (a b) -> p a b", b=21),
                            small[:, 256:340].rearrange("p (a b) -> p a b", b=21),
                            cols[:, 56:57],
                            bass.AP(cols, 44, [[64, 128], [1, 4], [0, 21]]),
                            ALU.mult, ALU.mult)
                        vector.scalar_tensor_tensor(
                            attn_sb[:, islot * 248 + 84:islot * 248 + 248]
                            .rearrange("p (a b) -> p a b", b=41),
                            small[:, 340:504].rearrange("p (a b) -> p a b", b=41),
                            cols[:, 57:58],
                            bass.AP(cols, 48, [[64, 128], [1, 4], [0, 41]]),
                            ALU.mult, ALU.mult)
                        d_sp.inc(vector.drain())

                    @block.sync
                    def _(sync, i=i):
                        islot = i % 2
                        d_sp.wait(sync)
                        st_a[islot].inc(sync.dma_start(
                            out=bass.AP(atn_scr_t, i * 128 * FA,
                                        [[FA, 128], [1, FA]]),
                            in_=attn_sb[:, islot * 248:(islot + 1) * 248]), 16)
                        atn_tok[islot] = (st_a[islot], st_a[islot].count)

                rq_tok[0] = (pe_d, pe_d.count)

                for vb in range(4):
                    bi = vb % 2
                    buf = [wA, wB][bi]

                    @block.sync
                    def _(sync, vb=vb, bi=bi, buf=buf):
                        load_weight(sync, bi,
                                    bass.AP(buf, 0, [[2048, 128], [256, 8], [1, 256]]),
                                    wqv[:, 1024 + vb * 256:1024 + (vb + 1) * 256]
                                    .rearrange("(c p) f -> p c f", p=128))
                    lv = ((chW[2 * bi].sem, chW[2 * bi].count),
                          (chW[2 * bi + 1].sem, chW[2 * bi + 1].count))

                    @block.tensor
                    def _(tensor, vb=vb, bi=bi, buf=buf, g=g, lv=lv):
                        tensor = EW(tensor)
                        [tensor.wait_ge(s_, c_) for s_, c_ in lv]
                        if vb == 0:
                            d_a.wait(tensor)
                            a_d.wait(tensor)
                        for tsub in range(4):
                            bank = 2 + tsub
                            d_pe.wait(tensor)
                            for dc in range(ND):
                                tensor.matmul(pb[bank][:, 0:256],
                                              fmA[:, dc, ts(g * 4 + tsub)],
                                              bass.AP(buf, dc * 256,
                                                      [[2048, 128], [1, 256]]),
                                              start=(dc == 0), stop=(dc == ND - 1))
                            pe_d.inc(tensor.last)
                        buf_free[bi] = (pe_d, pe_d.count)

                    for tsub in range(4):
                        vslot = (vb * 4 + tsub) % 2

                        @block.vector
                        def _(vector, vb=vb, tsub=tsub, vslot=vslot):
                            vector = EW(vector)
                            bank = 2 + tsub
                            pe_d.wait(vector, pe_d.count - 3 + tsub)
                            tok = vsl_tok.get(vslot)
                            if tok is not None:
                                vector.wait_ge(tok[0].sem, tok[1])
                            vector.tensor_copy(
                                bass.AP(big3, vslot * 256, [[B3F, 128], [1, 256]]),
                                pb[bank][:, 0:256])
                            d_pe.inc(vector.drain())
                            d_sp.inc_eng(vector)

                        @block.sync
                        def _(sync, vb=vb, tsub=tsub, vslot=vslot, g=g):
                            d_sp.wait(sync)
                            st_v[vslot].inc(sync.dma_start(
                                out=bass.AP(v0_scr_t,
                                            (g * 4 + tsub) * 128 * D + vb * 256,
                                            [[D, 128], [1, 256]]),
                                in_=bass.AP(big3, vslot * 256,
                                            [[B3F, 128], [1, 256]])), 16)
                            vsl_tok[vslot] = (st_v[vslot], st_v[vslot].count)

            if debug:
                @block.sync
                def _(sync):
                    st_a[0].wait(sync)
                    st_a[1].wait(sync)
                    st.inc(sync.dma_start(
                        out=bass.AP(taps["t_attn"], 0,
                                    [[NT * FA, 128], [FA, NT], [1, FA]]),
                        in_=bass.AP(atn_scr_t, 0,
                                    [[FA, 128], [128 * FA, NT], [1, FA]])), 16)

        if stage >= 3:
            @block.gpsimd
            def _(gpsimd):
                gpsimd = EW(gpsimd)
                st_a[0].wait(gpsimd)
                st_a[1].wait(gpsimd)
                gpsimd.wait_ge(gdma.sem, MZERO)
                for ci, (c, half, coff) in enumerate([(21, 10, 0), (41, 20, 84)]):
                    if ci == 1:
                        gpsimd.wait_ge(gdma.sem, gdma.count)
                    for h in range(4):
                        dst = bass.AP(m_scr_t, h * NT * 128 * FM + (128 - half),
                                      [[FM + 1, 128], [128 * FM, NT], [1, c]])
                        src = bass.AP(atn_scr_t, coff + h * c,
                                      [[FA, 128], [128 * FA, NT], [1, c]])
                        gdma.inc(gpsimd.dma_start(
                            out=dst, in_=src,
                            accum_op=(ALU.add if ci == 1 else ALU.bypass)), 16)
            SKEW_DONE = gdma.count

            vring_tok = {}
            vr_seen = {}

            def load_vring(gpsimd, tile):
                slot = tile % 4
                tok = vring_tok.get(slot)
                if tok is not None:
                    gpsimd.wait_ge(tok[0].sem, tok[1])
                vr[slot].inc(gpsimd.dma_start(
                    out=bass.AP(big2, slot * 1024, [[B2F, 128], [1, 1024]]),
                    in_=bass.AP(v0_scr_t, tile * 128 * D, [[D, 128], [1, D]])), 16)
                vr_seen[tile] = (vr[slot], vr[slot].count)

            @block.gpsimd
            def _(gpsimd):
                gpsimd = EW(gpsimd)
                st_v[0].wait(gpsimd)
                st_v[1].wait(gpsimd)
                load_vring(gpsimd, 0)
                load_vring(gpsimd, 1)
            mrd_tok = {}
            mtc_state = {}

            for g in range(NG):
                for tt in range(4):
                    tile = g * 4 + tt
                    @block.gpsimd
                    def _(gpsimd, tile=tile):
                        gpsimd = EW(gpsimd)
                        if tile + 2 < NT:
                            load_vring(gpsimd, tile + 2)

                    for h in range(4):
                        u = (tile * 4 + h)
                        mslot = u % 2

                        @block.gpsimd
                        def _(gpsimd, tile=tile, h=h, mslot=mslot):
                            gpsimd = EW(gpsimd)
                            if tile == 0 and h == 0:
                                gpsimd.wait_ge(gdma.sem, SKEW_DONE)
                            tok = mrd_tok.get(mslot)
                            if tok is not None:
                                gpsimd.wait_ge(tok[0].sem, tok[1])
                            mr[mslot].inc(gpsimd.dma_start(
                                out=bass.AP(big2, 4096 + mslot * 384,
                                            [[B2F, 128], [1, FM]]),
                                in_=bass.AP(m_scr_t,
                                            (h * NT + tile) * 128 * FM,
                                            [[FM, 128], [1, FM]])), 16)
                        RB = (mr[mslot].sem, mr[mslot].count)

                        @block.tensor
                        def _(tensor, tile=tile, h=h, mslot=mslot, RB=RB):
                            tensor = EW(tensor)
                            tensor.wait_ge(*RB)
                            d_pe.wait(tensor)
                            for k in range(3):
                                tensor.transpose(pb[6][:, ts(k)],
                                                 bass.AP(big2,
                                                         4096 + mslot * 384 + k * 128,
                                                         [[B2F, 128], [1, 128]]),
                                                 ident[:, :])
                            pe_d.inc(tensor.last)
                        mrd_tok[mslot] = (pe_d, pe_d.count)

                        @block.vector
                        def _(vector, mslot=mslot):
                            vector = EW(vector)
                            pe_d.wait(vector)
                            vector.tensor_copy(
                                bass.AP(big2, 4864 + mslot * 384,
                                        [[B2F, 128], [1, FM]]),
                                pb[6][:, 0:FM])
                            d_pe.inc(vector.drain())
                        MTC = d_pe.count

                        @block.tensor
                        def _(tensor, tile=tile, h=h, mslot=mslot, MTC=MTC):
                            tensor = EW(tensor)
                            tensor.wait_ge(d_pe.sem, MTC)
                            if h == 0:
                                for dt_ in (tile - 1, tile, tile + 1):
                                    if 0 <= dt_ < NT and dt_ in vr_seen:
                                        tensor.wait_ge(vr_seen[dt_][0].sem,
                                                       vr_seen[dt_][1])
                            ks = [k for k in range(3)
                                  if 0 <= tile - 1 + k < NT]
                            for d2 in range(2):
                                first = True
                                for k in ks:
                                    vt = (tile - 1 + k) % 4
                                    tensor.matmul(
                                        pb[7][:, d2 * 128:(d2 + 1) * 128],
                                        bass.AP(big2,
                                                vt * 1024 + h * 256 + d2 * 128,
                                                [[B2F, 128], [1, 128]]),
                                        bass.AP(big2, 4864 + mslot * 384 + k * 128,
                                                [[B2F, 128], [1, 128]]),
                                        start=first, stop=(k == ks[-1]))
                                    first = False
                            pe_d.inc(tensor.last)

                        @block.vector
                        def _(vector, tt=tt, h=h):
                            vector = EW(vector)
                            pe_d.wait(vector)
                            vector.tensor_copy(
                                bass.AP(big3, (2 * h) * 512 + tt * 128,
                                        [[B3F, 128], [512, 2], [1, 128]]),
                                pb[7][:, 0:256].rearrange("p (a d) -> p a d", d=128))
                            d_pe.inc(vector.drain())
                    vring_tok[(tile - 1) % 4] = (pe_d, pe_d.count)

                MIX = d_pe.count

                for ob in range(4):
                    bi = ob % 2
                    buf = [wA, wB][bi]

                    @block.sync
                    def _(sync, ob=ob, bi=bi, buf=buf):
                        load_weight(sync, bi,
                                    bass.AP(buf, 0, [[2048, 128], [256, 8], [1, 256]]),
                                    wout_in[:, ob * 256:(ob + 1) * 256]
                                    .rearrange("(c p) f -> p c f", p=128))
                    lo = ((chW[2 * bi].sem, chW[2 * bi].count),
                          (chW[2 * bi + 1].sem, chW[2 * bi + 1].count))

                    @block.tensor
                    def _(tensor, ob=ob, bi=bi, buf=buf, g=g, lo=lo, MIX=MIX):
                        tensor = EW(tensor)
                        [tensor.wait_ge(s_, c_) for s_, c_ in lo]
                        tensor.wait_ge(d_pe.sem, MIX)
                        for tsub in range(4):
                            bank = tsub
                            d_pe.wait(tensor)
                            for dc in range(ND):
                                tensor.matmul(pb[bank][:, 0:256],
                                              bass.AP(big3, dc * 512 + tsub * 128,
                                                      [[B3F, 128], [1, 128]]),
                                              bass.AP(buf, dc * 256,
                                                      [[2048, 128], [1, 256]]),
                                              start=(dc == 0), stop=(dc == ND - 1))
                            pe_d.inc(tensor.last)
                        buf_free[bi] = (pe_d, pe_d.count)

                    @block.vector
                    def _(vector, ob=ob, g=g):
                        vector = EW(vector)
                        base = pe_d.count - 4
                        for tsub in range(4):
                            bank = tsub
                            pe_d.wait(vector, base + tsub + 1)
                            i = g * 4 + tsub
                            vector.tensor_add(x_sb[:, i, ob * 256:(ob + 1) * 256],
                                              x_sb[:, i, ob * 256:(ob + 1) * 256],
                                              pb[bank][:, 0:256])
                            d_pe.inc(vector.drain())

            if debug:
                @block.sync
                def _(sync):
                    d_pe.wait(sync)
                    st.inc(sync.dma_start(out=taps["t_x1"].ap(),
                                          in_=bass.AP(x_sb, 0,
                                                      [[NT * D, 128], [1, NT * D]])), 16)

        if stage >= 4:
            @block.sync
            def _(sync):
                pe_d.wait(sync)
                d_pe.wait(sync)
                st.wait(sync)
                ld.inc(sync.dma_start(
                    out=bass.AP(big2, 0, [[B2F, 128], [8, 8], [1, 8]]),
                    in_=wg_in.rearrange("(c p) e -> p c e", p=128)), 16)
                ld.inc(sync.dma_start(
                    out=bass.AP(big2, 512, [[B2F, 128], [128, 2], [1, 128]]),
                    in_=tri2_in.rearrange("a p d -> p a d")), 16)
            LWSM = ld.count

            for i in range(NT):
                @block.tensor
                def _(tensor, i=i):
                    tensor = EW(tensor)
                    d_pe.wait(tensor)
                    if i == 0:
                        tensor.wait_ge(ld.sem, LWSM)
                    for dc in range(ND):
                        tensor.matmul(pb[5][:, 0:8], fmA[:, dc, ts(i)],
                                      bass.AP(big2, dc * 8, [[B2F, 128], [1, 8]]),
                                      start=(dc == 0), stop=(dc == ND - 1))
                    pe_d.inc(tensor.last)

                @block.vector
                def _(vector):
                    vector = EW(vector)
                    pe_d.wait(vector)
                    st_v[0].wait(vector)
                    st_v[1].wait(vector)
                    vector.reduce_max(cols[:, 28:29], pb[5][:, 0:8], AX.X,
                                      negate=True)
                    vector.drain()
                    vector.tensor_scalar(small[:, 256:264], pb[5][:, 0:8],
                                         cols[:, 28:29], None, ALU.add)
                    d_a.inc(vector.drain())
                    d_pe.inc_eng(vector)

                @block.scalar
                def _(scalar):
                    scalar = EW(scalar)
                    d_a.wait(scalar)
                    scalar.activation(small[:, 264:272], small[:, 256:264],
                                      AF.Exp, accum_out=cols[:, 29:30])
                    a_d.inc(scalar.drain())

                @block.vector
                def _(vector, i=i):
                    vector = EW(vector)
                    a_d.wait(vector)
                    vector.reciprocal(cols[:, 30:31], cols[:, 29:30])
                    vector.drain()
                    vector.tensor_scalar(small[:, 272:280], small[:, 264:272],
                                         cols[:, 30:31], None, ALU.mult)
                    vector.drain()
                    vector.reduce_max(cols[:, 31:32], small[:, 272:280], AX.X)
                    vector.drain()
                    vector.tensor_scalar(small[:, 280:288], small[:, 272:280],
                                         cols[:, 31:32], None, ALU.is_ge)
                    vector.drain()
                    vector.scalar_tensor_tensor(small[:, 288:296], small[:, 280:288],
                                                -1000.0, iotas[:, 0:8],
                                                ALU.mult, ALU.add)
                    vector.drain()
                    vector.tensor_reduce(cols[:, 32:33], small[:, 288:296], AX.X,
                                         ALU.min)
                    vector.drain()
                    vector.tensor_scalar(cols[:, 33:34], cols[:, 32:33], 1000.0,
                                         None, ALU.add)
                    vector.drain()
                    vector.tensor_scalar(small[:, 280:288], iotas[:, 0:8],
                                         cols[:, 33:34], None, ALU.is_equal)
                    vector.drain()
                    vector.scalar_tensor_tensor(small[:, 296:304], small[:, 280:288],
                                                -2.0, small[:, 272:280],
                                                ALU.mult, ALU.add)
                    vector.drain()
                    vector.reduce_max(cols[:, 34:35], small[:, 296:304], AX.X)
                    vector.drain()
                    vector.tensor_scalar(small[:, 304:312], small[:, 296:304],
                                         cols[:, 34:35], None, ALU.is_ge)
                    vector.drain()
                    vector.scalar_tensor_tensor(small[:, 312:320], small[:, 304:312],
                                                -1000.0, iotas[:, 0:8],
                                                ALU.mult, ALU.add)
                    vector.drain()
                    vector.tensor_reduce(cols[:, 35:36], small[:, 312:320], AX.X,
                                         ALU.min)
                    vector.drain()
                    vector.tensor_scalar(cols[:, 36:37], cols[:, 35:36], 1000.0,
                                         None, ALU.add)
                    vector.drain()
                    vector.tensor_scalar(small[:, 304:312], iotas[:, 0:8],
                                         cols[:, 36:37], None, ALU.is_equal)
                    vector.drain()
                    vector.tensor_add(
                        bass.AP(big2, 1024 + i * 8, [[B2F, 128], [1, 8]]),
                        small[:, 280:288], small[:, 304:312])
                    vector.tensor_copy(small[:, 320 + i * 4:321 + i * 4],
                                       cols[:, 32:33])
                    vector.tensor_copy(small[:, 321 + i * 4:322 + i * 4],
                                       cols[:, 35:36])
                    vector.tensor_copy(small[:, 322 + i * 4:323 + i * 4],
                                       cols[:, 31:32])
                    vector.tensor_copy(small[:, 323 + i * 4:324 + i * 4],
                                       cols[:, 34:35])
                    vector.drain()
                    vector.tensor_scalar(small[:, 288:296], small[:, 280:288],
                                         cols[:, 31:32], None, ALU.mult)
                    vector.drain()
                    vector.tensor_scalar(small[:, 304:312], small[:, 304:312],
                                         cols[:, 34:35], None, ALU.mult)
                    vector.drain()
                    vector.tensor_add(small[:, i * 8:(i + 1) * 8],
                                      small[:, 288:296], small[:, 304:312])
                    d_pe.inc(vector.drain())

                @block.tensor
                def _(tensor, i=i):
                    tensor = EW(tensor)
                    d_pe.wait(tensor)
                    tensor.transpose(pb[5][0:8, 0:128], small[:, i * 8:(i + 1) * 8],
                                     ident[:, :])
                    pe_d.inc(tensor.last)

                @block.vector
                def _(vector):
                    vector = EW(vector)
                    pe_d.wait(vector)
                    st.wait(vector)
                    vector.tensor_copy(small[0:8, 384:512], pb[5][0:8, 0:128])
                    d_sp.inc(vector.drain())
                    d_pe.inc_eng(vector)

                @block.sync
                def _(sync, i=i):
                    d_sp.wait(sync)
                    st.inc(sync.dma_start(
                        out=bass.AP(gwt_scr_t, i * 128, [[NT * 128, 8], [1, 128]]),
                        in_=small[0:8, 384:512]), 16)

            if debug:
                @block.sync
                def _(sync):
                    d_pe.wait(sync)
                    st.inc(sync.dma_start(out=taps["t_gw"].ap(),
                                          in_=small[:, 0:128]), 16)

        if stage >= 5:
            bank_tok = {b: (d_pe, 0) for b in range(8)}
            h1t_tok = (pe_d, pe_d.count)
            w2slot_tok = {}
            b1_tok = [None]

            GTOK = {0: None, 1: None}
            for i in range(NT):
                par = i % 2

                @block.tensor
                def _(tensor, i=i):
                    tensor = EW(tensor)
                    d_pe.wait(tensor)
                    for j in range(i):
                        tensor.matmul(pb[4][:, 0:8],
                                      bass.AP(big2, 512, [[B2F, 128], [1, 128]]),
                                      bass.AP(big2, 1024 + j * 8,
                                              [[B2F, 128], [1, 8]]),
                                      start=(j == 0), stop=False)
                    tensor.matmul(pb[4][:, 0:8],
                                  bass.AP(big2, 640, [[B2F, 128], [1, 128]]),
                                  bass.AP(big2, 1024 + i * 8,
                                          [[B2F, 128], [1, 8]]),
                                  start=(i == 0), stop=True)
                    pe_d.inc(tensor.last)

                @block.vector
                def _(vector, i=i, par=par):
                    vector = EW(vector)
                    pe_d.wait(vector)
                    tok = GTOK[par]
                    if tok is not None:
                        vector.wait_ge(tok[0].sem, tok[1])
                    vector.tensor_copy(bass.AP(big2, 1152 + i * 8,
                                               [[B2F, 128], [1, 8]]),
                                       pb[4][:, 0:8])
                    vector.drain()
                    d_pe.inc_eng(vector)
                    vector.tensor_scalar(cols[:, 28 + par:29 + par],
                                         iotas[:, 14:15],
                                         float(i * 128), None, ALU.add)
                    vector.drain()
                    for k in range(2):
                        idxc = small[:, 320 + i * 4 + k:321 + i * 4 + k]
                        vector.tensor_scalar(cols[:, 40:41], idxc, 1000.0,
                                             None, ALU.add)
                        vector.drain()
                        vector.tensor_scalar(small[:, 288:296], iotas[:, 0:8],
                                             cols[:, 40:41], None, ALU.is_equal)
                        vector.drain()
                        vector.tensor_tensor(small[:, 296:304],
                                             small[:, 288:296],
                                             bass.AP(big2, 1152 + i * 8,
                                                     [[B2F, 128], [1, 8]]),
                                             ALU.mult)
                        vector.drain()
                        vector.reduce_sum(cols[:, 41:42], small[:, 296:304],
                                          AX.X)
                        vector.drain()
                        vector.tensor_scalar(cols[:, 42:43], cols[:, 41:42],
                                             float(C0), None, ALU.is_ge)
                        vector.tensor_scalar(cols[:, 43:44], idxc, float(C0),
                                             cols[:, 41:42], ALU.mult, ALU.add)
                        vector.drain()
                        vector.tensor_scalar(cols[:, 47:48],
                                             cols[:, 28 + par:29 + par],
                                             2.0, float(TRASH0 + k),
                                             ALU.mult, ALU.add)
                        vector.drain()
                        vector.tensor_scalar(cols[:, 44:45], cols[:, 43:44],
                                             -1.0, cols[:, 47:48], ALU.mult,
                                             ALU.add)
                        vector.drain()
                        vector.scalar_tensor_tensor(cols[:, 45:46],
                                                    cols[:, 44:45],
                                                    cols[:, 42:43],
                                                    cols[:, 43:44],
                                                    ALU.mult, ALU.add)
                        vector.drain()
                        vector.tensor_copy(i32g[:, 4 * par + k:4 * par + k + 1],
                                           cols[:, 45:46])
                        vector.drain()
                    d_sp.inc_eng(vector)
                DSPR = d_sp.count

                @block.gpsimd
                def _(gpsimd, i=i, par=par, DSPR=DSPR):
                    gpsimd = EW(gpsimd)
                    gpsimd.wait_ge(d_sp.sem, DSPR)
                    if i == 0:
                        gpsimd.wait_ge(gdma.sem, MZERO)
                    for k in range(2):
                        sc[par].inc(gpsimd.indirect_dma_start(
                            out=bass.AP(inv2_t, 0, [[1, 32768], [1, 1]]),
                            out_offset=IndirectOffsetOnAxis(
                                i32g[:, 4 * par + k:4 * par + k + 1], 0),
                            in_=cols[:, 28 + par:29 + par], in_offset=None), 16)
                        sc[par].inc(gpsimd.indirect_dma_start(
                            out=bass.AP(inv2_t, 0, [[1, 32768], [1, 1]]),
                            out_offset=IndirectOffsetOnAxis(
                                i32g[:, 4 * par + k:4 * par + k + 1], 0),
                            in_=small[:, 322 + i * 4 + k:323 + i * 4 + k],
                            in_offset=None, element_offset=16384), 16)
                GTOK[par] = (sc[par], sc[par].count)
            SCAT_DONE = (sc[0].count, sc[1].count)
            h1t_tok = (d_sp, d_sp.count)

            @block.gpsimd
            def _(gpsimd):
                gpsimd = EW(gpsimd)
                gpsimd.wait_ge(sc[0].sem, SCAT_DONE[0])
                gpsimd.wait_ge(sc[1].sem, SCAT_DONE[1])
                mr[0].inc(gpsimd.dma_start(
                    out=bass.AP(attn_sb, 1024, [[2048, 48], [1, 128]]),
                    in_=bass.AP(inv2_t, 0, [[128, 48], [1, 128]])), 16)
                mr[0].inc(gpsimd.dma_start(
                    out=bass.AP(attn_sb, 48 * 2048 + 1024, [[2048, 48], [1, 128]]),
                    in_=bass.AP(inv2_t, 16384, [[128, 48], [1, 128]])), 16)
            MRP = mr[0].count

            @block.tensor
            def _(tensor, MRP=MRP):
                tensor = EW(tensor)
                tensor.wait_ge(mr[0].sem, MRP)
                d_pe.wait(tensor)
                tensor.transpose(pb[7][:, 0:96],
                                 bass.AP(attn_sb, 1024, [[2048, 96], [1, 128]]),
                                 ident[0:96, 0:96])
                pe_d.inc(tensor.last)
            PEP = pe_d.count

            @block.vector
            def _(vector, PEP=PEP):
                vector = EW(vector)
                vector.wait_ge(pe_d.sem, PEP)
                vector.tensor_copy(
                    bass.AP(attn_sb, 1152, [[2048, 128], [1, 96]]),
                    pb[7][:, 0:96])
                vector.drain()
                vector.tensor_copy(
                    i32a[:, 0:48],
                    bass.AP(attn_sb, 1152, [[2048, 128], [1, 48]]))
                d_pe.inc(vector.drain())
                d_sp.inc_eng(vector)
            IDP_RDY = d_sp.count

            ring_slots = [(stg, 0, 1024), (attn_sb, 0, 2048),
                          (fmA, 0, FMF), (fmA, 1024, FMF)]
            ring_tok = {0: None, 1: None, 2: None, 3: None}
            SPAN = C0 // 2
            NCSP = SPAN // 128
            pend_esc = [None]

            def emit_esc0(args):
                if args is None:
                    return
                cc0, W2DONE = args
                for cts in range(NCSP):
                    @block.vector
                    def _(vector, cts=cts, cc0=cc0, W2DONE=W2DONE):
                        vector = EW(vector)
                        vector.wait_ge(pe_g.sem, W2DONE)
                        vector.wait_ge(mr[1].sem, mr[1].count)
                        for db in range(2):
                            bank = cts * 2 + db
                            vector.tensor_scalar(
                                bass.AP(big3, 3072 + db * 512,
                                        [[B3F, 128], [1, 512]]),
                                pb[bank][:, 0:512],
                                bass.AP(attn_sb, 1200 + cc0 + cts,
                                        [[2048, 128], [1, 1]]),
                                None, ALU.mult)
                            vector.drain()
                            d_pe.inc_eng(vector)
                            bank_tok[bank] = (d_pe, d_pe.count)
                        d_sp.inc_eng(vector)
                    DSE = d_sp.count

                    @block.gpsimd
                    def _(gpsimd, cts=cts, cc0=cc0, DSE=DSE):
                        gpsimd = EW(gpsimd)
                        gpsimd.wait_ge(d_sp.sem, DSE)
                        gpsimd.wait_ge(mr[1].sem, mr[1].count)
                        mr[1].inc(gpsimd.indirect_dma_start(
                            out=bass.AP(moe_acc_t, 0, [[D, T], [1, D]]),
                            out_offset=IndirectOffsetOnAxis(
                                i32a[:, cc0 + cts:cc0 + cts + 1], 0),
                            in_=bass.AP(big3, 3072, [[B3F, 128], [1, 1024]]),
                            in_offset=None,
                            compute_op=ALU.add), 16)

            def emit_gather0(cc):
                @block.gpsimd
                def _(gpsimd, cc=cc):
                    gpsimd = EW(gpsimd)
                    if cc == 0:
                        gpsimd.wait_ge(d_sp.sem, IDP_RDY)
                    tok = ring_tok[cc % 4]
                    if tok is not None:
                        gpsimd.wait_ge(tok[0].sem, tok[1])
                    rt, roff, F = ring_slots[cc % 4]
                    vr[cc % 4].inc(gpsimd.indirect_dma_start(
                        out=bass.AP(rt, roff, [[F, 128], [1, 1024]]),
                        out_offset=None,
                        in_=bass.AP(xn_scr_t, 0, [[D, T], [1, D]]),
                        in_offset=IndirectOffsetOnAxis(
                            i32a[:, cc:cc + 1], 0)), 16)
                return vr[cc % 4].count
            gather_vrc = {c: emit_gather0(c) for c in range(NCSP)}

            for e in range(8):
                @block.sync
                def _(sync, e=e):
                    sync = EW(sync)
                    if b1_tok[0] is not None:
                        sync.wait_ge(b1_tok[0][0].sem, b1_tok[0][1])
                    chb1.inc(sync.dma_start(out=b1sb[:, :],
                                            in_=b1moe[e]), 16)
                LB1 = chb1.count

                for half in range(2):
                    for c in range(NCSP):
                        cc = e * 2 * NCSP + half * NCSP + c
                        VRC = gather_vrc.pop(cc)

                        @block.tensor
                        def _(tensor, c=c, cc=cc, half=half, VRC=VRC):
                            tensor = EW(tensor)
                            tensor.wait_ge(vr[cc % 4].sem, VRC)
                            d_pe.wait(tensor)
                            rt, roff, F = ring_slots[cc % 4]
                            for half2 in range(2):
                                for cch in range(4):
                                    tensor.transpose(
                                        pb[6 + half2][:, ts(cch)],
                                        bass.AP(rt, roff + (half2 * 4 + cch) * 128,
                                                [[F, 128], [1, 128]]),
                                        ident[:, :])
                                pe_a.inc(tensor.last)
                        ring_tok[cc % 4] = (pe_a, pe_a.count)

                        @block.vector
                        def _(vector, c=c):
                            vector = EW(vector)
                            pe_a.wait(vector)
                            for half2 in range(2):
                                vector.tensor_copy(
                                    bass.AP(big3, (4 * half2) * SPAN + c * 128,
                                            [[B3F, 128], [SPAN, 4], [1, 128]]),
                                    pb[6 + half2][:, 0:512].rearrange(
                                        "p (a d) -> p a d", d=128))
                            d_pe.inc(vector.drain())
                    XST_DONE = d_pe.count
                    nh0 = (e * 2 + half + 1) * NCSP
                    for cc2 in range(nh0, min(nh0 + NCSP, 16 * NCSP)):
                        gather_vrc[cc2] = emit_gather0(cc2)
                    emit_esc0(pend_esc[0])
                    pend_esc[0] = None

                    for blk in range(8):
                        bi = blk % 2
                        buf = [wA, wB][bi]

                        @block.sync
                        def _(sync, e=e, blk=blk, bi=bi, buf=buf):
                            load_weight(sync, bi,
                                        bass.AP(buf, 0,
                                                [[2048, 128], [256, 8], [1, 256]]),
                                        w1moe[e][:, blk * 256:(blk + 1) * 256]
                                        .rearrange("(c p) f -> p c f", p=128))
                        lw = ((chW[2 * bi].sem, chW[2 * bi].count),
                              (chW[2 * bi + 1].sem, chW[2 * bi + 1].count))

                        @block.tensor
                        def _(tensor, blk=blk, bi=bi, buf=buf, lw=lw,
                              XST_DONE=XST_DONE):
                            tensor = EW(tensor)
                            [tensor.wait_ge(s_, c_) for s_, c_ in lw]
                            tensor.wait_ge(d_pe.sem, XST_DONE)
                            for hh in range(2):
                                hc = blk * 2 + hh
                                bank = hc % 2
                                tok = bank_tok[bank]
                                tensor.wait_ge(tok[0].sem, tok[1])
                                for dc in range(ND):
                                    tensor.matmul(pb[bank][:, 0:SPAN],
                                                  bass.AP(buf, dc * 256 + hh * 128,
                                                          [[2048, 128], [1, 128]]),
                                                  bass.AP(big3, dc * SPAN,
                                                          [[B3F, 128], [1, SPAN]]),
                                                  start=(dc == 0),
                                                  stop=(dc == ND - 1))
                                pe_a.inc(tensor.last)
                            buf_free[bi] = (pe_a, pe_a.count)

                        @block.scalar
                        def _(scalar, blk=blk, e=e, LB1=LB1):
                            scalar = EW(scalar)
                            if blk == 0:
                                scalar.wait_ge(chb1.sem, LB1)
                                scalar.wait_ge(h1t_tok[0].sem, h1t_tok[1])
                            for hh in range(2):
                                hc = blk * 2 + hh
                                bank = hc % 2
                                pe_a.wait(scalar, pe_a.count - 1 + hh)
                                scalar.activation(
                                    b2v(hc * SPAN, [[1, SPAN]]),
                                    pb[bank][:, 0:SPAN], AF.Relu,
                                    bias=b1sb[:, hc:hc + 1])
                                a_pe.inc(scalar.drain())
                                bank_tok[bank] = (a_pe, a_pe.count)

                    A_PE_H1 = a_pe.count
                    for hc in range(16):
                        slot = hc % 4
                        sbuf_t = [wA, wA, wB, wB][slot]
                        soff = [0, 1024, 0, 1024][slot]

                        @block.sync
                        def _(sync, e=e, hc=hc, slot=slot, sbuf_t=sbuf_t,
                              soff=soff):
                            sync = EW(sync)
                            tok = w2slot_tok.get(slot)
                            if tok is not None:
                                sync.wait_ge(tok[0].sem, tok[1])
                            tok2 = buf_free[0 if sbuf_t is wA else 1]
                            if tok2 is not None:
                                sync.wait_ge(tok2[0].sem, tok2[1])
                            chW[slot].inc(sync.dma_start(
                                out=bass.AP(sbuf_t, soff, [[2048, 128], [1, 1024]]),
                                in_=w2moe[e][hc * 128:(hc + 1) * 128, :]), 16)
                        lw2 = ((chW[slot].sem, chW[slot].count),)

                        @block.tensor
                        def _(tensor, hc=hc, slot=slot, sbuf_t=sbuf_t, soff=soff,
                              lw2=lw2, A_PE_H1=A_PE_H1, e=e):
                            tensor = EW(tensor)
                            [tensor.wait_ge(s_, c_) for s_, c_ in lw2]
                            if hc == 0:
                                tensor.wait_ge(a_pe.sem, A_PE_H1)
                                for b in range(6):
                                    tok = bank_tok[b]
                                    tensor.wait_ge(tok[0].sem, tok[1])
                            for cts in range(NCSP):
                                for db in range(2):
                                    bank = cts * 2 + db
                                    tensor.matmul(
                                        pb[bank][:, 0:512],
                                        b2v(hc * SPAN + cts * 128, [[1, 128]]),
                                        bass.AP(sbuf_t, soff + db * 512,
                                                [[2048, 128], [1, 512]]),
                                        start=(hc == 0), stop=(hc == 15))
                            pe_g.inc(tensor.last)
                        w2slot_tok[slot] = (pe_g, pe_g.count)
                    h1t_tok = (pe_g, pe_g.count)
                    buf_free[0] = (pe_g, pe_g.count)
                    buf_free[1] = (pe_g, pe_g.count)
                    b1_tok[0] = (a_pe, a_pe.count)
                    W2DONE = pe_g.count

                    pend_esc[0] = (e * 2 * NCSP + half * NCSP, W2DONE)

            emit_esc0(pend_esc[0])

            MOE_SC = mr[1].count
            add_tok = {0: None, 1: None, 2: None, 3: None}
            for i in range(NT):
                slot = i % 4

                @block.gpsimd
                def _(gpsimd, i=i, slot=slot):
                    gpsimd = EW(gpsimd)
                    if i == 0:
                        gpsimd.wait_ge(mr[1].sem, MOE_SC)
                    tok = add_tok[slot]
                    if tok is not None:
                        gpsimd.wait_ge(tok[0].sem, tok[1])
                    vr[slot].inc(gpsimd.dma_start(
                        out=bass.AP(big3, slot * 1024, [[B3F, 128], [1, 1024]]),
                        in_=bass.AP(moe_acc_t, i * 128 * D, [[D, 128], [1, D]])),
                        16)
                VRA = vr[slot].count

                @block.vector
                def _(vector, i=i, slot=slot, VRA=VRA):
                    vector = EW(vector)
                    vector.wait_ge(vr[slot].sem, VRA)
                    vector.tensor_add(x_sb[:, i, :], x_sb[:, i, :],
                                      bass.AP(big3, slot * 1024,
                                              [[B3F, 128], [1, 1024]]))
                    d_pe.inc(vector.drain())
                add_tok[slot] = (d_pe, d_pe.count)

            @block.sync
            def _(sync):
                tok = buf_free[0]
                if tok is not None:
                    sync.wait_ge(tok[0].sem, tok[1])
                chW[0].inc(sync.dma_start(
                    out=bass.AP(wA, 0, [[2048, 8], [1, 1024]]),
                    in_=b2moe), 16)
                tok = buf_free[1]
                if tok is not None:
                    sync.wait_ge(tok[0].sem, tok[1])
                st.wait(sync)
                chW[1].inc(sync.dma_start(
                    out=bass.AP(wB, 0, [[2048, 8], [1, NT * 128]]),
                    in_=bass.AP(gwt_scr_t, 0, [[NT * 128, 8], [1, NT * 128]])),
                    16)
            LB2 = (chW[0].sem, chW[0].count)
            LGW = (chW[1].sem, chW[1].count)
            buf_free[0] = None
            buf_free[1] = None

            for i in range(NT):
                @block.tensor
                def _(tensor, i=i):
                    tensor = EW(tensor)
                    if i == 0:
                        tensor.wait_ge(*LB2)
                        tensor.wait_ge(*LGW)
                    for db in range(2):
                        bank = db
                        tok = bank_tok[bank]
                        tensor.wait_ge(tok[0].sem, tok[1])
                        tensor.matmul(pb[bank][:, 0:512],
                                      bass.AP(wB, i * 128, [[2048, 8], [1, 128]]),
                                      bass.AP(wA, db * 512, [[2048, 8], [1, 512]]),
                                      start=True, stop=True)
                        pe_d.inc(tensor.last)

                @block.vector
                def _(vector, i=i):
                    vector = EW(vector)
                    base = pe_d.count - 2
                    for db in range(2):
                        bank = db
                        pe_d.wait(vector, base + db + 1)
                        vector.tensor_add(x_sb[:, i, db * 512:(db + 1) * 512],
                                          x_sb[:, i, db * 512:(db + 1) * 512],
                                          pb[bank][:, 0:512])
                        d_pe.inc(vector.drain())
                        bank_tok[bank] = (d_pe, d_pe.count)
                    d_sp.inc_eng(vector)

            buf_free[0] = (pe_d, pe_d.count)
            buf_free[1] = (pe_d, pe_d.count)

            @block.vector
            def _(vector):
                vector = EW(vector)
                vector.wait_ge(pe_g.sem, pe_g.count)
                vector.memset(bass.AP(big2, 0, [[B2F, 128], [1, 2048]]), 0.0)
                d_sp.inc(vector.drain())
            DZ6 = d_sp.count

            @block.gpsimd
            def _(gpsimd, DZ6=DZ6):
                gpsimd = EW(gpsimd)
                gpsimd.wait_ge(d_sp.sem, DZ6)
                gpsimd.wait_ge(mr[1].sem, mr[1].count)
                for ch_ in vr:
                    ch_.wait(gpsimd)
                zs = bass.AP(big2, 0, [[B2F, 128], [1, 2048]])
                for k in range(8):
                    gdma.inc(gpsimd.dma_start(
                        out=bass.AP(moe_acc_t, k * 128 * 2048,
                                    [[2048, 128], [1, 2048]]), in_=zs), 16)
                gdma.inc(gpsimd.dma_start(
                    out=bass.AP(inv2_t, 0,
                                [[16384, 2], [2048, 4], [1, 2048]]),
                    in_=bass.AP(big2, 0, [[B2F, 8], [1, 2048]])), 16)
            MZERO6 = gdma.count

            if debug:
                @block.sync
                def _(sync):
                    d_pe.wait(sync)
                    st.inc(sync.dma_start(out=taps["t_x2"].ap(),
                                          in_=bass.AP(x_sb, 0,
                                                      [[NT * D, 128], [1, NT * D]])), 16)

        if stage >= 6:
            P5A, P5D, P5S = pe_a.count, d_pe.count, d_sp.count
            @block.sync
            def _(sync):
                d_pe.wait(sync)
                pe_g.wait(sync)
                ld.inc(sync.dma_start(out=bass.AP(big3, 0, [[B3F, 1], [1, D]]),
                                      in_=gamma_in), 16)
                ld.inc(sync.dma_start(out=bass.AP(big3, D, [[B3F, 1], [1, D]]),
                                      in_=beta_in), 16)
                ld.inc(sync.dma_start(
                    out=bass.AP(big2, 7688, [[B2F, 128], [6, 8], [1, 6]]),
                    in_=pwT_in.rearrange("(c p) e -> p c e", p=128)), 16)
                ld.inc(sync.dma_start(
                    out=bass.AP(big2, 7736, [[B2F, 6], [1, 6]]),
                    in_=ancT_in), 16)
            LGB = ld.count

            @block.vector
            def _(vector):
                vector = EW(vector)
                vector.wait_ge(pe_d.sem, pe_d.count)
                st.wait(vector)
                vector.memset(ones_r, 1.0)
                d_pe.inc(vector.drain())

            @block.tensor
            def _(tensor, LGB=LGB):
                tensor = EW(tensor)
                tensor.wait_ge(ld.sem, LGB)
                d_pe.wait(tensor)
                for k in range(4):
                    tensor.matmul(pb[4 + k][:, 0:512],
                                  ones_r,
                                  bass.AP(big3, (k // 2) * D + (k % 2) * 512,
                                          [[B3F, 1], [1, 512]]),
                                  start=True, stop=True)
                    pe_d.inc(tensor.last)

            @block.vector
            def _(vector):
                vector = EW(vector)
                base = pe_d.count - 4
                for k in range(4):
                    pe_d.wait(vector, base + k + 1)
                    vector.tensor_copy(
                        attn_sb[:, (k // 2) * 1024 + (k % 2) * 512:
                                (k // 2) * 1024 + (k % 2) * 512 + 512],
                        pb[4 + k][:, 0:512])
                d_pe.inc(vector.drain())
                d_a.inc_eng(vector)

            gbG = attn_sb[:, 0:D]
            gbB = attn_sb[:, D:2 * D]

            qkv_bank_tok = {b: (d_pe, 0) for b in range(8)}
            vst_tok = {}
            qst_tok = {}
            for g in range(NG):
                gs = ts(g, 512)
                emit_ln_stats([g * 4 + tt for tt in range(4)],
                              lambda i: x_sb[:, i, :], stg[:, 0, :],
                              first_wait=((d_pe.sem, d_pe.count)
                                          if g == 0 else None),
                              keep_s1=True)

                @block.vector
                def _(vector):
                    vector = EW(vector)
                    vector.tensor_tensor(cols[:, 36:40], cols[:, 0:4],
                                         cols[:, 0:4], ALU.mult)
                    vector.drain()
                    vector.tensor_tensor(cols[:, 40:44], cols[:, 48:52],
                                         cols[:, 36:40], ALU.mult)
                    d_a.inc(vector.drain())

                @block.scalar
                def _(scalar):
                    scalar = EW(scalar)
                    d_a.wait(scalar)
                    scalar.activation(cols[:, 44:48], cols[:, 40:44],
                                      AF.Sqrt, bias=cols[:, 62:63])
                    a_d.inc(scalar.drain())

                @block.vector
                def _(vector):
                    vector = EW(vector)
                    a_d.wait(vector)
                    vector.reciprocal(cols[:, 44:48], cols[:, 44:48])
                    d_a.inc(vector.drain())

                for tt in range(4):
                    i = g * 4 + tt

                    @block.vector
                    def _(vector, i=i, tt=tt):
                        vector = EW(vector)
                        pe_a.wait(vector)
                        vector.tensor_scalar(stg[:, 0, :], x_sb[:, i, :],
                                             cols[:, 16 + tt:17 + tt],
                                             cols[:, tt:tt + 1],
                                             ALU.subtract, ALU.mult)
                        d_pe.inc(vector.drain())
                        d_sp.inc_eng(vector)

                    @block.sync
                    def _(sync, i=i):
                        d_sp.wait(sync)
                        st.inc(sync.dma_start(
                            out=bass.AP(xn_scr_t, i * 128 * D,
                                        [[D, 128], [1, D]]),
                            in_=stg[:, 0, :]), 16)

                    @block.tensor
                    def _(tensor, tt=tt):
                        tensor = EW(tensor)
                        d_pe.wait(tensor)
                        for half in range(2):
                            for c in range(4):
                                tensor.transpose(pb[6 + half][:, ts(c)],
                                                 stg[:, 0, ts(half * 4 + c)],
                                                 ident[:, :])
                            pe_a.inc(tensor.last)

                    @block.vector
                    def _(vector, g=g, tt=tt):
                        vector = EW(vector)
                        pe_a.wait(vector)
                        if g == 0 and tt == 0:
                            vector.wait_ge(gdma.sem, MZERO6)
                        for half in range(2):
                            vector.tensor_copy(
                                bass.AP(big2, (4 * half) * 512 + tt * 128,
                                        [[B2F, 128], [512, 4], [1, 128]]),
                                pb[6 + half][:, 0:512].rearrange(
                                    "p (a d) -> p a d", d=128))
                        vector.drain()
                        st.wait(vector)
                        vector.scalar_tensor_tensor(stg[:, 0, :], stg[:, 0, :],
                                                    cols[:, 44 + tt:45 + tt],
                                                    gbG, ALU.mult, ALU.mult)
                        vector.drain()
                        vector.tensor_tensor(stg[:, 0, :], stg[:, 0, :], gbB,
                                             ALU.add)
                        d_pe.inc(vector.drain())

                    @block.tensor
                    def _(tensor, tt=tt):
                        tensor = EW(tensor)
                        d_pe.wait(tensor)
                        for half in range(2):
                            for c in range(4):
                                tensor.transpose(pb[6 + half][:, ts(c)],
                                                 stg[:, 0, ts(half * 4 + c)],
                                                 ident[:, :])
                            pe_a.inc(tensor.last)

                    @block.vector
                    def _(vector, tt=tt):
                        vector = EW(vector)
                        pe_a.wait(vector)
                        for half in range(2):
                            vector.tensor_copy(
                                bass.AP(big3, (4 * half) * 512 + tt * 128,
                                        [[B3F, 128], [512, 4], [1, 128]]),
                                pb[6 + half][:, 0:512].rearrange(
                                    "p (a d) -> p a d", d=128))
                        d_pe.inc(vector.drain())

                XN_DONE = d_pe.count

                for fb in range(12):
                    bi = fb % 2
                    buf = [wA, wB][bi]

                    @block.sync
                    def _(sync, fb=fb, bi=bi, buf=buf):
                        load_weight(sync, bi,
                                    bass.AP(buf, 0, [[2048, 128], [256, 8], [1, 256]]),
                                    wqkT_in[:, fb * 256:(fb + 1) * 256]
                                    .rearrange("(c p) f -> p c f", p=128))
                    lf = ((chW[2 * bi].sem, chW[2 * bi].count),
                          (chW[2 * bi + 1].sem, chW[2 * bi + 1].count))

                    if fb < 8:
                        @block.tensor
                        def _(tensor, fb=fb, bi=bi, buf=buf, lf=lf,
                              XN_DONE=XN_DONE):
                            tensor = EW(tensor)
                            [tensor.wait_ge(s_, c_) for s_, c_ in lf]
                            tensor.wait_ge(d_pe.sem, XN_DONE)
                            for fc in range(2):
                                bank = fc
                                tok = qkv_bank_tok[bank]
                                tensor.wait_ge(tok[0].sem, tok[1])
                                for ec in range(ND):
                                    tensor.matmul(pb[bank][:, 0:512],
                                                  bass.AP(buf, ec * 256 + fc * 128,
                                                          [[2048, 128], [1, 128]]),
                                                  bass.AP(big3, ec * 512,
                                                          [[B3F, 128], [1, 512]]),
                                                  start=(ec == 0),
                                                  stop=(ec == ND - 1))
                                pe_d.inc(tensor.last)
                            buf_free[bi] = (pe_d, pe_d.count)

                        @block.vector
                        def _(vector, fb=fb, g=g):
                            vector = EW(vector)
                            base = pe_d.count - 2
                            for fc in range(2):
                                bank = fc
                                pe_d.wait(vector, base + fc + 1)
                                fq = fb * 2 + fc
                                if fb < 4:
                                    tok = qst_tok.get(fc)
                                    if tok is not None:
                                        vector.wait_ge(tok[0].sem, tok[1])
                                    vector.tensor_scalar(
                                        bass.AP(big2, 6664 + fc * 512,
                                                [[B2F, 128], [1, 512]]),
                                        pb[bank][:, 0:512], 0.125, None, ALU.mult)
                                    d_sp.inc(vector.drain())
                                    d_pe.inc_eng(vector)
                                else:
                                    vector.tensor_copy(fmA[:, fq - 8, ts(g, 512)],
                                                       pb[bank][:, 0:512])
                                    d_pe.inc(vector.drain())
                                qkv_bank_tok[bank] = (d_pe, d_pe.count)

                        if fb < 4:
                            for fc in range(2):
                                @block.sync
                                def _(sync, fb=fb, fc=fc, g=g):
                                    d_sp.wait(sync, d_sp.count - 1 + fc)
                                    fq = fb * 2 + fc
                                    st_q[fc].inc(sync.dma_start(
                                        out=bass.AP(qt_scr_t, fq * 128 * T + g * 512,
                                                    [[T, 128], [1, 512]]),
                                        in_=bass.AP(big2, 6664 + fc * 512,
                                                    [[B2F, 128], [1, 512]])), 16)
                                qst_tok[fc] = (st_q[fc], st_q[fc].count)
                    else:
                        vb = fb - 8

                        @block.tensor
                        def _(tensor, vb=vb, bi=bi, buf=buf, g=g, lf=lf,
                              XN_DONE=XN_DONE):
                            tensor = EW(tensor)
                            [tensor.wait_ge(s_, c_) for s_, c_ in lf]
                            tensor.wait_ge(d_pe.sem, XN_DONE)
                            for tsub in range(4):
                                bank = 2 + tsub
                                tok = qkv_bank_tok[bank]
                                tensor.wait_ge(tok[0].sem, tok[1])
                                for ec in range(ND):
                                    tensor.matmul(pb[bank][:, 0:256],
                                                  bass.AP(big3, ec * 512 + tsub * 128,
                                                          [[B3F, 128], [1, 128]]),
                                                  bass.AP(buf, ec * 256,
                                                          [[2048, 128], [1, 256]]),
                                                  start=(ec == 0),
                                                  stop=(ec == ND - 1))
                                pe_d.inc(tensor.last)
                            buf_free[bi] = (pe_d, pe_d.count)

                        PDB = pe_d.count - 4
                        for tsub in range(4):
                            @block.vector
                            def _(vector, vb=vb, g=g, tsub=tsub, PDB=PDB):
                                vector = EW(vector)
                                bank = 2 + tsub
                                slot = tsub % 2
                                pe_d.wait(vector, PDB + tsub + 1)
                                tok = vst_tok.get(slot)
                                if tok is not None:
                                    vector.wait_ge(tok[0].sem, tok[1])
                                vector.tensor_copy(
                                    bass.AP(big2, 6144 + slot * 260,
                                            [[B2F, 128], [65, 4], [1, 64]]),
                                    pb[bank][:, 0:256].rearrange(
                                        "p (a d) -> p a d", d=64))
                                vector.drain()
                                vector.memset(
                                    bass.AP(big2, 6144 + slot * 260 + 64,
                                            [[B2F, 128], [65, 4], [1, 1]]), 1.0)
                                d_sp.inc(vector.drain())
                                d_pe.inc_eng(vector)
                                qkv_bank_tok[bank] = (d_pe, d_pe.count)

                            @block.sync
                            def _(sync, vb=vb, g=g, tsub=tsub):
                                slot = tsub % 2
                                sync.wait_ge(d_sp.sem, d_sp.count)
                                tile = g * 4 + tsub
                                st_l[slot].inc(sync.dma_start(
                                    out=bass.AP(vl1_scr_t,
                                                tile * 128 * 1040 + vb * 260,
                                                [[1040, 128], [1, 260]]),
                                    in_=bass.AP(big2, 6144 + slot * 260,
                                                [[B2F, 128], [1, 260]])), 16)
                                vst_tok[slot] = (st_l[slot], st_l[slot].count)

                DP0, DA0, APE0 = d_pe.count, d_a.count, pe_a.count
                TS6 = [128, 256, 384, 0]
                stage6 = [bass.AP(big2, 7936, [[B2F, 6], [1, 128]]),
                          bass.AP(big2, 8064, [[B2F, 6], [1, 128]]),
                          small[0:6, 384:512],
                          small[0:6, 96:224]]
                for tt in range(4):
                    i = g * 4 + tt
                    BB = 7744 + tt * 48

                    def bb(off, n=6, BB=BB):
                        return bass.AP(big2, BB + off, [[B2F, 128], [1, n]])

                    @block.tensor
                    def _(tensor, tt=tt, DP0=DP0, APE0=APE0):
                        tensor = EW(tensor)
                        tensor.wait_ge(d_pe.sem, DP0)
                        tensor.wait_ge(pe_a.sem, APE0)
                        for dc in range(ND):
                            tensor.matmul(pb[5][:, tt * 6:(tt + 1) * 6],
                                          bass.AP(big2, dc * 512 + tt * 128,
                                                  [[B2F, 128], [1, 128]]),
                                          bass.AP(big2, 7688 + dc * 6,
                                                  [[B2F, 128], [1, 6]]),
                                          start=(dc == 0), stop=(dc == ND - 1))
                        pe_a.inc(tensor.last)

                    @block.scalar
                    def _(scalar, tt=tt, bb=bb, DP0=DP0, DA0=DA0):
                        scalar = EW(scalar)
                        pe_a.wait(scalar)
                        scalar.wait_ge(d_pe.sem, DP0)
                        scalar.wait_ge(d_a.sem, DA0)
                        scalar.activation(bb(0), pb[5][:, tt * 6:(tt + 1) * 6],
                                          AF.Tanh, scale=cols[:, 63:64])
                        a_pe.inc(scalar.drain())
                        a_d.inc_eng(scalar)

                    @block.tensor
                    def _(tensor, tt=tt, bb=bb):
                        tensor = EW(tensor)
                        a_pe.wait(tensor)
                        tensor.transpose(pb[5][0:6, TS6[tt]:TS6[tt] + 128],
                                         bb(0), ident[:, :])
                        pe_d.inc(tensor.last)

                    @block.vector
                    def _(vector, tt=tt):
                        vector = EW(vector)
                        pe_d.wait(vector)
                        vector.tensor_copy(stage6[tt],
                                           pb[5][0:6, TS6[tt]:TS6[tt] + 128])
                        d_pe.inc(vector.drain())
                    DT6 = d_pe.count

                    @block.tensor
                    def _(tensor, tt=tt, DT6=DT6, DA0=DA0):
                        tensor = EW(tensor)
                        tensor.wait_ge(d_pe.sem, DT6)
                        tensor.wait_ge(d_a.sem, DA0)
                        tensor.matmul(pb[4][:, 256 + tt * 6:256 + (tt + 1) * 6],
                                      stage6[tt],
                                      bass.AP(big2, 7736, [[B2F, 6], [1, 6]]),
                                      start=True, stop=True)
                        pe_d.inc(tensor.last)

                    @block.vector
                    def _(vector, i=i, tt=tt, bb=bb):
                        vector = EW(vector)
                        pe_d.wait(vector)
                        vector.tensor_scalar(bb(6),
                                             pb[4][:, 256 + tt * 6:
                                                   256 + (tt + 1) * 6],
                                             6.0, cols[:, 60:61],
                                             ALU.subtract, ALU.mult)
                        vector.drain()
                        d_pe.inc(vector.last)
                        vector.reduce_max(cols[:, 28:29], bb(6), AX.X)
                        vector.drain()
                        vector.tensor_scalar(bb(12), bb(6),
                                             cols[:, 28:29], None, ALU.is_ge)
                        vector.drain()
                        vector.scalar_tensor_tensor(bb(18),
                                                    bb(12), -1000.0,
                                                    iotas[:, 8:14],
                                                    ALU.mult, ALU.add)
                        vector.drain()
                        vector.tensor_reduce(cols[:, 29:30], bb(18),
                                             AX.X, ALU.min)
                        vector.drain()
                        vector.tensor_scalar(cols[:, 30:31], cols[:, 29:30],
                                             1000.0, None, ALU.add)
                        vector.drain()
                        vector.tensor_scalar(bb(12), iotas[:, 8:14],
                                             cols[:, 30:31], None, ALU.is_equal)
                        vector.drain()
                        vector.scalar_tensor_tensor(bb(24),
                                                    bb(12), -1.0e5,
                                                    bb(6),
                                                    ALU.mult, ALU.add)
                        vector.drain()
                        vector.reduce_max(cols[:, 31:32], bb(24), AX.X)
                        vector.drain()
                        vector.tensor_scalar(bb(30), bb(24),
                                             cols[:, 31:32], None, ALU.is_ge)
                        vector.drain()
                        vector.scalar_tensor_tensor(bb(36),
                                                    bb(30), -1000.0,
                                                    iotas[:, 8:14],
                                                    ALU.mult, ALU.add)
                        vector.drain()
                        vector.tensor_reduce(cols[:, 32:33], bb(36),
                                             AX.X, ALU.min)
                        vector.drain()
                        vector.tensor_scalar(cols[:, 33:34], cols[:, 32:33],
                                             1000.0, None, ALU.add)
                        vector.drain()
                        vector.tensor_scalar(bb(30), iotas[:, 8:14],
                                             cols[:, 33:34], None, ALU.is_equal)
                        vector.drain()
                        vector.tensor_scalar(cols[:, 34:35], cols[:, 31:32],
                                             cols[:, 28:29], None, ALU.subtract)
                        d_a.inc(vector.drain())

                    @block.scalar
                    def _(scalar):
                        scalar = EW(scalar)
                        d_a.wait(scalar)
                        scalar.activation(cols[:, 35:36], cols[:, 34:35], AF.Exp)
                        a_d.inc(scalar.drain())

                    @block.vector
                    def _(vector, i=i, tt=tt, bb=bb):
                        vector = EW(vector)
                        a_d.wait(vector)
                        vector.tensor_scalar(cols[:, 36:37], cols[:, 35:36], 1.0,
                                             None, ALU.add)
                        vector.drain()
                        vector.reciprocal(cols[:, 37:38], cols[:, 36:37])
                        vector.drain()
                        vector.tensor_tensor(cols[:, 38:39], cols[:, 35:36],
                                             cols[:, 37:38], ALU.mult)
                        vector.drain()
                        vector.tensor_add(small[:, i * 6:(i + 1) * 6],
                                          bb(12), bb(30))
                        vector.tensor_copy(small[:, 320 + i * 4:321 + i * 4],
                                           cols[:, 29:30])
                        vector.tensor_copy(small[:, 321 + i * 4:322 + i * 4],
                                           cols[:, 32:33])
                        vector.tensor_copy(small[:, 322 + i * 4:323 + i * 4],
                                           cols[:, 37:38])
                        vector.tensor_copy(small[:, 323 + i * 4:324 + i * 4],
                                           cols[:, 38:39])
                        d_pe.inc(vector.drain())
                qkv_bank_tok[4] = (d_a, d_a.count)
                qkv_bank_tok[5] = (d_pe, d_pe.count)

            C1 = 1024
            TRASH1 = 6 * C1
            XN1ST = st.count
            @block.gpsimd
            def _(gpsimd):
                gpsimd = EW(gpsimd)
                pe_a.wait(gpsimd)
                gpsimd.wait_ge(pe_d.sem, pe_d.count)
                gpsimd.wait_ge(pe_g.sem, pe_g.count)
                gpsimd.wait_ge(gdma.sem, MZERO6)
                mr[0].inc(gpsimd.dma_start(
                    out=bass.AP(big2, 512, [[B2F, 128], [128, 2], [1, 128]]),
                    in_=tri2_in.rearrange("a p d -> p a d")), 16)
            LTRI6 = mr[0].count

            GTOK6 = {0: None, 1: None}
            for i in range(NT):
                par = i % 2

                @block.tensor
                def _(tensor, i=i, LTRI6=LTRI6):
                    tensor = EW(tensor)
                    if i == 0:
                        tensor.wait_ge(mr[0].sem, LTRI6)
                    d_pe.wait(tensor)
                    for j in range(i):
                        tensor.matmul(pb[4][:, 0:6],
                                      bass.AP(big2, 512, [[B2F, 128], [1, 128]]),
                                      small[:, j * 6:(j + 1) * 6],
                                      start=(j == 0), stop=False)
                    tensor.matmul(pb[4][:, 0:6],
                                  bass.AP(big2, 640, [[B2F, 128], [1, 128]]),
                                  small[:, i * 6:(i + 1) * 6],
                                  start=(i == 0), stop=True)
                    pe_d.inc(tensor.last)

                @block.vector
                def _(vector, i=i, par=par):
                    vector = EW(vector)
                    pe_d.wait(vector)
                    tok = GTOK6[par]
                    if tok is not None:
                        vector.wait_ge(tok[0].sem, tok[1])
                    vector.tensor_copy(bass.AP(big2, 1152 + i * 8,
                                               [[B2F, 128], [1, 6]]),
                                       pb[4][:, 0:6])
                    vector.drain()
                    d_pe.inc_eng(vector)
                    vector.tensor_scalar(cols[:, 28 + par:29 + par],
                                         iotas[:, 14:15],
                                         float(i * 128), None, ALU.add)
                    vector.drain()
                    for k in range(2):
                        idxc = small[:, 320 + i * 4 + k:321 + i * 4 + k]
                        vector.tensor_scalar(cols[:, 40:41], idxc, 1000.0,
                                             None, ALU.add)
                        vector.drain()
                        vector.tensor_scalar(small[:, 288:294], iotas[:, 8:14],
                                             cols[:, 40:41], None, ALU.is_equal)
                        vector.drain()
                        vector.tensor_tensor(small[:, 296:302],
                                             small[:, 288:294],
                                             bass.AP(big2, 1152 + i * 8,
                                                     [[B2F, 128], [1, 6]]),
                                             ALU.mult)
                        vector.drain()
                        vector.reduce_sum(cols[:, 41:42], small[:, 296:302],
                                          AX.X)
                        vector.drain()
                        vector.tensor_scalar(cols[:, 42:43], cols[:, 41:42],
                                             float(C1), None, ALU.is_ge)
                        vector.tensor_scalar(cols[:, 43:44], idxc, float(C1),
                                             cols[:, 41:42], ALU.mult, ALU.add)
                        vector.drain()
                        vector.tensor_scalar(cols[:, 47:48],
                                             cols[:, 28 + par:29 + par],
                                             2.0, float(TRASH1 + k),
                                             ALU.mult, ALU.add)
                        vector.drain()
                        vector.tensor_scalar(cols[:, 44:45], cols[:, 43:44],
                                             -1.0, cols[:, 47:48], ALU.mult,
                                             ALU.add)
                        vector.drain()
                        vector.scalar_tensor_tensor(cols[:, 45:46],
                                                    cols[:, 44:45],
                                                    cols[:, 42:43],
                                                    cols[:, 43:44],
                                                    ALU.mult, ALU.add)
                        vector.drain()
                        vector.tensor_copy(i32g[:, 4 * par + k:4 * par + k + 1],
                                           cols[:, 45:46])
                        vector.drain()
                    d_sp.inc_eng(vector)
                DSPR6 = d_sp.count

                @block.gpsimd
                def _(gpsimd, i=i, par=par, DSPR6=DSPR6):
                    gpsimd = EW(gpsimd)
                    gpsimd.wait_ge(d_sp.sem, DSPR6)
                    if i == 0:
                        gpsimd.wait_ge(gdma.sem, MZERO6)
                    for k in range(2):
                        sc[par].inc(gpsimd.indirect_dma_start(
                            out=bass.AP(inv2_t, 0, [[1, 32768], [1, 1]]),
                            out_offset=IndirectOffsetOnAxis(
                                i32g[:, 4 * par + k:4 * par + k + 1], 0),
                            in_=cols[:, 28 + par:29 + par], in_offset=None), 16)
                        sc[par].inc(gpsimd.indirect_dma_start(
                            out=bass.AP(inv2_t, 0, [[1, 32768], [1, 1]]),
                            out_offset=IndirectOffsetOnAxis(
                                i32g[:, 4 * par + k:4 * par + k + 1], 0),
                            in_=small[:, 322 + i * 4 + k:323 + i * 4 + k],
                            in_offset=None, element_offset=16384), 16)
                GTOK6[par] = (sc[par], sc[par].count)
            SCAT6 = (sc[0].count, sc[1].count)

            @block.gpsimd
            def _(gpsimd):
                gpsimd = EW(gpsimd)
                gpsimd.wait_ge(sc[0].sem, SCAT6[0])
                gpsimd.wait_ge(sc[1].sem, SCAT6[1])
                for ch_ in (st_l[0], st_l[1], st_q[0], st_q[1]):
                    ch_.wait(gpsimd)
                mr[0].inc(gpsimd.dma_start(
                    out=bass.AP(big2, 6144, [[B2F, 48], [1, 128]]),
                    in_=bass.AP(inv2_t, 0, [[128, 48], [1, 128]])), 16)
                mr[0].inc(gpsimd.dma_start(
                    out=bass.AP(big2, 48 * B2F + 6144, [[B2F, 48], [1, 128]]),
                    in_=bass.AP(inv2_t, 16384, [[128, 48], [1, 128]])), 16)
            MRP6 = mr[0].count

            @block.tensor
            def _(tensor, MRP6=MRP6):
                tensor = EW(tensor)
                tensor.wait_ge(mr[0].sem, MRP6)
                d_pe.wait(tensor)
                tensor.transpose(pb[7][:, 0:96],
                                 bass.AP(big2, 6144, [[B2F, 96], [1, 128]]),
                                 ident[0:96, 0:96])
                pe_d.inc(tensor.last)
            PEP6 = pe_d.count

            @block.vector
            def _(vector, PEP6=PEP6):
                vector = EW(vector)
                vector.wait_ge(pe_d.sem, PEP6)
                vector.tensor_copy(
                    bass.AP(big2, 6272, [[B2F, 128], [1, 96]]),
                    pb[7][:, 0:96])
                vector.drain()
                vector.tensor_copy(
                    i32a[:, 0:48],
                    bass.AP(big2, 6272, [[B2F, 128], [1, 48]]))
                d_pe.inc(vector.drain())
                d_sp.inc_eng(vector)
            IDP6_RDY = d_sp.count

            ring6 = [(big2, 0, B2F), (big2, 1024, B2F),
                     (big2, 2048, B2F), (big2, 3072, B2F)]
            ring6_tok = {0: None, 1: None, 2: None, 3: None}
            ost6_slots = [(stg, 0, 1024), (attn_sb, 0, 2048),
                          (attn_sb, 1024, 2048), (big2, 6368, B2F)]
            pend_sc6 = []

            def emit_sc6(cts, cc0, DSE6):
                ot, ooff, OF = ost6_slots[cts]

                @block.gpsimd
                def _(gpsimd, cts=cts, cc0=cc0, DSE6=DSE6,
                      ot=ot, ooff=ooff, OF=OF):
                    gpsimd = EW(gpsimd)
                    gpsimd.wait_ge(d_sp.sem, DSE6)
                    gpsimd.wait_ge(mr[1].sem, mr[1].count)
                    mr[1].inc(gpsimd.indirect_dma_start(
                        out=bass.AP(moe_acc_t, 0, [[D, T], [1, D]]),
                        out_offset=IndirectOffsetOnAxis(
                            i32a[:, cc0 + cts:cc0 + cts + 1], 0),
                        in_=bass.AP(ot, ooff, [[OF, 128], [1, 1024]]),
                        in_offset=None,
                        compute_op=ALU.add), 16)
            h1t6_tok = (d_sp, d_sp.count)
            SPAN1 = 512
            NCSP1 = 4
            ffn_bank_tok = {b: (d_pe, d_pe.count) for b in range(8)}

            def emit_gather6(cc):
                @block.gpsimd
                def _(gpsimd, cc=cc):
                    gpsimd = EW(gpsimd)
                    if cc == 0:
                        gpsimd.wait_ge(d_sp.sem, IDP6_RDY)
                        gpsimd.wait_ge(st.sem, XN1ST)
                    tok = ring6_tok[cc % 4]
                    if tok is not None:
                        gpsimd.wait_ge(tok[0].sem, tok[1])
                    rt, roff, F = ring6[cc % 4]
                    vr[cc % 4].inc(gpsimd.indirect_dma_start(
                        out=bass.AP(rt, roff, [[F, 128], [1, 1024]]),
                        out_offset=None,
                        in_=bass.AP(xn_scr_t, 0, [[D, T], [1, D]]),
                        in_offset=IndirectOffsetOnAxis(
                            i32a[:, cc:cc + 1], 0)), 16)
                return vr[cc % 4].count
            gather_vrc6 = {c: emit_gather6(c) for c in range(NCSP1)}

            for e in range(6):
                for half in range(2):
                    for c in range(NCSP1):
                        cc = e * 2 * NCSP1 + half * NCSP1 + c
                        VRC = gather_vrc6.pop(cc)

                        @block.tensor
                        def _(tensor, c=c, cc=cc, VRC=VRC):
                            tensor = EW(tensor)
                            tensor.wait_ge(vr[cc % 4].sem, VRC)
                            d_pe.wait(tensor)
                            rt, roff, F = ring6[cc % 4]
                            for half2 in range(2):
                                for cch in range(4):
                                    tensor.transpose(
                                        pb[6 + half2][:, ts(cch)],
                                        bass.AP(rt, roff + (half2 * 4 + cch) * 128,
                                                [[F, 128], [1, 128]]),
                                        ident[:, :])
                                pe_a.inc(tensor.last)
                        ring6_tok[cc % 4] = (pe_a, pe_a.count)

                        @block.vector
                        def _(vector, c=c):
                            vector = EW(vector)
                            pe_a.wait(vector)
                            for half2 in range(2):
                                vector.tensor_copy(
                                    bass.AP(big3, (4 * half2) * SPAN1 + c * 128,
                                            [[B3F, 128], [SPAN1, 4], [1, 128]]),
                                    pb[6 + half2][:, 0:512].rearrange(
                                        "p (a d) -> p a d", d=128))
                            d_pe.inc(vector.drain())
                    XST6 = d_pe.count
                    nh0 = (e * 2 + half + 1) * NCSP1
                    for cc2 in range(nh0, min(nh0 + NCSP1, 12 * NCSP1)):
                        gather_vrc6[cc2] = emit_gather6(cc2)
                    for args in pend_sc6:
                        emit_sc6(*args)
                    pend_sc6 = []

                    for blk in range(2):
                        bi = blk % 2
                        buf = [wA, wB][bi]

                        @block.sync
                        def _(sync, e=e, blk=blk, bi=bi, buf=buf):
                            load_weight(sync, bi,
                                        bass.AP(buf, 0,
                                                [[2048, 128], [256, 8], [1, 256]]),
                                        w1f[e][:, blk * 256:(blk + 1) * 256]
                                        .rearrange("(c p) f -> p c f", p=128))
                        lw = ((chW[2 * bi].sem, chW[2 * bi].count),
                              (chW[2 * bi + 1].sem, chW[2 * bi + 1].count))

                        @block.tensor
                        def _(tensor, blk=blk, bi=bi, buf=buf, lw=lw,
                              XST6=XST6):
                            tensor = EW(tensor)
                            [tensor.wait_ge(s_, c_) for s_, c_ in lw]
                            tensor.wait_ge(d_pe.sem, XST6)
                            for hh in range(2):
                                hc = blk * 2 + hh
                                bank = hc % 2
                                tok = ffn_bank_tok[bank]
                                tensor.wait_ge(tok[0].sem, tok[1])
                                for dc in range(ND):
                                    tensor.matmul(pb[bank][:, 0:512],
                                                  bass.AP(buf, dc * 256 + hh * 128,
                                                          [[2048, 128], [1, 128]]),
                                                  bass.AP(big3, dc * SPAN1,
                                                          [[B3F, 128], [1, SPAN1]]),
                                                  start=(dc == 0),
                                                  stop=(dc == ND - 1))
                                pe_a.inc(tensor.last)
                            buf_free[bi] = (pe_a, pe_a.count)

                        @block.scalar
                        def _(scalar, blk=blk):
                            scalar = EW(scalar)
                            if blk == 0:
                                scalar.wait_ge(h1t6_tok[0].sem, h1t6_tok[1])
                            for hh in range(2):
                                hc = blk * 2 + hh
                                bank = hc % 2
                                pe_a.wait(scalar, pe_a.count - 1 + hh)
                                scalar.activation(
                                    bass.AP(big2, 4096 + hc * 512,
                                            [[B2F, 128], [1, 512]]),
                                    pb[bank][:, 0:512], AF.Gelu_apprx_tanh)
                                a_pe.inc(scalar.drain())
                                ffn_bank_tok[bank] = (a_pe, a_pe.count)

                    A_PE_H6 = a_pe.count
                    for hc in range(4):
                        slot = hc % 4
                        sbuf_t = [wA, wA, wB, wB][slot]
                        soff = [0, 1024, 0, 1024][slot]

                        @block.sync
                        def _(sync, e=e, hc=hc, slot=slot, sbuf_t=sbuf_t,
                              soff=soff):
                            sync = EW(sync)
                            tok = w2slot_tok.get(slot)
                            if tok is not None:
                                sync.wait_ge(tok[0].sem, tok[1])
                            tok2 = buf_free[0 if sbuf_t is wA else 1]
                            if tok2 is not None:
                                sync.wait_ge(tok2[0].sem, tok2[1])
                            chW[slot].inc(sync.dma_start(
                                out=bass.AP(sbuf_t, soff, [[2048, 128], [1, 1024]]),
                                in_=w2f[e][hc * 128:(hc + 1) * 128, :]), 16)
                        lw2 = ((chW[slot].sem, chW[slot].count),)

                        @block.tensor
                        def _(tensor, hc=hc, slot=slot, sbuf_t=sbuf_t,
                              soff=soff, lw2=lw2, A_PE_H6=A_PE_H6):
                            tensor = EW(tensor)
                            [tensor.wait_ge(s_, c_) for s_, c_ in lw2]
                            if hc == 0:
                                tensor.wait_ge(a_pe.sem, A_PE_H6)
                                for b in range(8):
                                    tok = ffn_bank_tok[b]
                                    tensor.wait_ge(tok[0].sem, tok[1])
                            for cts in range(NCSP1):
                                for db in range(2):
                                    bank = cts * 2 + db
                                    tensor.matmul(
                                        pb[bank][:, 0:512],
                                        bass.AP(big2, 4096 + hc * 512 + cts * 128,
                                                [[B2F, 128], [1, 128]]),
                                        bass.AP(sbuf_t, soff + db * 512,
                                                [[2048, 128], [1, 512]]),
                                        start=(hc == 0), stop=(hc == 3))
                            pe_g.inc(tensor.last)
                        w2slot_tok[slot] = (pe_g, pe_g.count)
                    h1t6_tok = (pe_g, pe_g.count)
                    buf_free[0] = (pe_g, pe_g.count)
                    buf_free[1] = (pe_g, pe_g.count)
                    W2D6 = pe_g.count

                    cc0 = e * 2 * NCSP1 + half * NCSP1
                    for cts in range(NCSP1):
                        ot, ooff, OF = ost6_slots[cts]

                        @block.vector
                        def _(vector, cts=cts, cc0=cc0, W2D6=W2D6,
                              ot=ot, ooff=ooff, OF=OF):
                            vector = EW(vector)
                            vector.wait_ge(pe_g.sem, W2D6)
                            vector.wait_ge(mr[1].sem, mr[1].count)
                            for db in range(2):
                                bank = cts * 2 + db
                                vector.tensor_scalar(
                                    bass.AP(ot, ooff + db * 512,
                                            [[OF, 128], [1, 512]]),
                                    pb[bank][:, 0:512],
                                    bass.AP(big2, 6320 + cc0 + cts,
                                            [[B2F, 128], [1, 1]]),
                                    None, ALU.mult)
                                vector.drain()
                                d_pe.inc_eng(vector)
                                ffn_bank_tok[bank] = (d_pe, d_pe.count)
                            d_sp.inc_eng(vector)
                        pend_sc6.append((cts, cc0, d_sp.count))

            for args in pend_sc6:
                emit_sc6(*args)
            pend_sc6 = []

            MOE6_SC = mr[1].count
            add6_tok = {0: None, 1: None, 2: None, 3: None}
            for i in range(NT):
                slot = i % 4

                @block.gpsimd
                def _(gpsimd, i=i, slot=slot):
                    gpsimd = EW(gpsimd)
                    if i == 0:
                        gpsimd.wait_ge(mr[1].sem, MOE6_SC)
                    tok = add6_tok[slot]
                    if tok is not None:
                        gpsimd.wait_ge(tok[0].sem, tok[1])
                    vr[slot].inc(gpsimd.dma_start(
                        out=bass.AP(big3, slot * 1024, [[B3F, 128], [1, 1024]]),
                        in_=bass.AP(moe_acc_t, i * 128 * D, [[D, 128], [1, D]])),
                        16)
                VRA = vr[slot].count

                @block.vector
                def _(vector, i=i, slot=slot, VRA=VRA):
                    vector = EW(vector)
                    vector.wait_ge(vr[slot].sem, VRA)
                    vector.tensor_add(x_sb[:, i, :], x_sb[:, i, :],
                                      bass.AP(big3, slot * 1024,
                                              [[B3F, 128], [1, 1024]]))
                    d_pe.inc(vector.drain())
                    d_sp.inc_eng(vector)
                add6_tok[slot] = (d_pe, d_pe.count)
                DSY = d_sp.count

                @block.sync
                def _(sync, i=i, DSY=DSY):
                    sync = EW(sync)
                    sync.wait_ge(d_sp.sem, DSY)
                    st.inc(sync.dma_start(
                        out=bass.AP(y_t, (i * 128) * D, [[D, 128], [1, D]]),
                        in_=x_sb[:, i, :]), 16)
            Y_BASE = st.count

            if debug:
                @block.sync
                def _(sync):
                    d_pe.wait(sync)
                    st.inc(sync.dma_start(out=taps["t_kT"].ap(),
                                          in_=bass.AP(fmA, 0, [[FMF, 128], [1, FMF]])), 16)

        if stage >= 7:
            @block.sync
            def _(sync):
                sync.wait_ge(d_pe.sem, P5D)
                sync.wait_ge(pe_a.sem, P5A)
                sync.wait_ge(d_sp.sem, P5S)
                sync.wait_ge(mr[1].sem, mr[1].count)
                ld.inc(sync.dma_start(
                    out=bass.AP(attn_sb, 0, [[2048, 128], [512, 4], [1, 512]]),
                    in_=mask4_in.rearrange("a p d -> p a d")), 16)
            LMASK = ld.count
            PEA7, PEG7 = pe_a.count, pe_g.count

            exp_tok = {0: (pe_g, 0), 1: (pe_g, 0), 2: (pe_g, 0), 3: (pe_g, 0)}
            ut_tok = (d_pe, 0)
            ost_tok = (gdma, 0)
            head_tok = {0: None, 1: None}
            SHIFT = 3
            for h in range(16):
                p0 = (h % 2) * 64
                vhoff = 4096 + (h % 2) * 1040

                @block.gpsimd
                def _(gpsimd, h=h, p0=p0, vhoff=vhoff):
                    gpsimd = EW(gpsimd)
                    if h < 2:
                        gpsimd.wait_ge(pe_a.sem, PEA7)
                        gpsimd.wait_ge(pe_g.sem, PEG7)
                        for ch_ in (st_q[0], st_q[1], st_l[0], st_l[1]):
                            ch_.wait(gpsimd)
                    else:
                        tok = head_tok[h % 2]
                        gpsimd.wait_ge(tok[0].sem, tok[1])
                    hq.inc(gpsimd.dma_start(
                        out=bass.AP(big2, p0 * B2F + 2048, [[B2F, 64], [1, T]]),
                        in_=bass.AP(qt_scr_t, (h * 64) * T, [[T, 64], [1, T]])), 16)
                    hq.inc(gpsimd.dma_start(
                        out=bass.AP(big2, vhoff, [[B2F, 128], [65, 16], [1, 65]]),
                        in_=bass.AP(vl1_scr_t, h * 65,
                                    [[1040, 128], [128 * 1040, 16], [1, 65]])), 16)
                HL = hq.count

                for g in range(NG):
                    nch = 4 * g + 4

                    def emit_pv(c, sslot, EXPC, h=h, g=g, nch=nch,
                                vhoff=vhoff):
                        @block.tensor
                        def _(tensor, c=c, sslot=sslot, EXPC=EXPC):
                            tensor = EW(tensor)
                            tensor.wait_ge(a_pe.sem, EXPC)
                            if c == 0:
                                tensor.wait_ge(ut_tok[0].sem, ut_tok[1])
                            tensor.matmul(
                                pb[4][0:65, 0:512],
                                bass.AP(big2, vhoff + c * 65,
                                        [[B2F, 128], [1, 65]]),
                                bass.AP(big3, sslot * 512,
                                        [[B3F, 128], [1, 512]]),
                                start=(c == 0), stop=(c == nch - 1))
                            pe_g.inc(tensor.last)
                            exp_tok[sslot] = (pe_g, pe_g.count)

                    pend = []
                    for c in range(nch):
                        sslot = c % 4

                        @block.tensor
                        def _(tensor, h=h, g=g, c=c, sslot=sslot, p0=p0, HL=HL):
                            tensor = EW(tensor)
                            if c == 0 and g == 0:
                                tensor.wait_ge(hq.sem, HL)
                            tok = exp_tok[sslot]
                            tensor.wait_ge(tok[0].sem, tok[1])
                            tensor.matmul(
                                pb[sslot][:, 0:512],
                                bass.AP(fmA, p0 * FMF + (h // 2) * T + c * 128,
                                        [[FMF, 64], [1, 128]]),
                                bass.AP(big2, p0 * B2F + 2048 + g * 512,
                                        [[B2F, 64], [1, 512]]),
                                start=True, stop=True)
                            pe_d.inc(tensor.last)
                        SMM = pe_d.count

                        dd = c - 4 * g

                        if dd >= 0:
                            @block.vector
                            def _(vector, h=h, c=c, g=g, sslot=sslot, dd=dd,
                                  SMM=SMM):
                                vector = EW(vector)
                                if h == 0 and g == 0 and c == 0:
                                    vector.wait_ge(ld.sem, LMASK)
                                vector.wait_ge(pe_d.sem, SMM)
                                vector.tensor_add(
                                    pb[sslot][:, 0:512], pb[sslot][:, 0:512],
                                    bass.AP(attn_sb, dd * 512,
                                            [[2048, 128], [1, 512]]))
                                d_a.inc(vector.drain())
                        DAC = d_a.count

                        @block.scalar
                        def _(scalar, sslot=sslot, dd=dd, SMM=SMM, DAC=DAC):
                            scalar = EW(scalar)
                            if dd >= 0:
                                scalar.wait_ge(d_a.sem, DAC)
                            else:
                                scalar.wait_ge(pe_d.sem, SMM)
                            scalar.activation(
                                bass.AP(big3, sslot * 512,
                                        [[B3F, 128], [1, 512]]),
                                pb[sslot][:, 0:512], AF.Exp)
                            a_pe.inc(scalar.drain())
                        EXPC = a_pe.count

                        pend.append((c, sslot, EXPC))
                        if len(pend) > SHIFT:
                            emit_pv(*pend.pop(0))
                    while pend:
                        emit_pv(*pend.pop(0))

                    @block.vector
                    def _(vector):
                        vector = EW(vector)
                        pe_g.wait(vector)
                        vector.tensor_copy(
                            bass.AP(big2, 7392, [[B2F, 65], [1, 512]], ),
                            pb[4][0:65, 0:512])
                        d_pe.inc(vector.drain())
                    UT = d_pe.count
                    ut_tok = (d_pe, UT)

                    @block.tensor
                    def _(tensor, UT=UT):
                        tensor = EW(tensor)
                        tensor.wait_ge(d_pe.sem, UT)
                        d_pe.wait(tensor)
                        for tq in range(4):
                            tensor.transpose(
                                pb[5][:, tq * 65:(tq + 1) * 65],
                                bass.AP(big2, 7392 + tq * 128,
                                        [[B2F, 65], [1, 128]]),
                                ident[0:65, 0:65])
                        pe_d.inc(tensor.last)
                    TRP = pe_d.count

                    @block.vector
                    def _(vector, TRP=TRP):
                        vector = EW(vector)
                        vector.wait_ge(pe_d.sem, TRP)
                        vector.wait_ge(ost_tok[0].sem, ost_tok[1])
                        vector.reciprocal(cols[:, 52:56],
                                          bass.AP(pb[5], 64, [[512, 128], [65, 4], [1, 1]]))
                        vector.drain()
                        vector.tensor_tensor(
                            bass.AP(big2, 7904, [[B2F, 128], [64, 4], [1, 64]]),
                            bass.AP(pb[5], 0, [[512, 128], [65, 4], [1, 64]]),
                            bass.AP(cols, 52, [[64, 128], [1, 4], [0, 64]]),
                            ALU.mult)
                        d_pe.inc(vector.drain())
                        d_sp.inc_eng(vector)
                    OSG = d_sp.count

                    @block.gpsimd
                    def _(gpsimd, h=h, g=g, OSG=OSG):
                        gpsimd = EW(gpsimd)
                        if h == 0 and g == 0:
                            st.wait(gpsimd)
                        gpsimd.wait_ge(d_sp.sem, OSG)
                        gdma.inc(gpsimd.dma_start(
                            out=bass.AP(y_t, (g * 512) * D + h * 64,
                                        [[D, 128], [128 * D, 4], [1, 64]]),
                            in_=bass.AP(big2, 7904, [[B2F, 128], [64, 4], [1, 64]]),
                            accum_op=ALU.add), 16)
                    ost_tok = (gdma, gdma.count)
                head_tok[h % 2] = (pe_g, pe_g.count)

        @block.sync
        def _(sync):
            for ch_ in ([gdma, st, ld, hq, chb1] + chW + st_a + st_v +
                        st_q + st_l + vr + mr + sc):
                sync.wait_ge(ch_.sem, ch_.count)

    return nc


def make_consts():
    ident = np.eye(128, dtype=np.float32)
    iotas = np.zeros((128, 16), dtype=np.float32)
    iotas[:, 0:8] = np.arange(1000, 1008, dtype=np.float32)[None, :]
    iotas[:, 8:14] = np.arange(1000, 1006, dtype=np.float32)[None, :]
    iotas[:, 14] = np.arange(128, dtype=np.float32)
    ones_row = np.ones((1, 128), dtype=np.float32)
    mask4 = np.zeros((4, 128, 512), dtype=np.float32)
    for d in range(4):
        kp = np.arange(128)[:, None]
        qf = np.arange(512)[None, :]
        mask4[d] = np.where(qf >= 128 * d + kp, 0.0, NEG).astype(np.float32)
    tri2 = np.zeros((2, 128, 128), dtype=np.float32)
    tri2[0] = 1.0
    tri2[1] = np.triu(np.ones((128, 128), np.float32), 1)
    return ident, iotas, ones_row, mask4, tri2


def make_in_map(b, L0_attn_query, L0_attn_W_qv, L0_attn_W2_0, L0_attn_W2_1,
                L0_attn_scale_weights, L0_attn_W_out, L0_ffn_Wg, L0_ffn_W1,
                L0_ffn_b1, L0_ffn_W2, L0_ffn_b2, L1_attn_gamma, L1_attn_beta,
                L1_attn_w_qkv, L1_ffn_proj_w, L1_ffn_anchors, L1_ffn_routing_temp,
                L1_ffn_w1, L1_ffn_w2, shared):
    ident, iotas, ones_row, mask4, tri2 = shared["consts"]
    f32 = np.float32
    m = {
        "x0": np.ascontiguousarray(L0_attn_query[b], dtype=f32),
        "wqv": shared["wqv"],
        "w2cat": shared["w2cat"],
        "sw_in": shared["sw"],
        "wout": shared["wout"],
        "wg": shared["wg"],
        "w1moe": shared["w1moe"],
        "b1moe": shared["b1moe"],
        "w2moe": shared["w2moe"],
        "b2moe": shared["b2moe"],
        "gamma": shared["gamma"],
        "beta": shared["beta"],
        "wqkT": shared["wqkT"],
        "pwT": shared["pwT"],
        "ancT": shared["ancT"],
        "rtemp": shared["rtemp"],
        "w1f": shared["w1f"],
        "w2f": shared["w2f"],
        "mask4": mask4,
        "ident": ident,
        "iotas": iotas,
        "ones_row": ones_row,
        "tri2": tri2,
    }
    return m


def prep_shared(L0_attn_W_qv, L0_attn_W2_0, L0_attn_W2_1, L0_attn_scale_weights,
                L0_attn_W_out, L0_ffn_Wg, L0_ffn_W1, L0_ffn_b1, L0_ffn_W2,
                L0_ffn_b2, L1_attn_gamma, L1_attn_beta, L1_attn_w_qkv,
                L1_ffn_proj_w, L1_ffn_anchors, L1_ffn_routing_temp, L1_ffn_w1,
                L1_ffn_w2):
    f32 = np.float32
    c = np.ascontiguousarray
    shared = {
        "consts": make_consts(),
        "wqv": c(L0_attn_W_qv, dtype=f32),
        "w2cat": c(np.concatenate([np.asarray(L0_attn_W2_0),
                                   np.asarray(L0_attn_W2_1)], axis=1), dtype=f32),
        "sw": c(np.asarray(L0_attn_scale_weights, dtype=f32).reshape(1, 2)),
        "wout": c(L0_attn_W_out, dtype=f32),
        "wg": c(L0_ffn_Wg, dtype=f32),
        "w1moe": c(L0_ffn_W1, dtype=f32),
        "b1moe": c(np.asarray(L0_ffn_b1, dtype=f32).reshape(8, 16, 128).transpose(0, 2, 1)),
        "w2moe": c(L0_ffn_W2, dtype=f32),
        "b2moe": c(L0_ffn_b2, dtype=f32),
        "gamma": c(np.asarray(L1_attn_gamma, dtype=f32).reshape(1, D)),
        "beta": c(np.asarray(L1_attn_beta, dtype=f32).reshape(1, D)),
        "wqkT": c(np.asarray(L1_attn_w_qkv, dtype=f32).T),
        "pwT": c(np.asarray(L1_ffn_proj_w, dtype=f32).T),
        "ancT": c(np.asarray(L1_ffn_anchors, dtype=f32).T),
        "rtemp": c(np.asarray(L1_ffn_routing_temp, dtype=f32).reshape(1, 1)),
        "w1f": c(L1_ffn_w1, dtype=f32),
        "w2f": c(L1_ffn_w2, dtype=f32),
    }
    return shared


def kernel(**inputs):
    shared = prep_shared(**{k: v for k, v in inputs.items()
                            if k not in ("L0_attn_query", "L0_attn_key",
                                         "L0_attn_value")})
    nc = build_program()
    in_maps = [make_in_map(b, shared=shared,
                           **{k: v for k, v in inputs.items()
                              if k not in ("L0_attn_key", "L0_attn_value")})
               for b in range(8)]
    res = run_bass_kernel_spmd(nc, in_maps, core_ids=list(range(8)))
    return np.stack([r["y"] for r in res.results], axis=0)
